# version 04
# baseline (speedup 1.0000x reference)
import numpy as np
import ml_dtypes
from contextlib import ExitStack

import concourse.bass as bass
import concourse.mybir as mybir
from concourse.bass_utils import run_bass_kernel_spmd
from concourse.bass import IndirectOffsetOnAxis

F32 = mybir.dt.float32
BF16 = mybir.dt.bfloat16
AF = mybir.ActivationFunctionType
ALU = mybir.AluOpType
AX = mybir.AxisListType
BFnp = ml_dtypes.bfloat16

B, S, D, TILE = 4, 8192, 1024, 128
NT = S // TILE
OWN = NT // 2
NCH = D // 128
EPS = 1e-5
NEG = -30000.0

_CACHE = {}


def build_ffn():
    if "ffn" in _CACHE:
        return _CACHE["ffn"]
    nc = bass.Bass("TRN2", target_bir_lowering=False)
    stream = nc.dram_tensor("stream", [OWN, TILE, D], F32, kind="ExternalInput").ap()
    prow = nc.dram_tensor("prow", [OWN, D], F32, kind="ExternalInput").ap()
    wrT = nc.dram_tensor("wrT", [D, D], F32, kind="ExternalInput").ap()
    wkT = nc.dram_tensor("wkT", [D, D], F32, kind="ExternalInput").ap()
    wvT = nc.dram_tensor("wvT", [D, D], F32, kind="ExternalInput").ap()
    murf = nc.dram_tensor("murf", [128, D], F32, kind="ExternalInput").ap()
    mukf = nc.dram_tensor("mukf", [128, D], F32, kind="ExternalInput").ap()
    identb = nc.dram_tensor("identb", [128, 128], BF16, kind="ExternalInput").ap()
    constv = nc.dram_tensor("constv", [128, 2], F32, kind="ExternalInput").ap()
    ostream = nc.dram_tensor("ostream", [OWN, TILE, D], F32, kind="ExternalOutput").ap()

    with ExitStack() as ctx:
        def sb(name, shape, dt):
            return ctx.enter_context(nc.sbuf_tensor(name, shape, dt))

        def psr(name, shape, dt):
            return ctx.enter_context(nc.psum_tensor(name, shape, dt))

        def sem(name):
            return ctx.enter_context(nc.semaphore(name))

        xbuf = [sb("xbuf0", [128, D], F32), sb("xbuf1", [128, D], F32)]
        xb4 = xbuf + [sb("xbuf2", [128, D], F32), sb("xbuf3", [128, D], F32)]
        obuf = [sb("obuf0", [128, D], F32), sb("obuf1", [128, D], F32)]
        sqscr = sb("sqscr", [128, D], F32)
        xln = sb("xln", [128, D], BF16)
        fmx = sb("fmx", [128, D], BF16)
        fmp = sb("fmp", [128, D], BF16)
        dmix = sb("dmix", [128, D], BF16)
        xrf = sb("xrf", [128, D], BF16)
        xkf = sb("xkf", [128, D], BF16)
        e1t = sb("e1t", [128, D], F32)
        rt = sb("rt", [128, D], F32)
        krelu = sb("krelu", [128, D], BF16)
        kactb = sb("kactb", [128, D], BF16)
        kactf = sb("kactf", [128, D], BF16)
        ot = sb("ot", [128, D], F32)
        lnout = sb("lnout", [128, D], F32)
        wr_s = sb("wr_s", [128, NCH * D], BF16)
        wk_s = sb("wk_s", [128, NCH * D], BF16)
        wv_s = sb("wv_s", [128, NCH * D], BF16)
        mur_s = sb("mur_s", [128, D], BF16)
        muk_s = sb("muk_s", [128, D], BF16)
        idb = sb("idb", [128, 128], BF16)
        prraw = sb("prraw", [128, D], F32)
        prln = sb("prln", [128, D], BF16)
        prfm = sb("prfm", [128, NCH * OWN], BF16)
        stat = sb("stat", [128, 32], F32)
        s_cst = sb("s_cst", [128, 2], F32)
        nc.const_aps.register(F32, EPS, s_cst[:, 0:1])
        nc.const_aps.register(F32, 1.0, s_cst[:, 1:2])
        SUM, M, Q, VAR, SSQ, MSQ, LNV, RSTD, NB = range(9)

        def col(k, st=0):
            return stat[:, st * 16 + k:st * 16 + k + 1]

        p_tr = psr("p_tr", [128, 1024], BF16)
        p_r = psr("p_r", [128, 1024], F32)
        p_k = psr("p_k", [128, 1024], F32)
        p_v = psr("p_v", [128, 1024], F32)

        dIn = sem("dIn"); dPre = sem("dPre"); dW = sem("dW"); dOut = sem("dOut")
        vS = sem("vS"); aS = sem("aS"); vT = sem("vT"); aR = sem("aR")
        vN = sem("vN"); aX = sem("aX"); tT = sem("tT"); vEv = sem("vEv")
        vM = sem("vM"); tR = sem("tR"); tK = sem("tK"); aE1 = sem("aE1")
        aSig = sem("aSig"); aRel = sem("aRel"); aK = sem("aK"); tT2 = sem("tT2")
        vEv2 = sem("vEv2"); tV = sem("tV"); vO = sem("vO")
        aS2 = sem("aS2"); vT2 = sem("vT2"); aR2 = sem("aR2")
        vN2 = sem("vN2"); aX2 = sem("aX2"); vFin = sem("vFin")
        vPre = sem("vPre"); tPre = sem("tPre")
        vC = sem("vC"); aC = sem("aC")

        block = ctx.enter_context(nc.Block())

        @block.sync
        def _(sync):
            sync.dma_start(out=idb[:], in_=identb).then_inc(dPre, 16)
            sync.dma_start(out=prraw[0:OWN, :], in_=prow).then_inc(dPre, 16)
            sync.dma_start(out=s_cst[:], in_=constv).then_inc(dPre, 16)
            for i in range(OWN):
                if i >= 2:
                    sync.wait_ge(vFin, i - 1)
                sync.dma_start(out=xbuf[i % 2][:], in_=stream[i]).then_inc(dIn, 16)
            sync.wait_ge(dOut, 16 * OWN)

        @block.gpsimd
        def _(g):
            for w_dram, w_sb in ((wrT, wr_s), (wkT, wk_s), (wvT, wv_s)):
                g.dma_start(out=w_sb[:].rearrange("p (c f) -> p c f", c=NCH),
                            in_=w_dram.rearrange("(c p) f -> p c f", p=128)).then_inc(dW, 16)
            g.dma_start(out=mur_s[:], in_=murf).then_inc(dW, 16)
            g.dma_start(out=muk_s[:], in_=mukf).then_inc(dW, 16)
            for i in range(OWN):
                g.wait_ge(vFin, i + 1)
                g.dma_start(out=ostream[i], in_=obuf[i % 2][:]).then_inc(dOut, 16)
            g.wait_ge(dOut, 16 * OWN)

        @block.vector
        def _(v):
            _vn = [0]

            def ch(ins, cross=None, tgt=None):
                if cross is None:
                    _vn[0] += 1
                    ins.then_inc(vC, 1)
                    v.wait_ge(vC, _vn[0])
                else:
                    ins.then_inc(cross, 1)
                    v.wait_ge(cross, tgt)
                return ins

            v.wait_ge(dPre, 48)
            pr = prraw[0:OWN, :]
            st = stat[0:OWN, :]
            ch(v.reduce_sum(st[:, 0:1], pr, axis=AX.X))
            ch(v.tensor_scalar_mul(st[:, 1:2], st[:, 0:1], 1.0 / D))
            v.wait_ge(aS, 1)
            ch(v.tensor_scalar_mul(st[:, 2:3], st[:, 4:5], 1.0 / D))
            ch(v.tensor_mul(st[:, 5:6], st[:, 1:2], st[:, 1:2]))
            ch(v.tensor_sub(st[:, 3:4], st[:, 2:3], st[:, 5:6]), vT, 1)
            v.wait_ge(aR, 1)
            ch(v.tensor_scalar(st[:, 6:7], st[:, 1:2], st[:, 7:8], -1.0,
                               ALU.mult, ALU.mult), vN, 1)
            v.wait_ge(tPre, 1)
            ch(v.tensor_copy(prfm[:], p_tr[:, 0:NCH * OWN]), vPre, 1)

            for i in range(OWN):
                n = i + 1
                v.wait_ge(dIn, 16 * n)
                ch(v.reduce_sum(col(SUM), xbuf[i % 2][:], axis=AX.X))
                ch(v.tensor_scalar_mul(col(M), col(SUM), 1.0 / D))
                v.wait_ge(aS, n + 1)
                ch(v.tensor_scalar_mul(col(Q), col(SSQ), 1.0 / D))
                ch(v.tensor_mul(col(MSQ), col(M), col(M)))
                ch(v.tensor_sub(col(VAR), col(Q), col(MSQ)), vT, n + 1)
                v.wait_ge(aR, n + 1)
                ch(v.tensor_scalar(col(NB), col(M), col(RSTD), -1.0,
                                   ALU.mult, ALU.mult), vN, n + 1)
                v.wait_ge(tT, n)
                if i > 0:
                    v.wait_ge(tK, i)
                ch(v.tensor_copy(fmx[:], p_tr[:]), vEv, n)
                v.wait_ge(vPre, 1)
                for cc in range(NCH):
                    ch(v.tensor_copy(fmp[:, cc * 128 + 1:cc * 128 + 128],
                                     fmx[:, cc * 128:cc * 128 + 127]))
                    ch(v.tensor_copy(fmp[:, cc * 128:cc * 128 + 1],
                                     prfm[:, cc * OWN + i:cc * OWN + i + 1]))
                v.wait_ge(dW, 80)
                ch(v.tensor_sub(dmix[:], fmx[:], fmp[:]))
                ch(v.tensor_mul(xrf[:], dmix[:], mur_s[:]))
                ch(v.tensor_add(xrf[:], xrf[:], fmp[:]))
                ch(v.tensor_mul(xkf[:], dmix[:], muk_s[:]))
                ch(v.tensor_add(xkf[:], xkf[:], fmp[:]), vM, n)
                v.wait_ge(tT2, n)
                ch(v.tensor_copy(kactf[:], p_tr[:]), vEv2, n)
                v.wait_ge(tV, n)
                v.wait_ge(aSig, n)
                ch(v.tensor_mul(ot[:], rt[:], p_v[:]), vO, n)
                ch(v.reduce_sum(col(SUM, 1), ot[:], axis=AX.X))
                ch(v.tensor_scalar_mul(col(M, 1), col(SUM, 1), 1.0 / D))
                v.wait_ge(aS2, n)
                ch(v.tensor_scalar_mul(col(Q, 1), col(SSQ, 1), 1.0 / D))
                ch(v.tensor_mul(col(MSQ, 1), col(M, 1), col(M, 1)))
                ch(v.tensor_sub(col(VAR, 1), col(Q, 1), col(MSQ, 1)), vT2, n)
                v.wait_ge(aR2, n)
                ch(v.tensor_scalar(col(NB, 1), col(M, 1), col(RSTD, 1), -1.0,
                                   ALU.mult, ALU.mult), vN2, n)
                v.wait_ge(aX2, n)
                if i >= 2:
                    v.wait_ge(dOut, 16 * (i - 1))
                ch(v.tensor_add(obuf[i % 2][:], xbuf[i % 2][:], lnout[:]), vFin, n)

        @block.scalar
        def _(a):
            _an = [0]

            def ca(ins, cross=None, tgt=None):
                if cross is None:
                    _an[0] += 1
                    ins.then_inc(aC, 1)
                    a.wait_ge(aC, _an[0])
                else:
                    ins.then_inc(cross, 1)
                    a.wait_ge(cross, tgt)
                return ins

            a.wait_ge(dPre, 48)
            pr = prraw[0:OWN, :]
            st = stat[0:OWN, :]
            ca(a.activation(sqscr[0:OWN, :], pr, AF.Square,
                            accum_out=st[:, 4:5]), aS, 1)
            a.wait_ge(vT, 1)
            ca(a.activation(st[:, 8:9], st[:, 3:4], AF.Ln, bias=EPS))
            ca(a.activation(st[:, 7:8], st[:, 8:9], AF.Exp, scale=-0.5), aR, 1)
            a.wait_ge(vN, 1)
            ca(a.activation(prln[0:OWN, :], pr, AF.Identity, scale=st[:, 7:8],
                            bias=st[:, 6:7]), aX, 1)

            for i in range(OWN):
                n = i + 1
                a.wait_ge(dIn, 16 * n)
                ca(a.activation(sqscr[:], xbuf[i % 2][:], AF.Square,
                                accum_out=col(SSQ)), aS, n + 1)
                a.wait_ge(vT, n + 1)
                ca(a.activation(col(LNV), col(VAR), AF.Ln, bias=EPS))
                ca(a.activation(col(RSTD), col(LNV), AF.Exp, scale=-0.5), aR, n + 1)
                a.wait_ge(vN, n + 1)
                ca(a.activation(xln[:], xbuf[i % 2][:], AF.Identity, scale=col(RSTD),
                                bias=col(NB)), aX, n + 1)
                a.wait_ge(tR, n)
                ca(a.activation(e1t[:], p_r[:], AF.Exp, scale=-1.0), aE1, n)
                ca(a.activation(rt[:], e1t[:], AF.Ln, bias=1.0))
                ca(a.activation(rt[:], rt[:], AF.Exp, scale=-1.0), aSig, n)
                a.wait_ge(tK, n)
                ca(a.activation(krelu[:], p_k[:], AF.Relu), aRel, n)
                ca(a.activation(kactb[:], krelu[:], AF.Square), aK, n)
                a.wait_ge(vO, n)
                ca(a.activation(sqscr[:], ot[:], AF.Square,
                                accum_out=col(SSQ, 1)), aS2, n)
                a.wait_ge(vT2, n)
                ca(a.activation(col(LNV, 1), col(VAR, 1), AF.Ln, bias=EPS))
                ca(a.activation(col(RSTD, 1), col(LNV, 1), AF.Exp, scale=-0.5), aR2, n)
                a.wait_ge(vN2, n)
                ca(a.activation(lnout[:], ot[:], AF.Identity, scale=col(RSTD, 1),
                                bias=col(NB, 1)), aX2, n)

        @block.tensor
        def _(t):
            t.wait_ge(aX, 1)
            t.wait_ge(dPre, 48)
            for cc in range(NCH):
                ins = t.transpose(p_tr[:, cc * OWN:(cc + 1) * OWN],
                                  prln[0:OWN, cc * 128:(cc + 1) * 128],
                                  idb[0:OWN, 0:OWN])
            ins.then_inc(tPre, 1)

            for i in range(OWN):
                n = i + 1
                t.wait_ge(aX, n + 1)
                t.wait_ge(vPre, 1)
                if i > 0:
                    t.wait_ge(vEv2, i)
                for cc in range(NCH):
                    ins = t.transpose(p_tr[:, cc * 128:(cc + 1) * 128],
                                      xln[:, cc * 128:(cc + 1) * 128], idb[:])
                ins.then_inc(tT, 1)
                t.wait_ge(vM, n)
                t.wait_ge(dW, 80)
                if i > 0:
                    t.wait_ge(aE1, i)
                for h in range(2):
                    for kc in range(NCH):
                        ins = t.matmul(p_r[:, h * 512:(h + 1) * 512],
                                       xrf[:, kc * 128:(kc + 1) * 128],
                                       wr_s[:, kc * D + h * 512:kc * D + (h + 1) * 512],
                                       start=(kc == 0), stop=(kc == NCH - 1))
                ins.then_inc(tR, 1)
                if i > 0:
                    t.wait_ge(aRel, i)
                for h in range(2):
                    for kc in range(NCH):
                        ins = t.matmul(p_k[:, h * 512:(h + 1) * 512],
                                       xkf[:, kc * 128:(kc + 1) * 128],
                                       wk_s[:, kc * D + h * 512:kc * D + (h + 1) * 512],
                                       start=(kc == 0), stop=(kc == NCH - 1))
                ins.then_inc(tK, 1)
                t.wait_ge(aK, n)
                t.wait_ge(vEv, n)
                for cc in range(NCH):
                    ins = t.transpose(p_tr[:, cc * 128:(cc + 1) * 128],
                                      kactb[:, cc * 128:(cc + 1) * 128], idb[:])
                ins.then_inc(tT2, 1)
                t.wait_ge(vEv2, n)
                if i > 0:
                    t.wait_ge(vO, i)
                for h in range(2):
                    for kc in range(NCH):
                        ins = t.matmul(p_v[:, h * 512:(h + 1) * 512],
                                       kactf[:, kc * 128:(kc + 1) * 128],
                                       wv_s[:, kc * D + h * 512:kc * D + (h + 1) * 512],
                                       start=(kc == 0), stop=(kc == NCH - 1))
                ins.then_inc(tV, 1)

    _CACHE["ffn"] = nc
    return nc



def build_l1a():
    if "l1a" in _CACHE:
        return _CACHE["l1a"]
    nc = bass.Bass("TRN2", target_bir_lowering=False)
    x_d = nc.dram_tensor("x", [OWN, TILE, D], F32, kind="ExternalInput").ap()
    wkvT = nc.dram_tensor("wkvT", [D, 2 * D], F32, kind="ExternalInput").ap()
    identb = nc.dram_tensor("identb", [128, 128], BF16, kind="ExternalInput").ap()
    constv = nc.dram_tensor("constv", [128, 2], F32, kind="ExternalInput").ap()
    o_xtm = nc.dram_tensor("o_xtm", [OWN, TILE, D], F32, kind="ExternalOutput").ap()
    o_xfm = nc.dram_tensor("o_xfm", [OWN, NCH, 128, 128], BF16, kind="ExternalOutput").ap()
    o_kfm = nc.dram_tensor("o_kfm", [NCH, 128, OWN, 128], BF16, kind="ExternalOutput").ap()
    o_vtm = nc.dram_tensor("o_vtm", [OWN, TILE, D], BF16, kind="ExternalOutput").ap()

    with ExitStack() as ctx:
        def sb(name, shape, dt):
            return ctx.enter_context(nc.sbuf_tensor(name, shape, dt))

        def psr(name, shape, dt):
            return ctx.enter_context(nc.psum_tensor(name, shape, dt))

        def sem(name):
            return ctx.enter_context(nc.semaphore(name))

        xbuf = [sb("xbuf0", [128, D], F32), sb("xbuf1", [128, D], F32)]
        xb4 = xbuf + [sb("xbuf2", [128, D], F32), sb("xbuf3", [128, D], F32)]
        xq32 = [sb("xq320", [128, D], F32), sb("xq321", [128, D], F32)]
        fmx = [sb("fmx0", [128, D], BF16), sb("fmx1", [128, D], BF16)]
        vbf = [sb("vbf0", [128, D], BF16), sb("vbf1", [128, D], BF16)]
        kfms = [sb("kfms0", [128, D], BF16), sb("kfms1", [128, D], BF16)]
        xln = sb("xln", [128, D], BF16)
        ktm = sb("ktm", [128, D], BF16)
        sqscr = sb("sqscr", [128, D], F32)
        wkv_s = sb("wkv_s", [128, NCH * 2 * D], BF16)
        idb = sb("idb", [128, 128], BF16)
        stat = sb("stat", [128, 16], F32)
        s_cst = sb("s_cst", [128, 2], F32)
        nc.const_aps.register(F32, EPS, s_cst[:, 0:1])
        SUM, M, Q, VAR, SSQ, MSQ, LNV, RSTD, NB = range(9)

        def col(k):
            return stat[:, k:k + 1]

        p_tr = psr("p_tr", [128, 1024], BF16)
        p_kv = psr("p_kv", [128, 2048], F32)

        dIn = sem("dIn"); dPre = sem("dPre"); dW = sem("dW")
        dO1 = sem("dO1"); dO2 = sem("dO2")
        aS = sem("aS"); vT = sem("vT"); aR = sem("aR"); vN = sem("vN")
        aX = sem("aX"); aX2 = sem("aX2"); tT = sem("tT"); vEv = sem("vEv")
        tKV = sem("tKV"); aKC = sem("aKC"); vVC = sem("vVC"); tT2 = sem("tT2")
        vKE = sem("vKE")
        vC = sem("vC"); aC = sem("aC")

        block = ctx.enter_context(nc.Block())

        @block.sync
        def _(sync):
            sync.dma_start(out=idb[:], in_=identb).then_inc(dPre, 16)
            sync.dma_start(out=s_cst[:], in_=constv).then_inc(dPre, 16)
            for i in range(OWN + 1):
                if i < OWN:
                    if i >= 2:
                        sync.wait_ge(aX2, i - 1)
                    sync.dma_start(out=xbuf[i % 2][:], in_=x_d[i]).then_inc(dIn, 16)
                if i >= 1:
                    k = i - 1
                    sync.wait_ge(aX2, k + 1)
                    sync.dma_start(out=o_xtm[k], in_=xq32[k % 2][:]).then_inc(dO1, 16)
                    sync.wait_ge(vEv, k + 1)
                    sync.dma_start(out=o_xfm[k].rearrange("c p t -> p c t"),
                                   in_=fmx[k % 2][:].rearrange("p (c t) -> p c t", c=NCH)).then_inc(dO1, 16)
            sync.wait_ge(dO1, 32 * OWN)
            sync.wait_ge(dO2, 32 * OWN)

        @block.gpsimd
        def _(g):
            g.dma_start(out=wkv_s[:].rearrange("p (c f) -> p c f", c=NCH),
                        in_=wkvT.rearrange("(c p) f -> p c f", p=128)).then_inc(dW, 16)
            for i in range(OWN):
                g.wait_ge(vVC, i + 1)
                g.dma_start(out=o_vtm[i], in_=vbf[i % 2][:]).then_inc(dO2, 16)
                g.wait_ge(vKE, i + 1)
                g.dma_start(out=o_kfm[:, :, i, :].rearrange("c p t -> p c t"),
                            in_=kfms[i % 2][:].rearrange("p (c t) -> p c t", c=NCH)).then_inc(dO2, 16)
            g.wait_ge(dO2, 32 * OWN)

        @block.vector
        def _(v):
            _vn = [0]

            def ch(ins, cross=None, tgt=None):
                if cross is None:
                    _vn[0] += 1
                    ins.then_inc(vC, 1)
                    v.wait_ge(vC, _vn[0])
                else:
                    ins.then_inc(cross, 1)
                    v.wait_ge(cross, tgt)
                return ins

            for i in range(OWN):
                n = i + 1
                v.wait_ge(dIn, 16 * n)
                ch(v.reduce_sum(col(SUM), xbuf[i % 2][:], axis=AX.X))
                ch(v.tensor_scalar_mul(col(M), col(SUM), 1.0 / D))
                v.wait_ge(aS, n)
                ch(v.tensor_scalar_mul(col(Q), col(SSQ), 1.0 / D))
                ch(v.tensor_mul(col(MSQ), col(M), col(M)))
                ch(v.tensor_sub(col(VAR), col(Q), col(MSQ)), vT, n)
                v.wait_ge(aR, n)
                ch(v.tensor_scalar(col(NB), col(M), col(RSTD), -1.0,
                                   ALU.mult, ALU.mult), vN, n)
                v.wait_ge(tT, n)
                if i >= 2:
                    v.wait_ge(dO1, 32 * (i - 1))
                ch(v.tensor_copy(fmx[i % 2][:], p_tr[:]), vEv, n)
                v.wait_ge(tKV, n)
                if i >= 2:
                    v.wait_ge(dO2, 32 * (i - 1))
                ch(v.tensor_copy(vbf[i % 2][:], p_kv[:, 1024:2048]), vVC, n)
                v.wait_ge(tT2, n)
                ch(v.tensor_copy(kfms[i % 2][:], p_tr[:]), vKE, n)

        @block.scalar
        def _(a):
            _an = [0]

            def ca(ins, cross=None, tgt=None):
                if cross is None:
                    _an[0] += 1
                    ins.then_inc(aC, 1)
                    a.wait_ge(aC, _an[0])
                else:
                    ins.then_inc(cross, 1)
                    a.wait_ge(cross, tgt)
                return ins

            a.wait_ge(dPre, 32)
            for i in range(OWN):
                n = i + 1
                a.wait_ge(dIn, 16 * n)
                ca(a.activation(sqscr[:], xbuf[i % 2][:], AF.Square,
                                accum_out=col(SSQ)), aS, n)
                a.wait_ge(vT, n)
                ca(a.activation(col(LNV), col(VAR), AF.Ln, bias=EPS))
                ca(a.activation(col(RSTD), col(LNV), AF.Exp, scale=-0.5), aR, n)
                a.wait_ge(vN, n)
                ca(a.activation(xln[:], xbuf[i % 2][:], AF.Identity, scale=col(RSTD),
                                bias=col(NB)), aX, n)
                if i >= 2:
                    a.wait_ge(dO1, 32 * (i - 1) - 16)
                ca(a.activation(xq32[i % 2][:], xbuf[i % 2][:], AF.Identity,
                                scale=col(RSTD), bias=col(NB)), aX2, n)
                a.wait_ge(tKV, n)
                ca(a.activation(ktm[:], p_kv[:, 0:1024], AF.Identity), aKC, n)

        @block.tensor
        def _(t):
            t.wait_ge(dPre, 32)
            for i in range(OWN):
                n = i + 1
                t.wait_ge(aX, n)
                if i > 0:
                    t.wait_ge(vEv, i)
                for cc in range(NCH):
                    ins = t.transpose(p_tr[:, cc * 128:(cc + 1) * 128],
                                      xln[:, cc * 128:(cc + 1) * 128], idb[:])
                ins.then_inc(tT, 1)
                t.wait_ge(tT, n)
                t.wait_ge(vEv, n)
                t.wait_ge(dW, 16)
                if i > 0:
                    t.wait_ge(aKC, i)
                    t.wait_ge(vVC, i)
                for h in range(4):
                    for kc in range(NCH):
                        ins = t.matmul(p_kv[:, h * 512:(h + 1) * 512],
                                       fmx[i % 2][:, kc * 128:(kc + 1) * 128],
                                       wkv_s[:, kc * 2048 + h * 512:kc * 2048 + (h + 1) * 512],
                                       start=(kc == 0), stop=(kc == NCH - 1))
                ins.then_inc(tKV, 1)
                t.wait_ge(aKC, n)
                t.wait_ge(vEv, n)
                if i > 0:
                    t.wait_ge(vKE, i)
                for cc in range(NCH):
                    ins = t.transpose(p_tr[:, cc * 128:(cc + 1) * 128],
                                      ktm[:, cc * 128:(cc + 1) * 128], idb[:])
                ins.then_inc(tT2, 1)

    _CACHE["l1a"] = nc
    return nc


def _l1b_schedule():
    visits = []
    blocks = []
    fins = []
    for p, slots in ((0, range(0, 16)), (1, range(16, 32))):
        nkb = 4 if p == 0 else 8
        for kb in range(nkb):
            act = [j for j in slots if 2 * j + 2 > 8 * kb]
            blocks.append(dict(p=p, kb=kb, first=len(visits), nact=len(act)))
            for j in act:
                W = min(128 * (2 * j + 2) - 1024 * kb, 1024)
                visits.append(dict(p=p, kb=kb, j=j, W=W,
                                   mask=(kb == j // 4), fin=False))
            for j in act:
                if (2 * j + 2) <= 8 * (kb + 1):
                    fins.append(dict(j=j, after=len(visits) - 1))
    for f_i, f in enumerate(fins):
        f["fi"] = f_i
    return visits, blocks, fins


def build_l1b():
    if "l1b" in _CACHE:
        return _CACHE["l1b"]
    visits, blocks, fins = _l1b_schedule()
    NV = len(visits)
    nc = bass.Bass("TRN2", target_bir_lowering=False)
    q_d = nc.dram_tensor("q", [OWN, NCH, 128, 128], BF16, kind="ExternalInput").ap()
    kf_d = nc.dram_tensor("kf", [NCH, 128, S], BF16, kind="ExternalInput").ap()
    vf_d = nc.dram_tensor("vf", [S, D], BF16, kind="ExternalInput").ap()
    xtm_d = nc.dram_tensor("xtm", [OWN, TILE, D], F32, kind="ExternalInput").ap()
    x_d = nc.dram_tensor("x", [OWN, TILE, D], F32, kind="ExternalInput").ap()
    maskp = nc.dram_tensor("maskp", [2, 128, 128], F32, kind="ExternalInput").ap()
    identb = nc.dram_tensor("identb", [128, 128], BF16, kind="ExternalInput").ap()
    constv = nc.dram_tensor("constv", [128, 2], F32, kind="ExternalInput").ap()
    ostream = nc.dram_tensor("ostream", [OWN, TILE, D], F32, kind="ExternalOutput").ap()

    with ExitStack() as ctx:
        def sb(name, shape, dt):
            return ctx.enter_context(nc.sbuf_tensor(name, shape, dt))

        def psr(name, shape, dt):
            return ctx.enter_context(nc.psum_tensor(name, shape, dt))

        def sem(name):
            return ctx.enter_context(nc.semaphore(name))

        q_sb = sb("q_sb", [128, 16 * 1024], BF16)
        acc = sb("acc", [128, 16 * 1024], F32)
        kbuf = [sb("kbuf0", [128, NCH * 1024], BF16), sb("kbuf1", [128, NCH * 1024], BF16)]
        vbuf = [sb("vbuf0", [128, NCH * 1024], BF16), sb("vbuf1", [128, NCH * 1024], BF16)]
        P_sb = sb("P_sb", [128, 1024], BF16)
        pt_sb = sb("pt_sb", [128, 1024], BF16)
        msk = sb("msk", [128, 256], F32)
        negm = sb("negm", [128, 16], F32)
        lcol = sb("lcol", [128, 16], F32)
        alpha = sb("alpha", [128, 16], F32)
        rsum = sb("rsum", [128, 16], F32)
        tmpa = sb("tmpa", [128, 16], F32)
        xtb = [sb("xtb0", [128, D], F32), sb("xtb1", [128, D], F32)]
        xrb = [sb("xrb0", [128, D], F32), sb("xrb1", [128, D], F32)]
        obuf = [sb("obuf0", [128, D], F32), sb("obuf1", [128, D], F32)]
        otb = sb("otb", [128, D], F32)
        lnout = sb("lnout", [128, D], F32)
        sqscr = sb("sqscr", [128, D], F32)
        stat = sb("stat", [128, 16], F32)
        s_cst = sb("s_cst", [128, 2], F32)
        idb = sb("idb", [128, 128], BF16)
        nc.const_aps.register(F32, EPS, s_cst[:, 0:1])
        SUM, M, Q, VAR, SSQ, MSQ, LNV, RSTD, NB = range(9)

        def col(k):
            return stat[:, k:k + 1]

        p_sc = psr("p_sc", [128, 1024], F32)
        p_pv = psr("p_pv", [128, 1024], F32)
        p_pt = psr("p_pt", [128, 1024], BF16)

        dPre = sem("dPre"); dQ = sem("dQ"); dKV = sem("dKV"); dFin = sem("dFin")
        dOut = sem("dOut")
        tQK = sem("tQK"); vStat = sem("vStat"); aExp = sem("aExp")
        tPT = sem("tPT"); vPTe = sem("vPTe"); tPV = sem("tPV"); vAcc = sem("vAcc")
        vMS = sem("vMS"); vOt = sem("vOt")
        aS2 = sem("aS2"); vT2 = sem("vT2"); aR2 = sem("aR2"); vN2 = sem("vN2")
        aX2 = sem("aX2"); vFin = sem("vFin")
        vC = sem("vC"); aC = sem("aC")

        block = ctx.enter_context(nc.Block())

        p1_first = next(b["first"] for b in blocks if b["p"] == 1)

        blk_end = []
        for bi, b in enumerate(blocks):
            blk_end.append(b["first"] + b["nact"])

        @block.sync
        def _(sync):
            sync.dma_start(out=msk[:, 0:128], in_=maskp[0]).then_inc(dPre, 16)
            sync.dma_start(out=msk[:, 128:256], in_=maskp[1]).then_inc(dPre, 16)
            sync.dma_start(out=s_cst[:], in_=constv).then_inc(dPre, 16)
            sync.dma_start(out=idb[:], in_=identb).then_inc(dPre, 16)
            for s in range(16):
                sync.dma_start(out=q_sb[:].rearrange("p (s c t) -> p s c t", s=16, c=NCH)[:, s],
                               in_=q_d[s].rearrange("c p t -> p c t")).then_inc(dQ, 16)
            for bi, b in enumerate(blocks):
                if b["p"] == 1 and b["kb"] == 0:
                    sync.wait_ge(vFin, 16)
                    for s in range(16):
                        sync.dma_start(
                            out=q_sb[:].rearrange("p (s c t) -> p s c t", s=16, c=NCH)[:, s],
                            in_=q_d[16 + s].rearrange("c p t -> p c t")).then_inc(dQ, 16)
                if bi >= 2:
                    sync.wait_ge(tPV, blk_end[bi - 2])
                kb = b["kb"]
                for dc in range(NCH):
                    sync.dma_start(
                        out=kbuf[bi % 2][:, dc * 1024:(dc + 1) * 1024],
                        in_=kf_d[dc, :, kb * 1024:(kb + 1) * 1024]).then_inc(dKV, 16)
                for sub in range(NCH):
                    sync.dma_start(
                        out=vbuf[bi % 2][:, sub * 1024:(sub + 1) * 1024],
                        in_=vf_d[kb * 1024 + sub * 128:kb * 1024 + (sub + 1) * 128, :]).then_inc(dKV, 16)
                for f in [f for f in fins if visits[f["after"]]["kb"] == kb and visits[f["after"]]["p"] == b["p"]]:
                    fi = f["fi"]
                    if fi >= 2:
                        sync.wait_ge(vFin, fi - 1)
                    sync.dma_start(out=xtb[fi % 2][:], in_=xtm_d[f["j"]]).then_inc(dFin, 16)
                    sync.dma_start(out=xrb[fi % 2][:], in_=x_d[f["j"]]).then_inc(dFin, 16)
            sync.wait_ge(dOut, 16 * OWN)

        @block.gpsimd
        def _(g):
            g.memset(acc[:], 0.0).then_inc(vMS, 1)
            g.memset(lcol[:], 0.0).then_inc(vMS, 1)
            g.memset(negm[:], 3.0e38).then_inc(vMS, 1)
            for f in fins[:16]:
                fi = f["fi"]
                g.wait_ge(vFin, fi + 1)
                g.dma_start(out=ostream[f["j"]], in_=obuf[fi % 2][:]).then_inc(dOut, 16)
            g.memset(acc[:], 0.0).then_inc(vMS, 1)
            g.memset(lcol[:], 0.0).then_inc(vMS, 1)
            g.memset(negm[:], 3.0e38).then_inc(vMS, 1)
            for f in fins[16:]:
                fi = f["fi"]
                g.wait_ge(vFin, fi + 1)
                g.dma_start(out=ostream[f["j"]], in_=obuf[fi % 2][:]).then_inc(dOut, 16)
            g.wait_ge(dOut, 16 * OWN)

        @block.vector
        def _(v):
            _vn = [0]

            def ch(ins, cross=None, tgt=None):
                if cross is None:
                    _vn[0] += 1
                    ins.then_inc(vC, 1)
                    v.wait_ge(vC, _vn[0])
                else:
                    ins.then_inc(cross, 1)
                    v.wait_ge(cross, tgt)
                return ins

            v.wait_ge(dPre, 64)
            fin_iter = iter(fins)
            nextf = next(fin_iter, None)
            for u, vis in enumerate(visits):
                j = vis["j"]; s = j - 16 * vis["p"]; W = vis["W"]
                if vis["p"] == 0 and u == 0:
                    v.wait_ge(vMS, 3)
                if vis["p"] == 1 and u == p1_first:
                    v.wait_ge(vMS, 6)
                v.wait_ge(tQK, u + 1)
                if vis["mask"]:
                    r0 = 2 * j - 8 * vis["kb"]
                    ch(v.tensor_add(p_sc[:, r0 * 128:(r0 + 1) * 128],
                                    p_sc[:, r0 * 128:(r0 + 1) * 128], msk[:, 0:128]))
                    ch(v.tensor_add(p_sc[:, (r0 + 1) * 128:(r0 + 2) * 128],
                                    p_sc[:, (r0 + 1) * 128:(r0 + 2) * 128], msk[:, 128:256]))
                ch(v.reduce_max(tmpa[:, 0:1], p_sc[:, 0:W], axis=AX.X, negate=True))
                ch(v.tensor_tensor(tmpa[:, 1:2], negm[:, s:s + 1], tmpa[:, 0:1], ALU.min))
                ch(v.tensor_sub(tmpa[:, 2:3], tmpa[:, 1:2], negm[:, s:s + 1]))
                ch(v.tensor_copy(negm[:, s:s + 1], tmpa[:, 1:2]), vStat, u + 1)
                v.wait_ge(tPT, u + 1)
                ch(v.tensor_copy(pt_sb[:, 0:W], p_pt[:, 0:W]), vPTe, u + 1)
                v.wait_ge(tPV, u + 1)
                v.wait_ge(aExp, 2 * (u + 1))
                ch(v.scalar_tensor_tensor(lcol[:, s:s + 1], lcol[:, s:s + 1],
                                          alpha[:, s:s + 1], rsum[:, s:s + 1],
                                          ALU.mult, ALU.add))
                ch(v.scalar_tensor_tensor(acc[:, s * 1024:(s + 1) * 1024],
                                          acc[:, s * 1024:(s + 1) * 1024],
                                          alpha[:, s:s + 1], p_pv[:],
                                          ALU.mult, ALU.add), vAcc, u + 1)
                while nextf is not None and nextf["after"] == u:
                    f = nextf; fi = f["fi"]; fs = f["j"] - 16 * vis["p"]
                    fn = fi + 1
                    v.wait_ge(dFin, 32 * fn)
                    ch(v.reciprocal(tmpa[:, 3:4], lcol[:, fs:fs + 1]))
                    ch(v.scalar_tensor_tensor(otb[:], acc[:, fs * 1024:(fs + 1) * 1024],
                                              tmpa[:, 3:4], xtb[fi % 2][:],
                                              ALU.mult, ALU.add), vOt, fn)
                    ch(v.reduce_sum(col(SUM), otb[:], axis=AX.X))
                    ch(v.tensor_scalar_mul(col(M), col(SUM), 1.0 / D))
                    v.wait_ge(aS2, fn)
                    ch(v.tensor_scalar_mul(col(Q), col(SSQ), 1.0 / D))
                    ch(v.tensor_mul(col(MSQ), col(M), col(M)))
                    ch(v.tensor_sub(col(VAR), col(Q), col(MSQ)), vT2, fn)
                    v.wait_ge(aR2, fn)
                    ch(v.tensor_scalar(col(NB), col(M), col(RSTD), -1.0,
                                       ALU.mult, ALU.mult), vN2, fn)
                    v.wait_ge(aX2, fn)
                    if fi >= 2:
                        v.wait_ge(dOut, 16 * (fi - 1))
                    ch(v.tensor_add(obuf[fi % 2][:], xrb[fi % 2][:], lnout[:]), vFin, fn)
                    nextf = next(fin_iter, None)

        @block.scalar
        def _(a):
            _an = [0]

            def ca(ins, cross=None, tgt=None):
                if cross is None:
                    _an[0] += 1
                    ins.then_inc(aC, 1)
                    a.wait_ge(aC, _an[0])
                else:
                    ins.then_inc(cross, 1)
                    a.wait_ge(cross, tgt)
                return ins

            a.wait_ge(dPre, 64)
            fin_iter = iter(fins)
            nextf = next(fin_iter, None)
            for u, vis in enumerate(visits):
                j = vis["j"]; s = j - 16 * vis["p"]; W = vis["W"]
                a.wait_ge(vStat, u + 1)
                ca(a.activation(alpha[:, s:s + 1], tmpa[:, 2:3], AF.Exp), aExp, 2 * u + 1)
                ca(a.activation(P_sb[:, 0:W], p_sc[:, 0:W], AF.Exp,
                                bias=negm[:, s:s + 1],
                                accum_out=rsum[:, s:s + 1]), aExp, 2 * u + 2)
                while nextf is not None and nextf["after"] == u:
                    f = nextf; fi = f["fi"]; fn = fi + 1
                    a.wait_ge(vOt, fn)
                    ca(a.activation(sqscr[:], otb[:], AF.Square,
                                    accum_out=col(SSQ)), aS2, fn)
                    a.wait_ge(vT2, fn)
                    ca(a.activation(col(LNV), col(VAR), AF.Ln, bias=EPS))
                    ca(a.activation(col(RSTD), col(LNV), AF.Exp, scale=-0.5), aR2, fn)
                    a.wait_ge(vN2, fn)
                    ca(a.activation(lnout[:], otb[:], AF.Identity, scale=col(RSTD),
                                    bias=col(NB)), aX2, fn)
                    nextf = next(fin_iter, None)

        @block.tensor
        def _(t):
            t.wait_ge(dPre, 64)
            for u, vis in enumerate(visits):
                j = vis["j"]; s = j - 16 * vis["p"]; W = vis["W"]; kb = vis["kb"]
                bi = next(i for i, b in enumerate(blocks)
                          if b["p"] == vis["p"] and b["kb"] == kb)
                t.wait_ge(dQ, 256 * (vis["p"] + 1))
                t.wait_ge(dKV, 256 * (bi + 1))
                if u > 0:
                    t.wait_ge(aExp, 2 * u)
                nh = (W + 511) // 512
                for h in range(nh):
                    w_h = min(W - h * 512, 512)
                    for kc in range(NCH):
                        ins = t.matmul(
                            p_sc[:, h * 512:h * 512 + w_h],
                            q_sb[:, s * 1024 + kc * 128:s * 1024 + (kc + 1) * 128],
                            kbuf[bi % 2][:, kc * 1024 + h * 512:kc * 1024 + h * 512 + w_h],
                            start=(kc == 0), stop=(kc == NCH - 1))
                ins.then_inc(tQK, 1)
                t.wait_ge(aExp, 2 * u + 2)
                if u > 0:
                    t.wait_ge(vPTe, u)
                nsub = W // 128
                for sub in range(nsub):
                    ins = t.transpose(p_pt[:, sub * 128:(sub + 1) * 128],
                                      P_sb[:, sub * 128:(sub + 1) * 128], idb[:])
                ins.then_inc(tPT, 1)
                t.wait_ge(vPTe, u + 1)
                if u > 0:
                    t.wait_ge(vAcc, u)
                for sub in range(nsub):
                    for h in range(2):
                        ins = t.matmul(p_pv[:, h * 512:(h + 1) * 512],
                                       pt_sb[:, sub * 128:(sub + 1) * 128],
                                       vbuf[bi % 2][:, sub * 1024 + h * 512:sub * 1024 + (h + 1) * 512],
                                       start=(sub == 0), stop=(sub == nsub - 1))
                ins.then_inc(tPV, 1)

    _CACHE["l1b"] = nc
    return nc



def build_l3a():
    if "l3a" in _CACHE:
        return _CACHE["l3a"]
    nc = bass.Bass("TRN2", target_bir_lowering=False)
    stream = nc.dram_tensor("stream", [OWN, TILE, D], F32, kind="ExternalInput").ap()
    prow = nc.dram_tensor("prow", [OWN, D], F32, kind="ExternalInput").ap()
    wkT = nc.dram_tensor("wkT", [D, D], F32, kind="ExternalInput").ap()
    wvT = nc.dram_tensor("wvT", [D, D], F32, kind="ExternalInput").ap()
    wrT = nc.dram_tensor("wrT", [D, D], F32, kind="ExternalInput").ap()
    tmkf = nc.dram_tensor("tmkf", [128, D], F32, kind="ExternalInput").ap()
    tmvf = nc.dram_tensor("tmvf", [128, D], F32, kind="ExternalInput").ap()
    tmrf = nc.dram_tensor("tmrf", [128, D], F32, kind="ExternalInput").ap()
    td_col = nc.dram_tensor("td_col", [128, 8], F32, kind="ExternalInput").ap()
    identb = nc.dram_tensor("identb", [128, 128], BF16, kind="ExternalInput").ap()
    identf = nc.dram_tensor("identf", [128, 128], F32, kind="ExternalInput").ap()
    constv = nc.dram_tensor("constv", [128, 2], F32, kind="ExternalInput").ap()
    z_sp = nc.dram_tensor("z_sp", [OWN, 128, D], F32, kind="ExternalOutput").ap()
    t1_sp = nc.dram_tensor("t1_sp", [OWN, 128, D], F32, kind="ExternalOutput").ap()
    r_sp = nc.dram_tensor("r_sp", [OWN, 128, D], BF16, kind="ExternalOutput").ap()
    tot_o = nc.dram_tensor("tot_o", [OWN, 2, 128, 8], F32, kind="ExternalOutput").ap()

    with ExitStack() as ctx:
        def sb(name, shape, dt):
            return ctx.enter_context(nc.sbuf_tensor(name, shape, dt))

        def psr(name, shape, dt):
            return ctx.enter_context(nc.psum_tensor(name, shape, dt))

        def sem(name):
            return ctx.enter_context(nc.semaphore(name))

        xbuf = [sb("xbuf0", [128, D], F32), sb("xbuf1", [128, D], F32)]
        xb4 = xbuf + [sb("xbuf2", [128, D], F32), sb("xbuf3", [128, D], F32)]
        sqscr = sb("sqscr", [128, D], F32)
        xln = sb("xln", [128, D], BF16)
        fmx = sb("fmx", [128, D], BF16)
        fmp = sb("fmp", [128, D], BF16)
        dmix = sb("dmix", [128, D], BF16)
        xkf = sb("xkf", [128, D], BF16)
        xvf = sb("xvf", [128, D], BF16)
        xrf = sb("xrf", [128, D], BF16)
        ktm = sb("ktm", [128, D], F32)
        vtm_ = sb("vtm_", [128, D], F32)
        kfm = sb("kfm", [128, D], F32)
        vfm = sb("vfm", [128, D], F32)
        e1t = sb("e1t", [128, D], F32)
        rtm = sb("rtm", [128, D], BF16)
        t1t = [sb("t1t0", [128, D], F32), sb("t1t1", [128, D], F32)]
        zt = [sb("zt0", [128, D], F32), sb("zt1", [128, D], F32)]
        rfm = [sb("rfm0", [128, D], BF16), sb("rfm1", [128, D], BF16)]
        Sl = sb("Sl", [128, D], F32)
        Tl = sb("Tl", [128, D], F32)
        tots = [sb("tots0", [128, 16], F32), sb("tots1", [128, 16], F32)]
        wk_s = sb("wk_s", [128, NCH * D], BF16)
        wv_s = sb("wv_s", [128, NCH * D], BF16)
        wr_s = sb("wr_s", [128, NCH * D], BF16)
        tmk_s = sb("tmk_s", [128, D], BF16)
        tmv_s = sb("tmv_s", [128, D], BF16)
        tmr_s = sb("tmr_s", [128, D], BF16)
        idb = sb("idb", [128, 128], BF16)
        idf = sb("idf", [128, 128], F32)
        prraw = sb("prraw", [128, D], F32)
        prln = sb("prln", [128, D], BF16)
        prfm = sb("prfm", [128, NCH * OWN], BF16)
        EWf = sb("EWf", [128, D], F32)
        onef = sb("onef", [128, 128], F32)
        wcol = sb("wcol", [128, 8], F32)
        ewcol = sb("ewcol", [128, 8], F32)
        stat = sb("stat", [128, 32], F32)
        s_cst = sb("s_cst", [128, 2], F32)
        nc.const_aps.register(F32, EPS, s_cst[:, 0:1])
        nc.const_aps.register(F32, 1.0, s_cst[:, 1:2])
        SUM, M, Q, VAR, SSQ, MSQ, LNV, RSTD, NB = range(9)

        def col(k):
            return stat[:, k:k + 1]

        p_tr = psr("p_tr", [128, 1024], BF16)
        p_k = psr("p_k", [128, 1024], F32)
        p_v = psr("p_v", [128, 1024], F32)
        p_r = psr("p_r", [128, 1024], F32)

        dIn = sem("dIn"); dPre = sem("dPre"); dW = sem("dW"); dSp = sem("dSp")
        aS = sem("aS"); vT = sem("vT"); aR = sem("aR"); vN = sem("vN"); aX = sem("aX")
        tT = sem("tT"); vEv = sem("vEv"); vM = sem("vM")
        tP1 = sem("tP1"); tP2 = sem("tP2"); tP3 = sem("tP3")
        vKC = sem("vKC"); vVC = sem("vVC"); tTK = sem("tTK"); tTV = sem("tTV")
        vKF = sem("vKF"); vVF = sem("vVF"); aT1 = sem("aT1"); vZ = sem("vZ")
        vScan = sem("vScan"); vTot = sem("vTot"); aE1 = sem("aE1"); aSig = sem("aSig")
        tTR = sem("tTR"); vRE = sem("vRE")
        vPre = sem("vPre"); tPre = sem("tPre"); vEW = sem("vEW")
        vC = sem("vC"); aC = sem("aC")

        block = ctx.enter_context(nc.Block())

        @block.sync
        def _(sync):
            sync.dma_start(out=idb[:], in_=identb).then_inc(dPre, 16)
            sync.dma_start(out=idf[:], in_=identf).then_inc(dPre, 16)
            sync.dma_start(out=s_cst[:], in_=constv).then_inc(dPre, 16)
            sync.dma_start(out=prraw[0:OWN, :], in_=prow).then_inc(dPre, 16)
            sync.dma_start(out=wcol[:], in_=td_col).then_inc(dPre, 16)
            for i in range(OWN):
                if i >= 2:
                    sync.wait_ge(vM, i - 1)
                sync.dma_start(out=xbuf[i % 2][:], in_=stream[i]).then_inc(dIn, 16)
                if i >= 1:
                    k = i - 1
                    sync.wait_ge(vZ, k + 1)
                    sync.dma_start(out=z_sp[k], in_=zt[k % 2][:]).then_inc(dSp, 16)
                    sync.dma_start(out=t1_sp[k], in_=t1t[k % 2][:]).then_inc(dSp, 16)
                    sync.wait_ge(vRE, k + 1)
                    sync.dma_start(out=r_sp[k], in_=rfm[k % 2][:]).then_inc(dSp, 16)
                    sync.wait_ge(vTot, k + 1)
                    sync.dma_start(out=tot_o[k].rearrange("q p c -> p q c"),
                                   in_=tots[k % 2][:].rearrange("p (q c) -> p q c", q=2)).then_inc(dSp, 16)
            k = OWN - 1
            sync.wait_ge(vZ, k + 1)
            sync.dma_start(out=z_sp[k], in_=zt[k % 2][:]).then_inc(dSp, 16)
            sync.dma_start(out=t1_sp[k], in_=t1t[k % 2][:]).then_inc(dSp, 16)
            sync.wait_ge(vRE, k + 1)
            sync.dma_start(out=r_sp[k], in_=rfm[k % 2][:]).then_inc(dSp, 16)
            sync.wait_ge(vTot, k + 1)
            sync.dma_start(out=tot_o[k].rearrange("q p c -> p q c"),
                           in_=tots[k % 2][:].rearrange("p (q c) -> p q c", q=2)).then_inc(dSp, 16)
            sync.wait_ge(dSp, 64 * OWN)

        @block.gpsimd
        def _(g):
            for w_dram, w_sb in ((wkT, wk_s), (wvT, wv_s), (wrT, wr_s)):
                g.dma_start(out=w_sb[:].rearrange("p (c f) -> p c f", c=NCH),
                            in_=w_dram.rearrange("(c p) f -> p c f", p=128)).then_inc(dW, 16)
            g.dma_start(out=tmk_s[:], in_=tmkf).then_inc(dW, 16)
            g.dma_start(out=tmv_s[:], in_=tmvf).then_inc(dW, 16)
            g.dma_start(out=tmr_s[:], in_=tmrf).then_inc(dW, 16)
            g.memset(onef[:], 1.0).then_inc(dW, 16)

        @block.vector
        def _(v):
            _vn = [0]

            def ch(ins, cross=None, tgt=None):
                if cross is None:
                    _vn[0] += 1
                    ins.then_inc(vC, 1)
                    v.wait_ge(vC, _vn[0])
                else:
                    ins.then_inc(cross, 1)
                    v.wait_ge(cross, tgt)
                return ins

            v.wait_ge(dPre, 80)
            v.wait_ge(dW, 112)
            v.wait_ge(aC, 2)
            for cc in range(NCH):
                ch(v.tensor_scalar_mul(EWf[:, cc * 128:(cc + 1) * 128], onef[:],
                                       ewcol[:, cc:cc + 1]))
            v.nop().then_inc(vEW, 1)
            pr = prraw[0:OWN, :]
            st = stat[0:OWN, :]
            ch(v.reduce_sum(st[:, 0:1], pr, axis=AX.X))
            ch(v.tensor_scalar_mul(st[:, 1:2], st[:, 0:1], 1.0 / D))
            v.wait_ge(aS, 1)
            ch(v.tensor_scalar_mul(st[:, 2:3], st[:, 4:5], 1.0 / D))
            ch(v.tensor_mul(st[:, 5:6], st[:, 1:2], st[:, 1:2]))
            ch(v.tensor_sub(st[:, 3:4], st[:, 2:3], st[:, 5:6]), vT, 1)
            v.wait_ge(aR, 1)
            ch(v.tensor_scalar(st[:, 6:7], st[:, 1:2], st[:, 7:8], -1.0,
                               ALU.mult, ALU.mult), vN, 1)
            v.wait_ge(tPre, 1)
            ch(v.tensor_copy(prfm[:], p_tr[:, 0:NCH * OWN]), vPre, 1)

            for i in range(OWN):
                n = i + 1
                v.wait_ge(dIn, 16 * n)
                ch(v.reduce_sum(col(SUM), xbuf[i % 2][:], axis=AX.X))
                ch(v.tensor_scalar_mul(col(M), col(SUM), 1.0 / D))
                v.wait_ge(aS, n + 1)
                ch(v.tensor_scalar_mul(col(Q), col(SSQ), 1.0 / D))
                ch(v.tensor_mul(col(MSQ), col(M), col(M)))
                ch(v.tensor_sub(col(VAR), col(Q), col(MSQ)), vT, n + 1)
                v.wait_ge(aR, n + 1)
                ch(v.tensor_scalar(col(NB), col(M), col(RSTD), -1.0,
                                   ALU.mult, ALU.mult), vN, n + 1)
                v.wait_ge(tT, n)
                if i > 0:
                    v.wait_ge(tP3, i)
                ch(v.tensor_copy(fmx[:], p_tr[:]), vEv, n)
                v.wait_ge(vPre, 1)
                for cc in range(NCH):
                    ch(v.tensor_copy(fmp[:, cc * 128 + 1:cc * 128 + 128],
                                     fmx[:, cc * 128:cc * 128 + 127]))
                    ch(v.tensor_copy(fmp[:, cc * 128:cc * 128 + 1],
                                     prfm[:, cc * OWN + i:cc * OWN + i + 1]))
                ch(v.tensor_sub(dmix[:], fmx[:], fmp[:]))
                ch(v.tensor_mul(xkf[:], dmix[:], tmk_s[:]))
                ch(v.tensor_add(xkf[:], xkf[:], fmp[:]))
                ch(v.tensor_mul(xvf[:], dmix[:], tmv_s[:]))
                ch(v.tensor_add(xvf[:], xvf[:], fmp[:]))
                ch(v.tensor_mul(xrf[:], dmix[:], tmr_s[:]))
                ch(v.tensor_add(xrf[:], xrf[:], fmp[:]), vM, n)
                v.wait_ge(tP1, n)
                ch(v.tensor_copy(ktm[:], p_k[:]), vKC, n)
                v.wait_ge(tP2, n)
                ch(v.tensor_copy(vtm_[:], p_v[:]), vVC, n)
                v.wait_ge(tTK, n)
                ch(v.tensor_copy(kfm[:], p_k[:]), vKF, n)
                v.wait_ge(tTV, n)
                ch(v.tensor_copy(vfm[:], p_v[:]), vVF, n)
                v.wait_ge(aT1, n)
                if i >= 2:
                    v.wait_ge(dSp, 64 * (i - 1))
                ch(v.tensor_mul(zt[i % 2][:], t1t[i % 2][:], vfm[:]), vZ, n)
                v.wait_ge(vEW, 1)
                for cc in range(NCH):
                    ch(v.tensor_tensor_scan(Sl[:, cc * 128:(cc + 1) * 128],
                                            EWf[:, cc * 128:(cc + 1) * 128],
                                            zt[i % 2][:, cc * 128:(cc + 1) * 128],
                                            0.0, ALU.mult, ALU.add))
                    ch(v.tensor_tensor_scan(Tl[:, cc * 128:(cc + 1) * 128],
                                            EWf[:, cc * 128:(cc + 1) * 128],
                                            t1t[i % 2][:, cc * 128:(cc + 1) * 128],
                                            0.0, ALU.mult, ALU.add))
                v.nop().then_inc(vScan, 1)
                v.wait_ge(vScan, n)
                ch(v.tensor_copy(tots[i % 2][:, 0:8],
                                 Sl[:].rearrange("p (c t) -> p c t", c=NCH)[:, :, 127]))
                ch(v.tensor_copy(tots[i % 2][:, 8:16],
                                 Tl[:].rearrange("p (c t) -> p c t", c=NCH)[:, :, 127]), vTot, n)
                v.wait_ge(tTR, n)
                ch(v.tensor_copy(rfm[i % 2][:], p_tr[:]), vRE, n)

        @block.scalar
        def _(a):
            _an = [0]

            def ca(ins, cross=None, tgt=None):
                if cross is None:
                    _an[0] += 1
                    ins.then_inc(aC, 1)
                    a.wait_ge(aC, _an[0])
                else:
                    ins.then_inc(cross, 1)
                    a.wait_ge(cross, tgt)
                return ins

            a.wait_ge(dPre, 80)
            ca(a.activation(ewcol[:], wcol[:], AF.Exp))
            ca(a.activation(ewcol[:], ewcol[:], AF.Exp, scale=-1.0))
            pr = prraw[0:OWN, :]
            st = stat[0:OWN, :]
            ca(a.activation(sqscr[0:OWN, :], pr, AF.Square, accum_out=st[:, 4:5]), aS, 1)
            a.wait_ge(vT, 1)
            ca(a.activation(st[:, 8:9], st[:, 3:4], AF.Ln, bias=EPS))
            ca(a.activation(st[:, 7:8], st[:, 8:9], AF.Exp, scale=-0.5), aR, 1)
            a.wait_ge(vN, 1)
            ca(a.activation(prln[0:OWN, :], pr, AF.Identity, scale=st[:, 7:8],
                            bias=st[:, 6:7]), aX, 1)

            for i in range(OWN):
                n = i + 1
                a.wait_ge(dIn, 16 * n)
                ca(a.activation(sqscr[:], xbuf[i % 2][:], AF.Square,
                                accum_out=col(SSQ)), aS, n + 1)
                a.wait_ge(vT, n + 1)
                ca(a.activation(col(LNV), col(VAR), AF.Ln, bias=EPS))
                ca(a.activation(col(RSTD), col(LNV), AF.Exp, scale=-0.5), aR, n + 1)
                a.wait_ge(vN, n + 1)
                ca(a.activation(xln[:], xbuf[i % 2][:], AF.Identity, scale=col(RSTD),
                                bias=col(NB)), aX, n + 1)
                a.wait_ge(vKF, n)
                if i >= 2:
                    a.wait_ge(dSp, 64 * (i - 1))
                ca(a.activation(t1t[i % 2][:], kfm[:], AF.Exp), aT1, n)
                a.wait_ge(tP3, n)
                ca(a.activation(e1t[:], p_r[:], AF.Exp, scale=-1.0), aE1, n)
                ca(a.activation(rtm[:], e1t[:], AF.Ln, bias=1.0))
                ca(a.activation(rtm[:], rtm[:], AF.Exp, scale=-1.0), aSig, n)

        @block.tensor
        def _(t):
            t.wait_ge(dPre, 80)
            t.wait_ge(aX, 1)
            for cc in range(NCH):
                ins = t.transpose(p_tr[:, cc * OWN:(cc + 1) * OWN],
                                  prln[0:OWN, cc * 128:(cc + 1) * 128],
                                  idb[0:OWN, 0:OWN])
            ins.then_inc(tPre, 1)
            t.wait_ge(dW, 112)
            for i in range(OWN):
                n = i + 1
                t.wait_ge(aX, n + 1)
                t.wait_ge(vPre, 1)
                if i > 0:
                    t.wait_ge(vRE, i)
                for cc in range(NCH):
                    ins = t.transpose(p_tr[:, cc * 128:(cc + 1) * 128],
                                      xln[:, cc * 128:(cc + 1) * 128], idb[:])
                ins.then_inc(tT, 1)
                t.wait_ge(vM, n)
                if i > 0:
                    t.wait_ge(vKF, i)
                for h in range(2):
                    for kc in range(NCH):
                        ins = t.matmul(p_k[:, h * 512:(h + 1) * 512],
                                       xkf[:, kc * 128:(kc + 1) * 128],
                                       wk_s[:, kc * D + h * 512:kc * D + (h + 1) * 512],
                                       start=(kc == 0), stop=(kc == NCH - 1))
                ins.then_inc(tP1, 1)
                if i > 0:
                    t.wait_ge(vVF, i)
                for h in range(2):
                    for kc in range(NCH):
                        ins = t.matmul(p_v[:, h * 512:(h + 1) * 512],
                                       xvf[:, kc * 128:(kc + 1) * 128],
                                       wv_s[:, kc * D + h * 512:kc * D + (h + 1) * 512],
                                       start=(kc == 0), stop=(kc == NCH - 1))
                ins.then_inc(tP2, 1)
                if i > 0:
                    t.wait_ge(aE1, i)
                for h in range(2):
                    for kc in range(NCH):
                        ins = t.matmul(p_r[:, h * 512:(h + 1) * 512],
                                       xrf[:, kc * 128:(kc + 1) * 128],
                                       wr_s[:, kc * D + h * 512:kc * D + (h + 1) * 512],
                                       start=(kc == 0), stop=(kc == NCH - 1))
                ins.then_inc(tP3, 1)
                t.wait_ge(vKC, n)
                for cc in range(NCH):
                    ins = t.transpose(p_k[:, cc * 128:(cc + 1) * 128],
                                      ktm[:, cc * 128:(cc + 1) * 128], idf[:])
                ins.then_inc(tTK, 1)
                t.wait_ge(vVC, n)
                for cc in range(NCH):
                    ins = t.transpose(p_v[:, cc * 128:(cc + 1) * 128],
                                      vtm_[:, cc * 128:(cc + 1) * 128], idf[:])
                ins.then_inc(tTV, 1)
                t.wait_ge(aSig, n)
                t.wait_ge(vEv, n)
                for cc in range(NCH):
                    ins = t.transpose(p_tr[:, cc * 128:(cc + 1) * 128],
                                      rtm[:, cc * 128:(cc + 1) * 128], idb[:])
                ins.then_inc(tTR, 1)

    _CACHE["l3a"] = nc
    return nc


def build_l3c(debug=False):
    key = "l3c_dbg" if debug else "l3c"
    if key in _CACHE:
        return _CACHE[key]
    nc = bass.Bass("TRN2", target_bir_lowering=False)
    stream = nc.dram_tensor("stream", [OWN, TILE, D], F32, kind="ExternalInput").ap()
    z_sp = nc.dram_tensor("z_sp", [OWN, 128, D], F32, kind="ExternalInput").ap()
    t1_sp = nc.dram_tensor("t1_sp", [OWN, 128, D], F32, kind="ExternalInput").ap()
    r_sp = nc.dram_tensor("r_sp", [OWN, 128, D], BF16, kind="ExternalInput").ap()
    tot_sh = nc.dram_tensor("tot_sh", [128, 2, NCH, NT], F32, kind="ExternalInput").ap()
    td_col = nc.dram_tensor("td_col", [128, 8], F32, kind="ExternalInput").ap()
    tf_col = nc.dram_tensor("tf_col", [128, 8], F32, kind="ExternalInput").ap()
    woT = nc.dram_tensor("woT", [D, D], F32, kind="ExternalInput").ap()
    constv = nc.dram_tensor("constv", [128, 2], F32, kind="ExternalInput").ap()
    ostream = nc.dram_tensor("ostream", [OWN, TILE, D], F32, kind="ExternalOutput").ap()
    dbg = {}
    if debug:
        for nm, shp in (("d_carr", [128, 2 * NCH * NT]), ("d_sf", [128, D]),
                        ("d_tf", [128, D]), ("d_num", [128, D]), ("d_den", [128, D]),
                        ("d_ivd", [128, D]), ("d_yt", [128, D]), ("d_ot", [128, D]),
                        ("d_cols", [128, 64]), ("d_totsb", [128, 2 * NCH * NT])):
            dbg[nm] = nc.dram_tensor(nm, shp, F32, kind="ExternalOutput").ap()

    with ExitStack() as ctx:
        def sb(name, shape, dt):
            return ctx.enter_context(nc.sbuf_tensor(name, shape, dt))

        def psr(name, shape, dt):
            return ctx.enter_context(nc.psum_tensor(name, shape, dt))

        def sem(name):
            return ctx.enter_context(nc.semaphore(name))

        xbuf = [sb("xbuf0", [128, D], F32), sb("xbuf1", [128, D], F32)]
        xb4 = xbuf + [sb("xbuf2", [128, D], F32), sb("xbuf3", [128, D], F32)]
        ztb = [sb("ztb0", [128, D], F32), sb("ztb1", [128, D], F32)]
        t1b = [sb("t1b0", [128, D], F32), sb("t1b1", [128, D], F32)]
        rfb = [sb("rfb0", [128, D], F32), sb("rfb1", [128, D], F32)]
        obuf = [sb("obuf0", [128, D], F32), sb("obuf1", [128, D], F32)]
        Sf = sb("Sf", [128, D], F32)
        Tf = sb("Tf", [128, D], F32)
        u1 = sb("u1", [128, D], F32)
        numt = sb("numt", [128, D], F32)
        dent = sb("dent", [128, D], F32)
        ivdt = sb("ivdt", [128, D], F32)
        yt = sb("yt", [128, D], F32)
        ryb = sb("ryb", [128, D], BF16)
        ot = sb("ot", [128, D], F32)
        lnout = sb("lnout", [128, D], F32)
        sqscr = sb("sqscr", [128, D], F32)
        totsb = sb("totsb", [128, 2 * NCH * NT], F32)
        Carr = sb("Carr", [128, 2 * NCH * NT], F32)
        EWf = sb("EWf", [128, D], F32)
        EL64 = sb("EL64", [128, NCH * NT], F32)
        onef = sb("onef", [128, 128], F32)
        wcol = sb("wcol", [128, 8], F32)
        ewcol = sb("ewcol", [128, 8], F32)
        elcol = sb("elcol", [128, 8], F32)
        eucol = sb("eucol", [128, 8], F32)
        gcol = sb("gcol", [128, 8], F32)
        ewinv = sb("ewinv", [128, 8], F32)
        wo_s = sb("wo_s", [128, NCH * D], BF16)
        stat = sb("stat", [128, 16], F32)
        s_cst = sb("s_cst", [128, 2], F32)
        nc.const_aps.register(F32, EPS, s_cst[:, 0:1])
        SUM, M, Q, VAR, SSQ, MSQ, LNV, RSTD, NB = range(9)

        def col(k):
            return stat[:, k:k + 1]

        p_o = psr("p_o", [128, 1024], F32)

        dIn = sem("dIn"); dPre = sem("dPre"); dW = sem("dW"); dOut = sem("dOut")
        vPro = sem("vPro"); aPro = sem("aPro")
        vScan = sem("vScan"); vNum = sem("vNum"); aDiv = sem("aDiv"); vRy = sem("vRy")
        tWo = sem("tWo"); vOt = sem("vOt")
        aS2 = sem("aS2"); vT2 = sem("vT2"); aR2 = sem("aR2"); vN2 = sem("vN2")
        aX2 = sem("aX2"); vFin = sem("vFin")
        vC = sem("vC"); aC = sem("aC")

        block = ctx.enter_context(nc.Block())

        @block.sync
        def _(sync):
            sync.dma_start(out=s_cst[:], in_=constv).then_inc(dPre, 16)
            sync.dma_start(out=wcol[:], in_=td_col).then_inc(dPre, 16)
            sync.dma_start(out=eucol[:], in_=tf_col).then_inc(dPre, 16)
            sync.dma_start(out=totsb[:].rearrange("p (q c g) -> p q c g", q=2, c=NCH),
                           in_=tot_sh.rearrange("p q c g -> p q c g")).then_inc(dPre, 16)
            for i in range(OWN):
                if i >= 2:
                    sync.wait_ge(vFin, i - 1)
                sync.dma_start(out=xbuf[i % 2][:], in_=stream[i]).then_inc(dIn, 16)
                sync.dma_start(out=ztb[i % 2][:], in_=z_sp[i]).then_inc(dIn, 16)
                sync.dma_start(out=t1b[i % 2][:], in_=t1_sp[i]).then_inc(dIn, 16)
            sync.wait_ge(dOut, 16 * OWN)
            if debug:
                for nm, srcap in (("d_carr", Carr[:]), ("d_sf", Sf[:]), ("d_tf", Tf[:]),
                                  ("d_num", numt[:]), ("d_den", dent[:]),
                                  ("d_ivd", ivdt[:]), ("d_yt", yt[:]), ("d_ot", ot[:]),
                                  ("d_totsb", totsb[:])):
                    sync.dma_start(out=dbg[nm], in_=srcap).then_inc(dPre, 16)
                sync.dma_start(out=dbg["d_cols"][:, 0:8], in_=ewcol[:]).then_inc(dPre, 16)
                sync.dma_start(out=dbg["d_cols"][:, 8:16], in_=elcol[:]).then_inc(dPre, 16)
                sync.dma_start(out=dbg["d_cols"][:, 16:24], in_=eucol[:]).then_inc(dPre, 16)
                sync.dma_start(out=dbg["d_cols"][:, 24:32], in_=gcol[:]).then_inc(dPre, 16)

        @block.gpsimd
        def _(g):
            g.dma_start(out=wo_s[:].rearrange("p (c f) -> p c f", c=NCH),
                        in_=woT.rearrange("(c p) f -> p c f", p=128)).then_inc(dW, 16)
            g.memset(onef[:], 1.0).then_inc(dW, 16)
            for i in range(OWN + 1):
                if i < OWN:
                    if i >= 2:
                        g.wait_ge(vRy, i - 1)
                    g.dma_start(out=rfb[i % 2][:], in_=r_sp[i]).then_inc(dIn, 16)
                if i >= 1:
                    g.wait_ge(vFin, i)
                    g.dma_start(out=ostream[i - 1], in_=obuf[(i - 1) % 2][:]).then_inc(dOut, 16)
            g.wait_ge(dOut, 16 * OWN)

        @block.vector
        def _(v):
            _vn = [0]

            def ch(ins, cross=None, tgt=None):
                if cross is None:
                    _vn[0] += 1
                    ins.then_inc(vC, 1)
                    v.wait_ge(vC, _vn[0])
                else:
                    ins.then_inc(cross, 1)
                    v.wait_ge(cross, tgt)
                return ins

            v.wait_ge(dPre, 64)
            v.wait_ge(dW, 32)
            v.wait_ge(aPro, 1)
            ch(v.tensor_mul(gcol[:], eucol[:], ewinv[:]))
            for cc in range(NCH):
                ch(v.tensor_scalar_mul(EWf[:, cc * 128:(cc + 1) * 128], onef[:],
                                       ewcol[:, cc:cc + 1]))
                ch(v.tensor_scalar_mul(EL64[:, cc * NT:(cc + 1) * NT], onef[:, 0:NT],
                                       elcol[:, cc:cc + 1]))
            for q in range(2):
                for cc in range(NCH):
                    off = (q * NCH + cc) * NT
                    ch(v.tensor_tensor_scan(Carr[:, off:off + NT],
                                            EL64[:, cc * NT:(cc + 1) * NT],
                                            totsb[:, off:off + NT],
                                            0.0, ALU.mult, ALU.add))
            v.nop().then_inc(vPro, 1)

            for i in range(OWN):
                n = i + 1
                v.wait_ge(dIn, 64 * n)
                for cc in range(NCH):
                    offS = (0 * NCH + cc) * NT
                    offT = (1 * NCH + cc) * NT
                    ch(v.tensor_tensor_scan(Sf[:, cc * 128:(cc + 1) * 128],
                                            EWf[:, cc * 128:(cc + 1) * 128],
                                            ztb[i % 2][:, cc * 128:(cc + 1) * 128],
                                            Carr[:, offS + 2 * i:offS + 2 * i + 1],
                                            ALU.mult, ALU.add))
                    ch(v.tensor_tensor_scan(Tf[:, cc * 128:(cc + 1) * 128],
                                            EWf[:, cc * 128:(cc + 1) * 128],
                                            t1b[i % 2][:, cc * 128:(cc + 1) * 128],
                                            Carr[:, offT + 2 * i:offT + 2 * i + 1],
                                            ALU.mult, ALU.add))
                v.nop().then_inc(vScan, 1)
                ch(v.tensor_sub(u1[:], Sf[:], ztb[i % 2][:]))
                for cc in range(NCH):
                    ch(v.scalar_tensor_tensor(numt[:, cc * 128:(cc + 1) * 128],
                                              u1[:, cc * 128:(cc + 1) * 128],
                                              gcol[:, cc:cc + 1],
                                              ztb[i % 2][:, cc * 128:(cc + 1) * 128],
                                              ALU.mult, ALU.add))
                ch(v.tensor_sub(u1[:], Tf[:], t1b[i % 2][:]))
                for cc in range(NCH):
                    ins = v.scalar_tensor_tensor(dent[:, cc * 128:(cc + 1) * 128],
                                                 u1[:, cc * 128:(cc + 1) * 128],
                                                 gcol[:, cc:cc + 1],
                                                 t1b[i % 2][:, cc * 128:(cc + 1) * 128],
                                                 ALU.mult, ALU.add)
                    if cc == NCH - 1:
                        ch(ins, vNum, n)
                    else:
                        ch(ins)
                v.wait_ge(aDiv, n)
                ch(v.tensor_mul(yt[:], numt[:], ivdt[:]))
                ch(v.tensor_mul(ryb[:], yt[:], rfb[i % 2][:]), vRy, n)
                v.wait_ge(tWo, n)
                ch(v.tensor_copy(ot[:], p_o[:]), vOt, n)
                ch(v.reduce_sum(col(SUM), ot[:], axis=AX.X))
                ch(v.tensor_scalar_mul(col(M), col(SUM), 1.0 / D))
                v.wait_ge(aS2, n)
                ch(v.tensor_scalar_mul(col(Q), col(SSQ), 1.0 / D))
                ch(v.tensor_mul(col(MSQ), col(M), col(M)))
                ch(v.tensor_sub(col(VAR), col(Q), col(MSQ)), vT2, n)
                v.wait_ge(aR2, n)
                ch(v.tensor_scalar(col(NB), col(M), col(RSTD), -1.0,
                                   ALU.mult, ALU.mult), vN2, n)
                v.wait_ge(aX2, n)
                if i >= 2:
                    v.wait_ge(dOut, 16 * (i - 1))
                ch(v.tensor_add(obuf[i % 2][:], xbuf[i % 2][:], lnout[:]), vFin, n)

        @block.scalar
        def _(a):
            _an = [0]

            def ca(ins, cross=None, tgt=None):
                if cross is None:
                    _an[0] += 1
                    ins.then_inc(aC, 1)
                    a.wait_ge(aC, _an[0])
                else:
                    ins.then_inc(cross, 1)
                    a.wait_ge(cross, tgt)
                return ins

            a.wait_ge(dPre, 64)
            ca(a.activation(ewcol[:], wcol[:], AF.Exp))
            ca(a.activation(elcol[:], ewcol[:], AF.Exp, scale=-128.0))
            ca(a.activation(ewinv[:], ewcol[:], AF.Exp))
            ca(a.activation(ewcol[:], ewcol[:], AF.Exp, scale=-1.0))
            ca(a.activation(eucol[:], eucol[:], AF.Exp), aPro, 1)

            for i in range(OWN):
                n = i + 1
                a.wait_ge(vNum, n)
                ca(a.activation(ivdt[:], dent[:], AF.Ln))
                ca(a.activation(ivdt[:], ivdt[:], AF.Exp, scale=-1.0), aDiv, n)
                a.wait_ge(vOt, n)
                ca(a.activation(sqscr[:], ot[:], AF.Square, accum_out=col(SSQ)), aS2, n)
                a.wait_ge(vT2, n)
                ca(a.activation(col(LNV), col(VAR), AF.Ln, bias=EPS))
                ca(a.activation(col(RSTD), col(LNV), AF.Exp, scale=-0.5), aR2, n)
                a.wait_ge(vN2, n)
                ca(a.activation(lnout[:], ot[:], AF.Identity, scale=col(RSTD),
                                bias=col(NB)), aX2, n)

        @block.tensor
        def _(t):
            t.wait_ge(dW, 32)
            for i in range(OWN):
                n = i + 1
                t.wait_ge(vRy, n)
                if i > 0:
                    t.wait_ge(vOt, i)
                for h in range(2):
                    for kc in range(NCH):
                        ins = t.matmul(p_o[:, h * 512:(h + 1) * 512],
                                       ryb[:, kc * 128:(kc + 1) * 128],
                                       wo_s[:, kc * D + h * 512:kc * D + (h + 1) * 512],
                                       start=(kc == 0), stop=(kc == NCH - 1))
                ins.then_inc(tWo, 1)

    _CACHE[key] = nc
    return nc



U32 = mybir.dt.uint32


def _fused_schedule():
    visits = []
    blocks = []
    fins = []
    for p in range(4):
        slots = range(8 * p, 8 * p + 8)
        nkb = 2 * p + 2
        for kb in range(nkb):
            act = [j for j in slots if 2 * j + 2 > 8 * kb]
            blocks.append(dict(p=p, kb=kb, first=len(visits), nact=len(act)))
            for j in act:
                W = min(128 * (2 * j + 2) - 1024 * kb, 1024)
                visits.append(dict(p=p, kb=kb, j=j, W=W, mask=(kb == j // 4)))
            for j in act:
                if (2 * j + 2) <= 8 * (kb + 1):
                    fins.append(dict(j=j, after=len(visits) - 1, p=p))
    for f_i, f in enumerate(fins):
        f["fi"] = f_i
    return visits, blocks, fins


def build_fused(debug=False):
    key = "fused_dbg" if debug else "fused"
    if key in _CACHE:
        return _CACHE[key]
    visits, blocks, fins = _fused_schedule()
    nc = bass.Bass("TRN2", target_bir_lowering=False)
    xfull = nc.dram_tensor("xfull", [NT, TILE, D], F32, kind="ExternalInput").ap()
    x_own = nc.dram_tensor("x_own", [OWN, TILE, D], F32, kind="ExternalInput").ap()
    wkvT = nc.dram_tensor("wkvT", [D, 2 * D], F32, kind="ExternalInput").ap()
    wT2 = nc.dram_tensor("wT2", [3, D, D], F32, kind="ExternalInput").ap()
    wT3 = nc.dram_tensor("wT3", [3, D, D], F32, kind="ExternalInput").ap()
    woT3 = nc.dram_tensor("woT3", [D, D], F32, kind="ExternalInput").ap()
    wT4 = nc.dram_tensor("wT4", [3, D, D], F32, kind="ExternalInput").ap()
    mu2f = nc.dram_tensor("mu2f", [2, 128, D], F32, kind="ExternalInput").ap()
    tm3f = nc.dram_tensor("tm3f", [3, 128, D], F32, kind="ExternalInput").ap()
    mu4f = nc.dram_tensor("mu4f", [2, 128, D], F32, kind="ExternalInput").ap()
    td_colD = nc.dram_tensor("td_col", [128, 8], F32, kind="ExternalInput").ap()
    tf_colD = nc.dram_tensor("tf_col", [128, 8], F32, kind="ExternalInput").ap()
    maskp = nc.dram_tensor("maskp", [2, 128, 128], F32, kind="ExternalInput").ap()
    identb = nc.dram_tensor("identb", [128, 128], BF16, kind="ExternalInput").ap()
    identf = nc.dram_tensor("identf", [128, 128], F32, kind="ExternalInput").ap()
    constv = nc.dram_tensor("constv", [128, 2], F32, kind="ExternalInput").ap()
    idxb_d = nc.dram_tensor("idxb", [32, 1], U32, kind="ExternalInput").ap()
    idxs_d = nc.dram_tensor("idxs", [128, 8], U32, kind="ExternalInput").ap()
    zpm_d = nc.dram_tensor("zpm", [32, 1], F32, kind="ExternalInput").ap()
    tzm_d = nc.dram_tensor("tzm", [128, 2], F32, kind="ExternalInput").ap()
    ostream = nc.dram_tensor("ostream", [OWN, TILE, D], F32, kind="ExternalOutput").ap()
    skind0 = "ExternalOutput" if debug else "Internal"
    kf_scr = nc.dram_tensor("kf_scr", [NCH, 128, S], BF16, kind=skind0).ap()
    vf_scr = nc.dram_tensor("vf_scr", [S, D], BF16, kind=skind0).ap()
    z_scr = nc.dram_tensor("z_scr", [OWN, 128, D], F32, kind="Internal").ap()
    t1_scr = nc.dram_tensor("t1_scr", [OWN, 128, D], F32, kind="Internal").ap()
    r_scr = nc.dram_tensor("r_scr", [OWN, 128, D], BF16, kind="Internal").ap()
    skind = "ExternalOutput" if debug else "Internal"
    s1_scr = nc.dram_tensor("s1_scr", [OWN, TILE, D], F32, kind=skind).ap()
    s2_scr = nc.dram_tensor("s2_scr", [OWN, TILE, D], F32, kind=skind).ap()
    s3_scr = nc.dram_tensor("s3_scr", [OWN, TILE, D], F32, kind=skind).ap()
    bd_in = nc.dram_tensor("bd_in", [32, D], F32, kind="Internal")
    bd_out = nc.dram_tensor("bd_out", [256, D], F32, kind="Internal")
    tt_in = nc.dram_tensor("tt_in", [512, 128], F32, kind="Internal")
    tt_out = nc.dram_tensor("tt_out", [4096, 128], F32, kind="Internal")

    with ExitStack() as ctx:
        def sb(name, shape, dt):
            return ctx.enter_context(nc.sbuf_tensor(name, shape, dt))

        def psr(name, shape, dt):
            return ctx.enter_context(nc.psum_tensor(name, shape, dt))

        def sem(name):
            return ctx.enter_context(nc.semaphore(name))

        bigA = sb("bigA", [128, 16384], F32)
        bigB = sb("bigB", [128, 16384], BF16)
        wpool = sb("wpool", [128, 24576], BF16)
        xbuf = [sb("xbuf0", [128, D], F32), sb("xbuf1", [128, D], F32)]
        xb4 = xbuf + [sb("xbuf2", [128, D], F32), sb("xbuf3", [128, D], F32)]
        obuf = [sb("obuf0", [128, D], F32), sb("obuf1", [128, D], F32)]
        P_sb = sb("P_sb", [128, 1024], BF16)
        pt_sb = sb("pt_sb", [128, 1024], BF16)
        prraw = sb("prraw", [128, D], F32)
        prln = sb("prln", [128, D], BF16)
        prfm = sb("prfm", [128, NCH * OWN], BF16)
        totsb = sb("totsb", [128, 1024], F32)
        CarrS = sb("CarrS", [128, 1024], F32)
        EWf = sb("EWf", [128, D], F32)
        EL64 = sb("EL64", [128, NCH * NT], F32)
        onef = sb("onef", [128, 128], F32)
        flatp = sb("flatp", [128, 3072], BF16)
        statsP = sb("statsP", [128, 24], F32)
        stat = sb("stat", [128, 32], F32)
        negm = sb("negm", [128, 8], F32)
        lcol = sb("lcol", [128, 8], F32)
        alpha = sb("alpha", [128, 8], F32)
        rsum = sb("rsum", [128, 8], F32)
        tmpa = sb("tmpa", [128, 16], F32)
        wcol = sb("wcol", [128, 8], F32)
        ewcol = sb("ewcol", [128, 8], F32)
        elcol = sb("elcol", [128, 8], F32)
        ewinv = sb("ewinv", [128, 8], F32)
        eucol = sb("eucol", [128, 8], F32)
        gcol = sb("gcol", [128, 8], F32)
        msk = sb("msk", [128, 256], F32)
        idb = sb("idb", [128, 128], BF16)
        idf = sb("idf", [128, 128], F32)
        s_cst = sb("s_cst", [128, 2], F32)
        sb_ib = sb("sb_ib", [32, 1], U32)
        sb_is = sb("sb_is", [128, 8], U32)
        sb_zm = sb("sb_zm", [32, 1], F32)
        sb_tz = sb("sb_tz", [128, 2], F32)
        nc.const_aps.register(F32, EPS, s_cst[:, 0:1])
        nc.const_aps.register(F32, 1.0, s_cst[:, 1:2])
        SUM, M, Q, VAR, SSQ, MSQ, LNV, RSTD, NB = range(9)

        def col(k, st=0):
            return stat[:, st * 16 + k:st * 16 + k + 1]

        P1 = psr("P1", [128, 2048], F32)
        P2 = psr("P2", [128, 1024], F32)
        P3 = psr("P3", [128, 1024], BF16)
        P4 = psr("P4", [128, 1024], BF16)
        P1a = P1[:, 0:1024]
        P1b = P1[:, 1024:2048]

        PH = sem("PH"); dPre = sem("dPre")
        paIn = sem("paIn"); paS = sem("paS"); paT = sem("paT"); paR = sem("paR")
        paN = sem("paN"); paX = sem("paX"); paTT = sem("paTT"); paEv = sem("paEv")
        paKV = sem("paKV"); paKC = sem("paKC"); paVC = sem("paVC"); paT2 = sem("paT2")
        paKE = sem("paKE"); paSp = sem("paSp"); paW = sem("paW")
        pbQd = sem("pbQd"); pbQs = sem("pbQs"); pbQt = sem("pbQt"); pbQe = sem("pbQe")
        pbQr = sem("pbQr"); pbQn = sem("pbQn"); pbQx = sem("pbQx"); pbQT = sem("pbQT")
        pbKV = sem("pbKV"); pbQK = sem("pbQK"); pbSt = sem("pbSt")
        pbPT = sem("pbPT"); pbPe = sem("pbPe"); pbPV = sem("pbPV"); pbAc = sem("pbAc")
        pbMS = sem("pbMS"); pbFl = sem("pbFl"); pbOt = sem("pbOt"); pbVo = sem("pbVo")
        pbS2 = sem("pbS2"); pbT2 = sem("pbT2"); pbR2 = sem("pbR2"); pbN2 = sem("pbN2")
        pbX2 = sem("pbX2"); pbFn = sem("pbFn"); pbSo = sem("pbSo")
        exD = sem("exD"); exC = sem("exC"); exG = sem("exG")
        prS = sem("prS"); prT = sem("prT"); prR = sem("prR"); prN = sem("prN")
        prX = sem("prX"); prTT = sem("prTT"); prEv = sem("prEv")
        fIn = sem("fIn"); fS = sem("fS"); fT = sem("fT"); fR = sem("fR")
        fN = sem("fN"); fX = sem("fX"); fTT = sem("fTT"); fEv = sem("fEv")
        fM = sem("fM"); fPR = sem("fPR"); fPK = sem("fPK"); fE1 = sem("fE1")
        fSig = sem("fSig"); fRel = sem("fRel"); fK = sem("fK"); fT2 = sem("fT2")
        fEv2 = sem("fEv2"); fPV = sem("fPV"); fO = sem("fO")
        fS2 = sem("fS2"); fV2 = sem("fV2"); fR2 = sem("fR2"); fN2 = sem("fN2")
        fX2 = sem("fX2"); fFin = sem("fFin"); fW = sem("fW"); fSo = sem("fSo")
        dS_ = paS; dT_ = paT; dR_ = paR; dN_ = paN; dX_ = paX
        dTT = paTT; dEv = paEv; dP1 = paKV; dKC = paKC; dVC = paVC
        dTK = paT2; dKF = paKE; dIn_ = paIn; dSp_ = paSp; dW_ = paW
        dM = sem("dM"); dP2 = sem("dP2"); dP3 = sem("dP3")
        dTV = sem("dTV"); dVF = sem("dVF"); dT1 = sem("dT1"); dZ = sem("dZ")
        dScan = sem("dScan"); dE1 = sem("dE1"); dSig = sem("dSig")
        dTR = sem("dTR"); dRE = sem("dRE")
        dTB = sem("dTB"); dTE = sem("dTE")
        eIn = pbFl; eFin = pbFn; eS2 = pbS2; eT2 = pbT2; eR2 = pbR2
        eN2 = pbN2; eX2 = pbX2; eSo = pbSo; eOt = pbVo; eRy = pbAc; eWo = pbQK
        eNum = sem("eNum"); eDiv = sem("eDiv"); eW_ = sem("eW_"); ePro = sem("ePro")
        dEW = ePro
        eTz = dTB
        eSg = dTE
        vC = sem("vC"); aC = sem("aC")

        block = ctx.enter_context(nc.Block())

        p_firsts = {p: next(b["first"] for b in blocks if b["p"] == p and b["kb"] == 0)
                    for p in range(4)}
        blk_end = [b["first"] + b["nact"] for b in blocks]
        NV = len(visits)
        B_LN = NT
        B_IN = 16 * (NT + 2)
        B_SP = 32 * NT
        B_W = 16
        B_FL = 16 * 32
        B_FN = 32
        B_S2 = 32
        B_X2 = 64
        B_SO = 16 * 32
        B_VO = 32
        B_AC = len(visits)
        B_QK = len(visits)
        fins_by_pass = {p: [f for f in fins if f["p"] == p] for p in range(4)}

        f_e1t = bigA[:, 0:1024]
        f_rt = [bigA[:, 1024:2048], bigA[:, 5120:6144]]
        f_ot = [bigA[:, 2048:3072], bigA[:, 6144:7168]]
        f_ln = bigA[:, 3072:4096]
        f_sq = bigA[:, 4096:5120]
        f_xln = [bigB[:, 0:1024], bigB[:, 10240:11264]]
        f_fmx = bigB[:, 1024:2048]
        f_fmp = bigB[:, 2048:3072]
        f_dmx = bigB[:, 3072:4096]
        f_xrf = [bigB[:, 4096:5120], bigB[:, 11264:12288]]
        f_xkf = [bigB[:, 5120:6144], bigB[:, 12288:13312]]
        f_krl = bigB[:, 6144:7168]
        f_kab = [bigB[:, 7168:8192], bigB[:, 13312:14336]]
        f_kaf = bigB[:, 8192:9216]
        d_ktm = bigA[:, 0:1024]; d_vtm = bigA[:, 1024:2048]
        d_kfm = bigA[:, 2048:3072]; d_vfm = bigA[:, 3072:4096]
        d_Sl = bigA[:, 4096:5120]; d_Tl = bigA[:, 5120:6144]; d_e1t = bigA[:, 6144:7168]
        d_zt = [bigA[:, 7168:8192], bigA[:, 8192:9216]]
        d_t1t = [bigA[:, 9216:10240], bigA[:, 10240:11264]]
        d_sq = bigA[:, 11264:12288]
        d_totc = bigA[:, 12288:12800]
        d_tote = bigA[:, 12800:13312]
        d_sgr = bigA[:, 13312:14336]
        d_xln = bigB[:, 0:1024]; d_fmx = bigB[:, 1024:2048]; d_fmp = bigB[:, 2048:3072]
        d_dmx = bigB[:, 3072:4096]; d_xkf = bigB[:, 4096:5120]; d_xvf = bigB[:, 5120:6144]
        d_xrf = bigB[:, 6144:7168]; d_rtm = bigB[:, 7168:8192]
        d_rfm = [bigB[:, 8192:9216], bigB[:, 9216:10240]]
        e_Sf = bigA[:, 0:1024]; e_Tf = bigA[:, 1024:2048]; e_u1 = bigA[:, 2048:3072]
        e_num = bigA[:, 3072:4096]; e_den = bigA[:, 4096:5120]; e_ivd = bigA[:, 5120:6144]
        e_yt = bigA[:, 6144:7168]; e_ot = bigA[:, 7168:8192]; e_ln = bigA[:, 8192:9216]
        e_sq = bigA[:, 9216:10240]
        e_ztb = [bigA[:, 10240:11264], bigA[:, 11264:12288]]
        e_t1b = [bigA[:, 12288:13312], bigA[:, 13312:14336]]
        e_rfb = [bigA[:, 14336:15360], bigA[:, 15360:16384]]
        e_ryb = bigB[:, 0:1024]
        a_xln = bigB[:, 0:1024]
        a_fmx = [bigB[:, 1024:2048], bigB[:, 2048:3072]]
        a_ktm = bigB[:, 3072:4096]
        a_kfm = [bigB[:, 4096:5120], bigB[:, 5120:6144]]
        a_vbf = [bigB[:, 6144:7168], bigB[:, 7168:8192]]
        a_sq = bigA[:, 0:1024]
        b_acc = bigA[:, 0:8192]
        pbx = bigA[:, 9216:11264]
        b_q = bigB[:, 0:8192]
        b_kv = [(wpool[:, 0:8192], wpool[:, 8192:16384]),
                (wpool[:, 16384:24576], bigB[:, 8192:16384])]

        def wslot(k):
            return wpool[:, k * 8192:(k + 1) * 8192]

        EXG_T = {0: 16, 1: 32, "sig": 160, 2: 176}

        def ffn_vector(v, ch, base):
            bo = base * OWN
            pb = 1 + base * 2
            v.wait_ge(fW, 80 * (base + 1))
            v.wait_ge(exG, EXG_T[0] if base == 0 else EXG_T[2])
            ch(v.tensor_scalar_mul(prraw[0:32, :], prraw[0:32, :], sb_zm[0:32, 0:1]))
            pr = prraw[0:OWN, :]
            st_ = stat[0:OWN, :]
            ch(v.reduce_sum(st_[:, 0:1], pr, axis=AX.X))
            ch(v.tensor_scalar_mul(st_[:, 1:2], st_[:, 0:1], 1.0 / D))
            v.wait_ge(prS, pb)
            ch(v.tensor_scalar_mul(st_[:, 2:3], st_[:, 4:5], 1.0 / D))
            ch(v.tensor_mul(st_[:, 5:6], st_[:, 1:2], st_[:, 1:2]))
            ch(v.tensor_sub(st_[:, 3:4], st_[:, 2:3], st_[:, 5:6]), prT, pb)
            v.wait_ge(prR, pb)
            ch(v.tensor_scalar(st_[:, 6:7], st_[:, 1:2], st_[:, 7:8], -1.0,
                               ALU.mult, ALU.mult), prN, pb)
            v.wait_ge(prTT, pb)
            ch(v.tensor_copy(prfm[:], P3[:, 0:NCH * OWN]), prEv, pb)

            def F(i):
                n = bo + i + 1
                v.wait_ge(fIn, 16 * n)
                ch(v.reduce_sum(col(SUM), xb4[i % 4][:], axis=AX.X))
                ch(v.tensor_scalar_mul(col(M), col(SUM), 1.0 / D))
                v.wait_ge(fS, n)
                ch(v.tensor_scalar_mul(col(Q), col(SSQ), 1.0 / D))
                ch(v.tensor_mul(col(MSQ), col(M), col(M)))
                ch(v.tensor_sub(col(VAR), col(Q), col(MSQ)), fT, n)
                v.wait_ge(fR, n)
                ch(v.tensor_scalar(col(NB), col(M), col(RSTD), -1.0,
                                   ALU.mult, ALU.mult), fN, n)
                v.wait_ge(fTT, n)
                if n > 2:
                    v.wait_ge(fPK, n - 2)
                ch(v.tensor_copy(f_fmx, P3[:]), fEv, n)
                for cc in range(NCH):
                    ch(v.tensor_copy(f_fmp[:, cc * 128 + 1:cc * 128 + 128],
                                     f_fmx[:, cc * 128:cc * 128 + 127]))
                    ch(v.tensor_copy(f_fmp[:, cc * 128:cc * 128 + 1],
                                     prfm[:, cc * OWN + i:cc * OWN + i + 1]))
                ch(v.tensor_sub(f_dmx, f_fmx, f_fmp))
                ch(v.tensor_mul(f_xrf[i % 2], f_dmx, flatp[:, 0:1024]))
                ch(v.tensor_add(f_xrf[i % 2], f_xrf[i % 2], f_fmp))
                ch(v.tensor_mul(f_xkf[i % 2], f_dmx, flatp[:, 1024:2048]))
                ch(v.tensor_add(f_xkf[i % 2], f_xkf[i % 2], f_fmp), fM, n)

            def Bk(i):
                n = bo + i + 1
                v.wait_ge(fT2, n)
                ch(v.tensor_copy(f_kaf, P4[:]), fEv2, n)
                v.wait_ge(fPV, n)
                v.wait_ge(fSig, n)
                ch(v.tensor_mul(f_ot[i % 2], f_rt[i % 2], P2[:]), fO, n)
                ch(v.reduce_sum(col(SUM, 1), f_ot[i % 2], axis=AX.X))
                ch(v.tensor_scalar_mul(col(M, 1), col(SUM, 1), 1.0 / D))
                v.wait_ge(fS2, n)
                ch(v.tensor_scalar_mul(col(Q, 1), col(SSQ, 1), 1.0 / D))
                ch(v.tensor_mul(col(MSQ, 1), col(M, 1), col(M, 1)))
                ch(v.tensor_sub(col(VAR, 1), col(Q, 1), col(MSQ, 1)), fV2, n)
                v.wait_ge(fR2, n)
                ch(v.tensor_scalar(col(NB, 1), col(M, 1), col(RSTD, 1), -1.0,
                                   ALU.mult, ALU.mult), fN2, n)
                v.wait_ge(fX2, n)
                if i >= 2:
                    v.wait_ge(fSo, 16 * (n - 2))
                ch(v.tensor_add(obuf[i % 2][:], xb4[i % 4][:], f_ln), fFin, n)

            for t_ in range(OWN + 2):
                if t_ < OWN:
                    F(t_)
                if t_ >= 2:
                    Bk(t_ - 2)

        def ffn_scalar(a, ca, base):
            bo = base * OWN
            pb = 1 + base * 2
            pr = prraw[0:OWN, :]
            st_ = stat[0:OWN, :]
            a.wait_ge(exG, EXG_T[0] if base == 0 else EXG_T[2])
            ca(a.activation(f_sq[0:OWN, :], pr, AF.Square, accum_out=st_[:, 4:5]),
               prS, pb)
            a.wait_ge(prT, pb)
            ca(a.activation(st_[:, 8:9], st_[:, 3:4], AF.Ln, bias=EPS))
            ca(a.activation(st_[:, 7:8], st_[:, 8:9], AF.Exp, scale=-0.5), prR, pb)
            a.wait_ge(prN, pb)
            ca(a.activation(prln[0:OWN, :], pr, AF.Identity, scale=st_[:, 7:8],
                            bias=st_[:, 6:7]), prX, pb)

            def F(i):
                n = bo + i + 1
                a.wait_ge(fIn, 16 * n)
                ca(a.activation(f_sq, xb4[i % 4][:], AF.Square,
                                accum_out=col(SSQ)), fS, n)
                a.wait_ge(fT, n)
                ca(a.activation(col(LNV), col(VAR), AF.Ln, bias=EPS))
                ca(a.activation(col(RSTD), col(LNV), AF.Exp, scale=-0.5), fR, n)
                a.wait_ge(fN, n)
                if n > 2:
                    a.wait_ge(fTT, n - 2)
                ca(a.activation(f_xln[i % 2], xb4[i % 4][:], AF.Identity,
                                scale=col(RSTD), bias=col(NB)), fX, n)

            def Md(i):
                n = bo + i + 1
                a.wait_ge(fPR, n)
                ca(a.activation(f_e1t, P1a, AF.Exp, scale=-1.0), fE1, n)
                if n > 2:
                    a.wait_ge(fO, n - 2)
                ca(a.activation(f_rt[i % 2], f_e1t, AF.Ln, bias=1.0))
                ca(a.activation(f_rt[i % 2], f_rt[i % 2], AF.Exp, scale=-1.0), fSig, n)
                a.wait_ge(fPK, n)
                ca(a.activation(f_krl, P1b, AF.Relu), fRel, n)
                ca(a.activation(f_kab[i % 2], f_krl, AF.Square), fK, n)

            def Bk(i):
                n = bo + i + 1
                a.wait_ge(fO, n)
                ca(a.activation(f_sq, f_ot[i % 2], AF.Square, accum_out=col(SSQ, 1)), fS2, n)
                a.wait_ge(fV2, n)
                ca(a.activation(col(LNV, 1), col(VAR, 1), AF.Ln, bias=EPS))
                ca(a.activation(col(RSTD, 1), col(LNV, 1), AF.Exp, scale=-0.5), fR2, n)
                a.wait_ge(fN2, n)
                ca(a.activation(f_ln, f_ot[i % 2], AF.Identity, scale=col(RSTD, 1),
                                bias=col(NB, 1)), fX2, n)

            for t_ in range(OWN + 2):
                if t_ < OWN:
                    F(t_)
                if 1 <= t_ <= OWN:
                    Md(t_ - 1)
                if t_ >= 2:
                    Bk(t_ - 2)

        def ffn_tensor(t, base):
            bo = base * OWN
            pb = 1 + base * 2
            t.wait_ge(prX, pb)
            for cc in range(NCH):
                ins = t.transpose(P3[:, cc * OWN:(cc + 1) * OWN],
                                  prln[0:OWN, cc * 128:(cc + 1) * 128],
                                  idb[0:OWN, 0:OWN])
            ins.then_inc(prTT, 1)
            t.wait_ge(fW, 80 * (base + 1))

            def F(i):
                n = bo + i + 1
                t.wait_ge(fX, n)
                t.wait_ge(prEv, pb)
                if n > 1:
                    t.wait_ge(fEv, n - 1)
                for cc in range(NCH):
                    ins = t.transpose(P3[:, cc * 128:(cc + 1) * 128],
                                      f_xln[i % 2][:, cc * 128:(cc + 1) * 128], idb[:])
                ins.then_inc(fTT, 1)

            def Md(i):
                n = bo + i + 1
                t.wait_ge(fM, n)
                if n > 1:
                    t.wait_ge(fE1, n - 1)
                for h in range(2):
                    for kc in range(NCH):
                        ins = t.matmul(P1a[:, h * 512:(h + 1) * 512],
                                       f_xrf[i % 2][:, kc * 128:(kc + 1) * 128],
                                       wslot(0)[:, kc * D + h * 512:kc * D + (h + 1) * 512],
                                       start=(kc == 0), stop=(kc == NCH - 1))
                ins.then_inc(fPR, 1)
                if n > 1:
                    t.wait_ge(fRel, n - 1)
                for h in range(2):
                    for kc in range(NCH):
                        ins = t.matmul(P1b[:, h * 512:(h + 1) * 512],
                                       f_xkf[i % 2][:, kc * 128:(kc + 1) * 128],
                                       wslot(1)[:, kc * D + h * 512:kc * D + (h + 1) * 512],
                                       start=(kc == 0), stop=(kc == NCH - 1))
                ins.then_inc(fPK, 1)

            def Bk(i):
                n = bo + i + 1
                t.wait_ge(fK, n)
                if n > 1:
                    t.wait_ge(fEv2, n - 1)
                for cc in range(NCH):
                    ins = t.transpose(P4[:, cc * 128:(cc + 1) * 128],
                                      f_kab[i % 2][:, cc * 128:(cc + 1) * 128], idb[:])
                ins.then_inc(fT2, 1)
                t.wait_ge(fEv2, n)
                if n > 1:
                    t.wait_ge(fO, n - 1)
                for h in range(2):
                    for kc in range(NCH):
                        ins = t.matmul(P2[:, h * 512:(h + 1) * 512],
                                       f_kaf[:, kc * 128:(kc + 1) * 128],
                                       wslot(2)[:, kc * D + h * 512:kc * D + (h + 1) * 512],
                                       start=(kc == 0), stop=(kc == NCH - 1))
                ins.then_inc(fPV, 1)

            for t_ in range(OWN + 2):
                if t_ < OWN:
                    F(t_)
                if 1 <= t_ <= OWN:
                    Md(t_ - 1)
                if t_ >= 2:
                    Bk(t_ - 2)

        @block.sync
        def _(sync):
            ph = [0]

            def bar():
                ph[0] += 5
                sync.nop().then_inc(PH, 1)
                sync.wait_ge(PH, ph[0])

            for _rep in range(2):
                for dst, src_ in ((idb[:], identb), (idf[:], identf), (s_cst[:], constv),
                                  (msk[:, 0:128], maskp[0]), (msk[:, 128:256], maskp[1]),
                                  (wcol[:], td_colD), (eucol[:], tf_colD),
                                  (sb_ib[:], idxb_d), (sb_is[:], idxs_d),
                                  (sb_zm[:], zpm_d), (sb_tz[:], tzm_d)):
                    sync.dma_start(out=dst, in_=src_).then_inc(dPre, 16)
            for g in range(NT + 1):
                if g < NT:
                    if g >= 2:
                        sync.wait_ge(paX, g - 1)
                    sync.dma_start(out=xbuf[g % 2][:], in_=xfull[g]).then_inc(paIn, 16)
                    if g < 2:
                        sync.dma_start(out=xbuf[g % 2][:], in_=xfull[g]).then_inc(paIn, 16)
                if g >= 1:
                    k = g - 1
                    sync.wait_ge(paVC, k + 1)
                    sync.dma_start(out=vf_scr[k * 128:(k + 1) * 128, :],
                                   in_=a_vbf[k % 2]).then_inc(paSp, 16)
                    sync.wait_ge(paKE, k + 1)
                    sync.dma_start(out=kf_scr[:, :, k * 128:(k + 1) * 128].rearrange("c p t -> p c t"),
                                   in_=a_kfm[k % 2].rearrange("p (c t) -> p c t", c=NCH)).then_inc(paSp, 16)
            bar()
            sync.wait_ge(paSp, 32 * NT)
            fin_iter = iter(fins)
            nextf = next(fin_iter, None)
            for bi, b in enumerate(blocks):
                p, kb = b["p"], b["kb"]
                if kb == 0:
                    if p > 0:
                        sync.wait_ge(pbFn, 8 * p)
                    for ss in range(8):
                        jn = 8 * p + ss
                        if jn >= 2:
                            sync.wait_ge(pbQx, jn - 1)
                        sync.dma_start(out=xbuf[ss % 2][:], in_=x_own[jn]).then_inc(pbQd, 16)
                if bi >= 2:
                    sync.wait_ge(pbPV, blk_end[bi - 2])
                for dc in range(NCH):
                    sync.dma_start(out=b_kv[bi % 2][0][:, dc * 1024:(dc + 1) * 1024],
                                   in_=kf_scr[dc, :, kb * 1024:(kb + 1) * 1024]).then_inc(pbKV, 16)
                for sub in range(NCH):
                    sync.dma_start(out=b_kv[bi % 2][1][:, sub * 1024:(sub + 1) * 1024],
                                   in_=vf_scr[kb * 1024 + sub * 128:kb * 1024 + (sub + 1) * 128, :]).then_inc(pbKV, 16)
                while nextf is not None and nextf["after"] < blk_end[bi]:
                    f = nextf; fi = f["fi"]
                    if fi >= 2:
                        sync.wait_ge(pbFn, fi - 1)
                    sync.wait_ge(pbQe, 8 * (f["p"] + 1))
                    sync.dma_start(out=xbuf[fi % 2][:], in_=x_own[f["j"]]).then_inc(pbFl, 16)
                    nextf = next(fin_iter, None)
            bar()
            sync.wait_ge(pbSo, 16 * OWN)
            sync.dma_start(out=bd_in[:, :],
                           in_=s1_scr.rearrange("j p d -> (j p) d")[127::128, :]).then_inc(exD, 16)
            for i in range(OWN):
                if i >= 4:
                    sync.wait_ge(fFin, i - 3)
                sync.dma_start(out=xb4[i % 4][:], in_=s1_scr[i]).then_inc(fIn, 16)
            bar()
            sync.wait_ge(fSo, 16 * OWN)
            sync.dma_start(out=bd_in[:, :],
                           in_=s2_scr.rearrange("j p d -> (j p) d")[127::128, :]).then_inc(exD, 16)
            for i in range(OWN + 1):
                if i < OWN:
                    if i >= 2:
                        sync.wait_ge(dM, i - 1)
                    sync.dma_start(out=xbuf[i % 2][:], in_=s2_scr[i]).then_inc(dIn_, 16)
                if i >= 1:
                    k = i - 1
                    sync.wait_ge(dZ, k + 1)
                    sync.dma_start(out=z_scr[k], in_=d_zt[k % 2]).then_inc(dSp_, 16)
                    sync.dma_start(out=t1_scr[k], in_=d_t1t[k % 2]).then_inc(dSp_, 16)
                    sync.wait_ge(dRE, k + 1)
                    sync.dma_start(out=r_scr[k], in_=d_rfm[k % 2]).then_inc(dSp_, 16)
            sync.wait_ge(dTE, 4)
            for q4 in range(4):
                sync.dma_start(out=tt_in[q4 * 128:(q4 + 1) * 128, :],
                               in_=d_tote[:, q4 * 128:(q4 + 1) * 128]).then_inc(exD, 16)
            bar()
            sync.wait_ge(dSp_, B_SP + 48 * OWN)
            sync.wait_ge(eSg, 5)
            for i in range(OWN + 1):
                if i < OWN:
                    if i >= 2:
                        sync.wait_ge(eFin, B_FN + i - 1)
                    sync.dma_start(out=xbuf[i % 2][:], in_=s2_scr[i]).then_inc(eIn, 16)
                    sync.dma_start(out=e_ztb[i % 2], in_=z_scr[i]).then_inc(eIn, 16)
                    sync.dma_start(out=e_t1b[i % 2], in_=t1_scr[i]).then_inc(eIn, 16)
                if i >= 1:
                    k = i - 1
                    sync.wait_ge(eFin, B_FN + k + 1)
                    sync.dma_start(out=s3_scr[k], in_=obuf[k % 2][:]).then_inc(eSo, 16)
            bar()
            sync.wait_ge(eSo, B_SO + 16 * OWN)
            sync.dma_start(out=bd_in[:, :],
                           in_=s3_scr.rearrange("j p d -> (j p) d")[127::128, :]).then_inc(exD, 16)
            for i in range(OWN):
                if i >= 4:
                    sync.wait_ge(fFin, OWN + i - 3)
                sync.dma_start(out=xb4[i % 4][:], in_=s3_scr[i]).then_inc(fIn, 16)
            sync.wait_ge(fSo, 16 * 2 * OWN)

        @block.gpsimd
        def _(g):
            ph = [0]

            def bar():
                ph[0] += 5
                g.nop().then_inc(PH, 1)
                g.wait_ge(PH, ph[0])

            g.dma_start(out=wpool[:, 0:16384].rearrange("p (c f) -> p c f", c=NCH),
                        in_=wkvT.rearrange("(c p) f -> p c f", p=128)).then_inc(paW, 16)
            bar()
            for p in range(4):
                for f in fins_by_pass[p]:
                    fi = f["fi"]
                    g.wait_ge(pbFn, fi + 1)
                    g.dma_start(out=s1_scr[f["j"]], in_=obuf[fi % 2][:]).then_inc(pbSo, 16)
            bar()
            g.wait_ge(exD, 16)
            g.collective_compute("AllGather", ALU.bypass, replica_groups=[list(range(8))],
                                 ins=[bd_in.ap().opt()], outs=[bd_out.ap().opt()]).then_inc(exC)
            g.wait_ge(exC, 1)
            g.indirect_dma_start(out=prraw[0:32, :], out_offset=None, in_=bd_out.ap(),
                                 in_offset=IndirectOffsetOnAxis(ap=sb_ib[:, 0:1], axis=0)).then_inc(exG, 16)
            for kk in range(3):
                g.dma_start(out=wslot(kk).rearrange("p (c f) -> p c f", c=NCH),
                            in_=wT2[kk].rearrange("(c p) f -> p c f", p=128)).then_inc(fW, 16)
            g.dma_start(out=flatp[:, 0:1024], in_=mu2f[0]).then_inc(fW, 16)
            g.dma_start(out=flatp[:, 1024:2048], in_=mu2f[1]).then_inc(fW, 16)
            for k in range(OWN):
                g.wait_ge(fFin, k + 1)
                g.dma_start(out=s2_scr[k], in_=obuf[k % 2][:]).then_inc(fSo, 16)
            bar()
            g.wait_ge(exD, 32)
            g.collective_compute("AllGather", ALU.bypass, replica_groups=[list(range(8))],
                                 ins=[bd_in.ap().opt()], outs=[bd_out.ap().opt()]).then_inc(exC)
            g.wait_ge(exC, 2)
            g.indirect_dma_start(out=prraw[0:32, :], out_offset=None, in_=bd_out.ap(),
                                 in_offset=IndirectOffsetOnAxis(ap=sb_ib[:, 0:1], axis=0)).then_inc(exG, 16)
            for kk in range(3):
                g.dma_start(out=wslot(kk).rearrange("p (c f) -> p c f", c=NCH),
                            in_=wT3[kk].rearrange("(c p) f -> p c f", p=128)).then_inc(dW_, 16)
            for kk in range(3):
                g.dma_start(out=flatp[:, kk * 1024:(kk + 1) * 1024], in_=tm3f[kk]).then_inc(dW_, 16)
            g.wait_ge(exD, 96)
            g.collective_compute("AllGather", ALU.bypass, replica_groups=[list(range(8))],
                                 ins=[tt_in.ap().opt()], outs=[tt_out.ap().opt()]).then_inc(exC)
            g.wait_ge(exC, 3)
            for kk in range(8):
                g.indirect_dma_start(out=d_sgr[:, kk * 128:(kk + 1) * 128], out_offset=None,
                                     in_=tt_out.ap(),
                                     in_offset=IndirectOffsetOnAxis(ap=sb_is[:, kk:kk + 1], axis=0)).then_inc(exG, 16)
            bar()
            g.dma_start(out=wslot(0).rearrange("p (c f) -> p c f", c=NCH),
                        in_=woT3.rearrange("(c p) f -> p c f", p=128)).then_inc(eW_, 16)
            for i in range(OWN):
                if i >= 2:
                    g.wait_ge(eRy, B_AC + i - 1)
                g.dma_start(out=e_rfb[i % 2], in_=r_scr[i]).then_inc(eIn, 16)
            bar()
            g.wait_ge(exD, 112)
            g.collective_compute("AllGather", ALU.bypass, replica_groups=[list(range(8))],
                                 ins=[bd_in.ap().opt()], outs=[bd_out.ap().opt()]).then_inc(exC)
            g.wait_ge(exC, 4)
            g.indirect_dma_start(out=prraw[0:32, :], out_offset=None, in_=bd_out.ap(),
                                 in_offset=IndirectOffsetOnAxis(ap=sb_ib[:, 0:1], axis=0)).then_inc(exG, 16)
            for kk in range(3):
                g.dma_start(out=wslot(kk).rearrange("p (c f) -> p c f", c=NCH),
                            in_=wT4[kk].rearrange("(c p) f -> p c f", p=128)).then_inc(fW, 16)
            g.dma_start(out=flatp[:, 0:1024], in_=mu4f[0]).then_inc(fW, 16)
            g.dma_start(out=flatp[:, 1024:2048], in_=mu4f[1]).then_inc(fW, 16)
            for i in range(OWN):
                g.wait_ge(fFin, OWN + i + 1)
                g.dma_start(out=ostream[i], in_=obuf[i % 2][:]).then_inc(fSo, 16)
            g.wait_ge(fSo, 16 * 2 * OWN)

        @block.vector
        def _(v):
            _vn = [0]

            def ch(ins, cross=None, tgt=None):
                if cross is None:
                    _vn[0] += 1
                    ins.then_inc(vC, 1)
                    v.wait_ge(vC, _vn[0])
                else:
                    ins.then_inc(cross, 1)
                    v.wait_ge(cross, tgt)
                return ins

            ph = [0]

            def bar():
                ph[0] += 5
                v.nop().then_inc(PH, 1)
                v.wait_ge(PH, ph[0])

            v.wait_ge(dPre, 352)
            for i in range(NT):
                n = i + 1
                v.wait_ge(paIn, 16 * (n + min(n, 2)))
                if i == 0:
                    ch(v.reduce_sum(col(SUM), xbuf[i % 2][:], axis=AX.X))
                ch(v.reduce_sum(col(SUM), xbuf[i % 2][:], axis=AX.X))
                ch(v.tensor_scalar_mul(col(M), col(SUM), 1.0 / D))
                v.wait_ge(paS, n)
                ch(v.tensor_scalar_mul(col(Q), col(SSQ), 1.0 / D))
                ch(v.tensor_mul(col(MSQ), col(M), col(M)))
                ch(v.tensor_sub(col(VAR), col(Q), col(MSQ)), paT, n)
                v.wait_ge(paR, n)
                ch(v.tensor_scalar(col(NB), col(M), col(RSTD), -1.0,
                                   ALU.mult, ALU.mult), paN, n)
                v.wait_ge(paTT, n)
                if i >= 2:
                    v.wait_ge(paSp, 32 * (i - 1))
                ch(v.tensor_copy(a_fmx[i % 2], P3[:]), paEv, n)
                v.wait_ge(paKV, n)
                ch(v.tensor_copy(a_vbf[i % 2], P1b), paVC, n)
                v.wait_ge(paT2, n)
                ch(v.tensor_copy(a_kfm[i % 2], P4[:]), paKE, n)
            bar()
            fin_iter = iter(fins)
            nextf = next(fin_iter, None)
            for u, vis in enumerate(visits):
                j = vis["j"]; p = vis["p"]; s = j - 8 * p; W = vis["W"]
                if u == p_firsts[p]:
                    ch(v.memset(b_acc, 0.0))
                    ch(v.memset(lcol[:], 0.0))
                    ch(v.memset(negm[:], 3.0e38))
                    for ss in range(8):
                        jn = 8 * p + ss + 1
                        v.wait_ge(pbQd, 16 * jn)
                        ch(v.reduce_sum(col(SUM), xbuf[ss % 2][:], axis=AX.X))
                        ch(v.tensor_scalar_mul(statsP[:, 3 * ss:3 * ss + 1],
                                               col(SUM), 1.0 / D))
                        v.wait_ge(pbQs, jn)
                        ch(v.tensor_scalar_mul(col(Q), col(SSQ), 1.0 / D))
                        ch(v.tensor_mul(col(MSQ), statsP[:, 3 * ss:3 * ss + 1],
                                        statsP[:, 3 * ss:3 * ss + 1]))
                        ch(v.tensor_sub(col(VAR), col(Q), col(MSQ)), pbQt, jn)
                        v.wait_ge(pbQr, jn)
                        ch(v.tensor_scalar(statsP[:, 3 * ss + 2:3 * ss + 3],
                                           statsP[:, 3 * ss:3 * ss + 1],
                                           statsP[:, 3 * ss + 1:3 * ss + 2], -1.0,
                                           ALU.mult, ALU.mult), pbQn, jn)
                        v.wait_ge(pbQT, jn)
                        ch(v.tensor_copy(b_q[:, ss * 1024:(ss + 1) * 1024], P3[:]),
                           pbQe, jn)
                v.wait_ge(pbQK, u + 1)
                if vis["mask"]:
                    r0 = 2 * j - 8 * vis["kb"]
                    ch(v.tensor_add(P1a[:, r0 * 128:(r0 + 1) * 128],
                                    P1a[:, r0 * 128:(r0 + 1) * 128], msk[:, 0:128]))
                    ch(v.tensor_add(P1a[:, (r0 + 1) * 128:(r0 + 2) * 128],
                                    P1a[:, (r0 + 1) * 128:(r0 + 2) * 128], msk[:, 128:256]))
                ch(v.reduce_max(tmpa[:, 0:1], P1a[:, 0:W], axis=AX.X, negate=True))
                ch(v.tensor_tensor(tmpa[:, 1:2], negm[:, s:s + 1], tmpa[:, 0:1], ALU.min))
                ch(v.tensor_sub(tmpa[:, 2:3], tmpa[:, 1:2], negm[:, s:s + 1]))
                ch(v.tensor_copy(negm[:, s:s + 1], tmpa[:, 1:2]), pbSt, u + 1)
                v.wait_ge(pbPT, u + 1)
                ch(v.tensor_copy(pt_sb[:, 0:W], P3[:, 0:W]), pbPe, u + 1)
                v.wait_ge(pbPV, u + 1)
                v.wait_ge(pbOt, 2 * (u + 1))
                ch(v.scalar_tensor_tensor(lcol[:, s:s + 1], lcol[:, s:s + 1],
                                          alpha[:, s:s + 1], rsum[:, s:s + 1],
                                          ALU.mult, ALU.add))
                ch(v.scalar_tensor_tensor(b_acc[:, s * 1024:(s + 1) * 1024],
                                          b_acc[:, s * 1024:(s + 1) * 1024],
                                          alpha[:, s:s + 1], P1b,
                                          ALU.mult, ALU.add), pbAc, u + 1)
                while nextf is not None and nextf["after"] == u:
                    f = nextf; fi = f["fi"]; fs = f["j"] - 8 * p; fn = fi + 1
                    v.wait_ge(pbFl, 16 * fn)
                    ch(v.reciprocal(tmpa[:, 3:4], lcol[:, fs:fs + 1]))
                    v.wait_ge(pbX2, 2 * fn - 1)
                    ch(v.scalar_tensor_tensor(pbx[:, 1024:2048],
                                              b_acc[:, fs * 1024:(fs + 1) * 1024],
                                              tmpa[:, 3:4], pbx[:, 0:1024],
                                              ALU.mult, ALU.add), pbVo, fn)
                    ch(v.reduce_sum(col(SUM, 1), pbx[:, 1024:2048], axis=AX.X))
                    ch(v.tensor_scalar_mul(col(M, 1), col(SUM, 1), 1.0 / D))
                    v.wait_ge(pbS2, fn)
                    ch(v.tensor_scalar_mul(col(Q, 1), col(SSQ, 1), 1.0 / D))
                    ch(v.tensor_mul(col(MSQ, 1), col(M, 1), col(M, 1)))
                    ch(v.tensor_sub(col(VAR, 1), col(Q, 1), col(MSQ, 1)), pbT2, fn)
                    v.wait_ge(pbR2, fn)
                    ch(v.tensor_scalar(col(NB, 1), col(M, 1), col(RSTD, 1), -1.0,
                                       ALU.mult, ALU.mult), pbN2, fn)
                    v.wait_ge(pbX2, 2 * fn)
                    if fi >= 2:
                        v.wait_ge(pbSo, 16 * (fi - 1))
                    ch(v.tensor_add(obuf[fi % 2][:], xbuf[fi % 2][:], pbx[:, 0:1024]),
                       pbFn, fn)
                    nextf = next(fin_iter, None)
            bar()
            ffn_vector(v, ch, base=0)
            bar()
            v.wait_ge(dW_, B_W + 96)
            v.wait_ge(dEW, 1)
            ch(v.memset(onef[:], 1.0))
            for cc in range(NCH):
                ch(v.tensor_scalar_mul(EWf[:, cc * 128:(cc + 1) * 128], onef[:],
                                       ewcol[:, cc:cc + 1]))
            v.wait_ge(exG, EXG_T[1])
            ch(v.tensor_scalar_mul(prraw[0:32, :], prraw[0:32, :], sb_zm[0:32, 0:1]))
            pr = prraw[0:OWN, :]
            st_ = stat[0:OWN, :]
            ch(v.reduce_sum(st_[:, 0:1], pr, axis=AX.X))
            ch(v.tensor_scalar_mul(st_[:, 1:2], st_[:, 0:1], 1.0 / D))
            v.wait_ge(prS, 2)
            ch(v.tensor_scalar_mul(st_[:, 2:3], st_[:, 4:5], 1.0 / D))
            ch(v.tensor_mul(st_[:, 5:6], st_[:, 1:2], st_[:, 1:2]))
            ch(v.tensor_sub(st_[:, 3:4], st_[:, 2:3], st_[:, 5:6]), prT, 2)
            v.wait_ge(prR, 2)
            ch(v.tensor_scalar(st_[:, 6:7], st_[:, 1:2], st_[:, 7:8], -1.0,
                               ALU.mult, ALU.mult), prN, 2)
            v.wait_ge(prTT, 2)
            ch(v.tensor_copy(prfm[:], P3[:, 0:NCH * OWN]), prEv, 2)
            for i in range(OWN):
                n = i + 1
                v.wait_ge(dIn_, B_IN + 16 * n)
                ch(v.reduce_sum(col(SUM), xbuf[i % 2][:], axis=AX.X))
                ch(v.tensor_scalar_mul(col(M), col(SUM), 1.0 / D))
                v.wait_ge(dS_, B_LN + n)
                ch(v.tensor_scalar_mul(col(Q), col(SSQ), 1.0 / D))
                ch(v.tensor_mul(col(MSQ), col(M), col(M)))
                ch(v.tensor_sub(col(VAR), col(Q), col(MSQ)), dT_, B_LN + n)
                v.wait_ge(dR_, B_LN + n)
                ch(v.tensor_scalar(col(NB), col(M), col(RSTD), -1.0,
                                   ALU.mult, ALU.mult), dN_, B_LN + n)
                v.wait_ge(dTT, B_LN + n)
                if i > 0:
                    v.wait_ge(dP3, i)
                ch(v.tensor_copy(d_fmx, P3[:]), dEv, B_LN + n)
                for cc in range(NCH):
                    ch(v.tensor_copy(d_fmp[:, cc * 128 + 1:cc * 128 + 128],
                                     d_fmx[:, cc * 128:cc * 128 + 127]))
                    ch(v.tensor_copy(d_fmp[:, cc * 128:cc * 128 + 1],
                                     prfm[:, cc * OWN + i:cc * OWN + i + 1]))
                ch(v.tensor_sub(d_dmx, d_fmx, d_fmp))
                ch(v.tensor_mul(d_xkf, d_dmx, flatp[:, 0:1024]))
                ch(v.tensor_add(d_xkf, d_xkf, d_fmp))
                ch(v.tensor_mul(d_xvf, d_dmx, flatp[:, 1024:2048]))
                ch(v.tensor_add(d_xvf, d_xvf, d_fmp))
                ch(v.tensor_mul(d_xrf, d_dmx, flatp[:, 2048:3072]))
                ch(v.tensor_add(d_xrf, d_xrf, d_fmp), dM, n)
                v.wait_ge(dP1, B_LN + n)
                ch(v.tensor_copy(d_ktm, P1a), dKC, B_LN + n)
                v.wait_ge(dP2, n)
                ch(v.tensor_copy(d_vtm, P1b), dVC, B_LN + n)
                v.wait_ge(dTK, B_LN + n)
                ch(v.tensor_copy(d_kfm, P1a), dKF, B_LN + n)
                v.wait_ge(dTV, n)
                ch(v.tensor_copy(d_vfm, P1b), dVF, n)
                v.wait_ge(dT1, n)
                if i >= 2:
                    v.wait_ge(dSp_, B_SP + 48 * (i - 1))
                ch(v.tensor_mul(d_zt[i % 2], d_t1t[i % 2], d_vfm), dZ, n)
                for cc in range(NCH):
                    ch(v.tensor_tensor_scan(d_Sl[:, cc * 128:(cc + 1) * 128],
                                            EWf[:, cc * 128:(cc + 1) * 128],
                                            d_zt[i % 2][:, cc * 128:(cc + 1) * 128],
                                            0.0, ALU.mult, ALU.add))
                    ch(v.tensor_tensor_scan(d_Tl[:, cc * 128:(cc + 1) * 128],
                                            EWf[:, cc * 128:(cc + 1) * 128],
                                            d_t1t[i % 2][:, cc * 128:(cc + 1) * 128],
                                            0.0, ALU.mult, ALU.add))
                ch(v.tensor_copy(d_totc[:, i * 16:i * 16 + 8],
                                 d_Sl.rearrange("p (c t) -> p c t", c=NCH)[:, :, 127]))
                ch(v.tensor_copy(d_totc[:, i * 16 + 8:i * 16 + 16],
                                 d_Tl.rearrange("p (c t) -> p c t", c=NCH)[:, :, 127]),
                   dScan, n)
                v.wait_ge(dTR, n)
                ch(v.tensor_copy(d_rfm[i % 2], P4[:]), dRE, n)
            v.wait_ge(dTB, 1)
            for q4 in range(4):
                ch(v.tensor_copy(d_tote[:, q4 * 128:(q4 + 1) * 128],
                                 P2[:, q4 * 128:(q4 + 1) * 128]), dTE, q4 + 1)
            bar()
            v.wait_ge(eW_, 16)
            v.wait_ge(ePro, 2)
            ch(v.tensor_mul(gcol[:], eucol[:], ewinv[:]))
            for cc in range(NCH):
                ch(v.tensor_scalar_mul(EL64[:, cc * NT:(cc + 1) * NT], onef[:, 0:NT],
                                       elcol[:, cc:cc + 1]))
            v.wait_ge(eTz, 2)
            for kk in range(NCH):
                ins = v.tensor_copy(totsb[:, kk * 128:(kk + 1) * 128],
                                    P2[:, kk * 128:(kk + 1) * 128])
                if kk == NCH - 1:
                    ch(ins, eSg, 5)
                else:
                    ch(ins)
            for kk in range(NCH):
                ch(v.tensor_scalar_mul(totsb[:, kk * 128:kk * 128 + 1],
                                       totsb[:, kk * 128:kk * 128 + 1], sb_tz[:, 0:1]))
                ch(v.tensor_scalar_mul(totsb[:, kk * 128 + 64:kk * 128 + 65],
                                       totsb[:, kk * 128 + 64:kk * 128 + 65], sb_tz[:, 0:1]))
            for cc in range(NCH):
                for half in range(2):
                    off = cc * 128 + half * 64
                    ch(v.tensor_tensor_scan(CarrS[:, off:off + 64],
                                            EL64[:, cc * NT:cc * NT + 64],
                                            totsb[:, off:off + 64],
                                            0.0, ALU.mult, ALU.add))
            for i in range(OWN):
                n = i + 1
                v.wait_ge(eIn, B_FL + 64 * n)
                for cc in range(NCH):
                    ch(v.tensor_tensor_scan(e_Sf[:, cc * 128:(cc + 1) * 128],
                                            EWf[:, cc * 128:(cc + 1) * 128],
                                            e_ztb[i % 2][:, cc * 128:(cc + 1) * 128],
                                            CarrS[:, cc * 128 + 2 * i:cc * 128 + 2 * i + 1],
                                            ALU.mult, ALU.add))
                    ch(v.tensor_tensor_scan(e_Tf[:, cc * 128:(cc + 1) * 128],
                                            EWf[:, cc * 128:(cc + 1) * 128],
                                            e_t1b[i % 2][:, cc * 128:(cc + 1) * 128],
                                            CarrS[:, cc * 128 + 64 + 2 * i:cc * 128 + 64 + 2 * i + 1],
                                            ALU.mult, ALU.add))
                ch(v.tensor_sub(e_u1, e_Sf, e_ztb[i % 2]))
                for cc in range(NCH):
                    ch(v.scalar_tensor_tensor(e_num[:, cc * 128:(cc + 1) * 128],
                                              e_u1[:, cc * 128:(cc + 1) * 128],
                                              gcol[:, cc:cc + 1],
                                              e_ztb[i % 2][:, cc * 128:(cc + 1) * 128],
                                              ALU.mult, ALU.add))
                ch(v.tensor_sub(e_u1, e_Tf, e_t1b[i % 2]))
                for cc in range(NCH):
                    ins = v.scalar_tensor_tensor(e_den[:, cc * 128:(cc + 1) * 128],
                                                 e_u1[:, cc * 128:(cc + 1) * 128],
                                                 gcol[:, cc:cc + 1],
                                                 e_t1b[i % 2][:, cc * 128:(cc + 1) * 128],
                                                 ALU.mult, ALU.add)
                    if cc == NCH - 1:
                        ch(ins, eNum, n)
                    else:
                        ch(ins)
                v.wait_ge(eDiv, n)
                ch(v.tensor_mul(e_yt, e_num, e_ivd))
                ch(v.tensor_mul(e_ryb, e_yt, e_rfb[i % 2]), eRy, B_AC + n)
                v.wait_ge(eWo, B_QK + n)
                ch(v.tensor_copy(e_ot, P2[:]), eOt, B_VO + n)
                ch(v.reduce_sum(col(SUM), e_ot, axis=AX.X))
                ch(v.tensor_scalar_mul(col(M), col(SUM), 1.0 / D))
                v.wait_ge(eS2, B_S2 + n)
                ch(v.tensor_scalar_mul(col(Q), col(SSQ), 1.0 / D))
                ch(v.tensor_mul(col(MSQ), col(M), col(M)))
                ch(v.tensor_sub(col(VAR), col(Q), col(MSQ)), eT2, B_S2 + n)
                v.wait_ge(eR2, B_S2 + n)
                ch(v.tensor_scalar(col(NB), col(M), col(RSTD), -1.0,
                                   ALU.mult, ALU.mult), eN2, B_S2 + n)
                v.wait_ge(eX2, B_X2 + n)
                if i >= 2:
                    v.wait_ge(eSo, B_SO + 16 * (i - 1))
                ch(v.tensor_add(obuf[i % 2][:], xbuf[i % 2][:], e_ln), eFin, B_FN + n)
            bar()
            ffn_vector(v, ch, base=1)

        @block.scalar
        def _(a):
            _an = [0]

            def ca(ins, cross=None, tgt=None):
                if cross is None:
                    _an[0] += 1
                    ins.then_inc(aC, 1)
                    a.wait_ge(aC, _an[0])
                else:
                    ins.then_inc(cross, 1)
                    a.wait_ge(cross, tgt)
                return ins

            ph = [0]

            def bar():
                ph[0] += 5
                a.nop().then_inc(PH, 1)
                a.wait_ge(PH, ph[0])

            a.wait_ge(dPre, 352)
            for i in range(NT):
                n = i + 1
                a.wait_ge(paIn, 16 * (n + min(n, 2)))
                if i == 0:
                    ca(a.activation(a_sq, xbuf[i % 2][:], AF.Square,
                                    accum_out=col(SSQ)))
                ca(a.activation(a_sq, xbuf[i % 2][:], AF.Square,
                                accum_out=col(SSQ)), paS, n)
                a.wait_ge(paT, n)
                ca(a.activation(col(LNV), col(VAR), AF.Ln, bias=EPS))
                ca(a.activation(col(RSTD), col(LNV), AF.Exp, scale=-0.5), paR, n)
                a.wait_ge(paN, n)
                ca(a.activation(a_xln, xbuf[i % 2][:], AF.Identity, scale=col(RSTD),
                                bias=col(NB)), paX, n)
                a.wait_ge(paKV, n)
                ca(a.activation(a_ktm, P1a, AF.Identity), paKC, n)
            bar()
            fin_iter = iter(fins)
            nextf = next(fin_iter, None)
            for u, vis in enumerate(visits):
                j = vis["j"]; p = vis["p"]; s = j - 8 * p; W = vis["W"]
                if u == p_firsts[p]:
                    for ss in range(8):
                        jn = 8 * p + ss + 1
                        a.wait_ge(pbQd, 16 * jn)
                        ca(a.activation(pbx[:, 0:1024], xbuf[ss % 2][:], AF.Square,
                                        accum_out=col(SSQ)), pbQs, jn)
                        a.wait_ge(pbQt, jn)
                        ca(a.activation(col(LNV), col(VAR), AF.Ln, bias=EPS))
                        ca(a.activation(statsP[:, 3 * ss + 1:3 * ss + 2], col(LNV),
                                        AF.Exp, scale=-0.5), pbQr, jn)
                        a.wait_ge(pbQn, jn)
                        if jn > 1:
                            a.wait_ge(pbQT, jn - 1)
                        ca(a.activation(P_sb[:], xbuf[ss % 2][:], AF.Identity,
                                        scale=statsP[:, 3 * ss + 1:3 * ss + 2],
                                        bias=statsP[:, 3 * ss + 2:3 * ss + 3]), pbQx, jn)
                a.wait_ge(pbSt, u + 1)
                ca(a.activation(alpha[:, s:s + 1], tmpa[:, 2:3], AF.Exp), pbOt, 2 * u + 1)
                ca(a.activation(P_sb[:, 0:W], P1a[:, 0:W], AF.Exp,
                                bias=negm[:, s:s + 1],
                                accum_out=rsum[:, s:s + 1]), pbOt, 2 * u + 2)
                while nextf is not None and nextf["after"] == u:
                    f = nextf; fi = f["fi"]; fs = f["j"] - 8 * p; fn = fi + 1
                    a.wait_ge(pbFl, 16 * fn)
                    if fi > 0:
                        a.wait_ge(pbFn, fi)
                    ca(a.activation(pbx[:, 0:1024], xbuf[fi % 2][:], AF.Identity,
                                    scale=statsP[:, 3 * fs + 1:3 * fs + 2],
                                    bias=statsP[:, 3 * fs + 2:3 * fs + 3]), pbX2, 2 * fn - 1)
                    a.wait_ge(pbVo, fn)
                    ca(a.activation(bigA[:, 8192:9216], pbx[:, 1024:2048], AF.Square,
                                    accum_out=col(SSQ, 1)), pbS2, fn)
                    a.wait_ge(pbT2, fn)
                    ca(a.activation(col(LNV, 1), col(VAR, 1), AF.Ln, bias=EPS))
                    ca(a.activation(col(RSTD, 1), col(LNV, 1), AF.Exp, scale=-0.5),
                       pbR2, fn)
                    a.wait_ge(pbN2, fn)
                    ca(a.activation(pbx[:, 0:1024], pbx[:, 1024:2048], AF.Identity,
                                    scale=col(RSTD, 1), bias=col(NB, 1)), pbX2, 2 * fn)
                    nextf = next(fin_iter, None)
            bar()
            ffn_scalar(a, ca, base=0)
            bar()
            ca(a.activation(ewcol[:], wcol[:], AF.Exp))
            ca(a.activation(elcol[:], ewcol[:], AF.Exp, scale=-128.0))
            ca(a.activation(ewinv[:], ewcol[:], AF.Exp))
            ca(a.activation(ewcol[:], ewcol[:], AF.Exp, scale=-1.0))
            ca(a.activation(eucol[:], eucol[:], AF.Exp), dEW, 1)
            pr = prraw[0:OWN, :]
            st_ = stat[0:OWN, :]
            a.wait_ge(exG, EXG_T[1])
            ca(a.activation(d_sq[0:OWN, :], pr, AF.Square, accum_out=st_[:, 4:5]), prS, 2)
            a.wait_ge(prT, 2)
            ca(a.activation(st_[:, 8:9], st_[:, 3:4], AF.Ln, bias=EPS))
            ca(a.activation(st_[:, 7:8], st_[:, 8:9], AF.Exp, scale=-0.5), prR, 2)
            a.wait_ge(prN, 2)
            ca(a.activation(prln[0:OWN, :], pr, AF.Identity, scale=st_[:, 7:8],
                            bias=st_[:, 6:7]), prX, 2)
            for i in range(OWN):
                n = i + 1
                a.wait_ge(dIn_, B_IN + 16 * n)
                ca(a.activation(d_sq, xbuf[i % 2][:], AF.Square,
                                accum_out=col(SSQ)), dS_, B_LN + n)
                a.wait_ge(dT_, B_LN + n)
                ca(a.activation(col(LNV), col(VAR), AF.Ln, bias=EPS))
                ca(a.activation(col(RSTD), col(LNV), AF.Exp, scale=-0.5), dR_, B_LN + n)
                a.wait_ge(dN_, B_LN + n)
                ca(a.activation(d_xln, xbuf[i % 2][:], AF.Identity, scale=col(RSTD),
                                bias=col(NB)), dX_, B_LN + n)
                a.wait_ge(dKF, B_LN + n)
                if i >= 2:
                    a.wait_ge(dSp_, B_SP + 48 * (i - 1))
                ca(a.activation(d_t1t[i % 2], d_kfm, AF.Exp), dT1, n)
                a.wait_ge(dP3, n)
                ca(a.activation(d_e1t, P2[:], AF.Exp, scale=-1.0), dE1, n)
                ca(a.activation(d_rtm, d_e1t, AF.Ln, bias=1.0))
                ca(a.activation(d_rtm, d_rtm, AF.Exp, scale=-1.0), dSig, n)
            a.nop().then_inc(ePro, 1)
            bar()
            for i in range(OWN):
                n = i + 1
                a.wait_ge(eNum, n)
                ca(a.activation(e_ivd, e_den, AF.Ln))
                ca(a.activation(e_ivd, e_ivd, AF.Exp, scale=-1.0), eDiv, n)
                a.wait_ge(eOt, B_VO + n)
                ca(a.activation(e_sq, e_ot, AF.Square, accum_out=col(SSQ)), eS2, B_S2 + n)
                a.wait_ge(eT2, B_S2 + n)
                ca(a.activation(col(LNV), col(VAR), AF.Ln, bias=EPS))
                ca(a.activation(col(RSTD), col(LNV), AF.Exp, scale=-0.5), eR2, B_S2 + n)
                a.wait_ge(eN2, B_S2 + n)
                ca(a.activation(e_ln, e_ot, AF.Identity, scale=col(RSTD),
                                bias=col(NB)), eX2, B_X2 + n)
            bar()
            ffn_scalar(a, ca, base=1)

        @block.tensor
        def _(t):
            ph = [0]

            def bar():
                ph[0] += 5
                t.nop().then_inc(PH, 1)
                t.wait_ge(PH, ph[0])

            t.wait_ge(dPre, 352)
            t.wait_ge(paW, 16)
            for i in range(NT):
                n = i + 1
                t.wait_ge(paX, n)
                if i > 0:
                    t.wait_ge(paEv, i)
                for cc in range(NCH):
                    ins = t.transpose(P3[:, cc * 128:(cc + 1) * 128],
                                      a_xln[:, cc * 128:(cc + 1) * 128], idb[:])
                ins.then_inc(paTT, 1)
                t.wait_ge(paEv, n)
                if i > 0:
                    t.wait_ge(paKC, i)
                    t.wait_ge(paVC, i)
                for h in range(4):
                    for kc in range(NCH):
                        ins = t.matmul(P1[:, h * 512:(h + 1) * 512],
                                       a_fmx[i % 2][:, kc * 128:(kc + 1) * 128],
                                       wpool[:, kc * 2048 + h * 512:kc * 2048 + (h + 1) * 512],
                                       start=(kc == 0), stop=(kc == NCH - 1))
                ins.then_inc(paKV, 1)
                t.wait_ge(paKC, n)
                if i > 0:
                    t.wait_ge(paKE, i)
                for cc in range(NCH):
                    ins = t.transpose(P4[:, cc * 128:(cc + 1) * 128],
                                      a_ktm[:, cc * 128:(cc + 1) * 128], idb[:])
                ins.then_inc(paT2, 1)
            bar()
            for u, vis in enumerate(visits):
                j = vis["j"]; p = vis["p"]; s = j - 8 * p; W = vis["W"]; kb = vis["kb"]
                bi = next(iB for iB, b in enumerate(blocks)
                          if b["p"] == p and b["kb"] == kb)
                if u == p_firsts[p]:
                    for ss in range(8):
                        jn = 8 * p + ss + 1
                        t.wait_ge(pbQx, jn)
                        if jn > 1:
                            t.wait_ge(pbQe, jn - 1)
                        if p > 0 and ss == 0:
                            t.wait_ge(pbPe, p_firsts[p])
                        for cc in range(NCH):
                            ins = t.transpose(P3[:, cc * 128:(cc + 1) * 128],
                                              P_sb[:, cc * 128:(cc + 1) * 128], idb[:])
                        ins.then_inc(pbQT, 1)
                t.wait_ge(pbQe, 8 * (p + 1))
                t.wait_ge(pbKV, 256 * (bi + 1))
                if u > 0:
                    t.wait_ge(pbOt, 2 * u)
                nh = (W + 511) // 512
                for h in range(nh):
                    w_h = min(W - h * 512, 512)
                    for kc in range(NCH):
                        ins = t.matmul(
                            P1a[:, h * 512:h * 512 + w_h],
                            b_q[:, s * 1024 + kc * 128:s * 1024 + (kc + 1) * 128],
                            b_kv[bi % 2][0][:, kc * 1024 + h * 512:kc * 1024 + h * 512 + w_h],
                            start=(kc == 0), stop=(kc == NCH - 1))
                ins.then_inc(pbQK, 1)
                t.wait_ge(pbOt, 2 * u + 2)
                if u > 0:
                    t.wait_ge(pbPe, u)
                nsub = W // 128
                for sub in range(nsub):
                    ins = t.transpose(P3[:, sub * 128:(sub + 1) * 128],
                                      P_sb[:, sub * 128:(sub + 1) * 128], idb[:])
                ins.then_inc(pbPT, 1)
                t.wait_ge(pbPe, u + 1)
                if u > 0:
                    t.wait_ge(pbAc, u)
                for sub in range(nsub):
                    for h in range(2):
                        ins = t.matmul(P1b[:, h * 512:(h + 1) * 512],
                                       pt_sb[:, sub * 128:(sub + 1) * 128],
                                       b_kv[bi % 2][1][:, sub * 1024 + h * 512:sub * 1024 + (h + 1) * 512],
                                       start=(sub == 0), stop=(sub == nsub - 1))
                ins.then_inc(pbPV, 1)
            bar()
            ffn_tensor(t, base=0)
            bar()
            t.wait_ge(prX, 2)
            for cc in range(NCH):
                ins = t.transpose(P3[:, cc * OWN:(cc + 1) * OWN],
                                  prln[0:OWN, cc * 128:(cc + 1) * 128],
                                  idb[0:OWN, 0:OWN])
            ins.then_inc(prTT, 1)
            t.wait_ge(dW_, B_W + 96)
            for i in range(OWN):
                n = i + 1
                t.wait_ge(dX_, B_LN + n)
                t.wait_ge(prEv, 2)
                if i > 0:
                    t.wait_ge(dEv, B_LN + i)
                for cc in range(NCH):
                    ins = t.transpose(P3[:, cc * 128:(cc + 1) * 128],
                                      d_xln[:, cc * 128:(cc + 1) * 128], idb[:])
                ins.then_inc(dTT, 1)
                t.wait_ge(dM, n)
                if i > 0:
                    t.wait_ge(dKF, B_LN + i)
                for h in range(2):
                    for kc in range(NCH):
                        ins = t.matmul(P1a[:, h * 512:(h + 1) * 512],
                                       d_xkf[:, kc * 128:(kc + 1) * 128],
                                       wslot(0)[:, kc * D + h * 512:kc * D + (h + 1) * 512],
                                       start=(kc == 0), stop=(kc == NCH - 1))
                ins.then_inc(dP1, 1)
                if i > 0:
                    t.wait_ge(dVF, i)
                for h in range(2):
                    for kc in range(NCH):
                        ins = t.matmul(P1b[:, h * 512:(h + 1) * 512],
                                       d_xvf[:, kc * 128:(kc + 1) * 128],
                                       wslot(1)[:, kc * D + h * 512:kc * D + (h + 1) * 512],
                                       start=(kc == 0), stop=(kc == NCH - 1))
                ins.then_inc(dP2, 1)
                if i > 0:
                    t.wait_ge(dE1, i)
                for h in range(2):
                    for kc in range(NCH):
                        ins = t.matmul(P2[:, h * 512:(h + 1) * 512],
                                       d_xrf[:, kc * 128:(kc + 1) * 128],
                                       wslot(2)[:, kc * D + h * 512:kc * D + (h + 1) * 512],
                                       start=(kc == 0), stop=(kc == NCH - 1))
                ins.then_inc(dP3, 1)
                t.wait_ge(dKC, B_LN + n)
                for cc in range(NCH):
                    ins = t.transpose(P1a[:, cc * 128:(cc + 1) * 128],
                                      d_ktm[:, cc * 128:(cc + 1) * 128], idf[:])
                ins.then_inc(dTK, 1)
                t.wait_ge(dVC, B_LN + n)
                for cc in range(NCH):
                    ins = t.transpose(P1b[:, cc * 128:(cc + 1) * 128],
                                      d_vtm[:, cc * 128:(cc + 1) * 128], idf[:])
                ins.then_inc(dTV, 1)
                t.wait_ge(dSig, n)
                t.wait_ge(dEv, n)
                if i > 0:
                    t.wait_ge(dRE, i)
                for cc in range(NCH):
                    ins = t.transpose(P4[:, cc * 128:(cc + 1) * 128],
                                      d_rtm[:, cc * 128:(cc + 1) * 128], idb[:])
                ins.then_inc(dTR, 1)
            t.wait_ge(dScan, OWN)
            t.wait_ge(dSig, OWN)
            for q4 in range(4):
                ins = t.transpose(P2[:, q4 * 128:(q4 + 1) * 128],
                                  d_totc[:, q4 * 128:(q4 + 1) * 128], idf[:])
            ins.then_inc(dTB, 1)
            bar()
            t.wait_ge(exG, EXG_T["sig"])
            for kk in range(NCH):
                ins = t.transpose(P2[:, (kk % 4) * 256:(kk % 4) * 256 + 128] if False else
                                  P2[:, kk * 128:(kk + 1) * 128],
                                  d_sgr[:, kk * 128:(kk + 1) * 128], idf[:])
            ins.then_inc(eTz, 1)
            t.wait_ge(eW_, 16)
            for i in range(OWN):
                n = i + 1
                t.wait_ge(eRy, B_AC + n)
                if i > 0:
                    t.wait_ge(eOt, B_VO + i)
                for h in range(2):
                    for kc in range(NCH):
                        ins = t.matmul(P2[:, h * 512:(h + 1) * 512],
                                       e_ryb[:, kc * 128:(kc + 1) * 128],
                                       wslot(0)[:, kc * D + h * 512:kc * D + (h + 1) * 512],
                                       start=(kc == 0), stop=(kc == NCH - 1))
                ins.then_inc(eWo, 1)
            bar()
            ffn_tensor(t, base=1)

    _CACHE[key] = nc
    return nc


def _tiles_of(core):
    par = core % 2
    return [2 * j + 1 - par for j in range(OWN)]


def _flatmu(vec):
    col = np.ascontiguousarray(np.asarray(vec, np.float32).reshape(NCH, 128).T)
    return np.ascontiguousarray(
        np.broadcast_to(col[:, :, None], (128, NCH, 128)).reshape(128, D)
    ).astype(np.float32)


def _colsof(vec):
    return np.ascontiguousarray(np.asarray(vec, np.float32).reshape(NCH, 128).T)


def run_ffn_launch(stream_own, prow_own, Wr, Wk, Wv, mu_r, mu_k):
    nc = build_ffn()
    wrT = np.ascontiguousarray(np.asarray(Wr, np.float32).T)
    wkT = np.ascontiguousarray(np.asarray(Wk, np.float32).T)
    wvT = np.ascontiguousarray(np.asarray(Wv, np.float32).T)
    murf = _flatmu(mu_r)
    mukf = _flatmu(mu_k)
    identb = np.eye(128, dtype=np.float32).astype(BFnp)
    constv = np.ascontiguousarray(
        np.broadcast_to(np.array([EPS, 1.0], np.float32)[None, :], (128, 2)))
    in_maps = [dict(stream=stream_own[c], prow=prow_own[c], wrT=wrT, wkT=wkT,
                    wvT=wvT, murf=murf, mukf=mukf, identb=identb, constv=constv)
               for c in range(8)]
    res = run_bass_kernel_spmd(nc, in_maps, core_ids=list(range(8)))
    return [r["ostream"] for r in res.results]


def _consts():
    identb = np.eye(128, dtype=np.float32).astype(BFnp)
    constv = np.ascontiguousarray(
        np.broadcast_to(np.array([EPS, 1.0], np.float32)[None, :], (128, 2)))
    return identb, constv


def run_l1_launches(x_full, W):
    identb, constv = _consts()
    x_t = np.ascontiguousarray(x_full.reshape(B, NT, TILE, D)).astype(np.float32)
    x_own = [np.ascontiguousarray(x_t[c // 2, _tiles_of(c)]) for c in range(8)]
    wkvT = np.ascontiguousarray(np.asarray(W, np.float32).T)
    nc_a = build_l1a()
    res_a = run_bass_kernel_spmd(
        nc_a,
        [dict(x=x_own[c], wkvT=wkvT, identb=identb, constv=constv) for c in range(8)],
        core_ids=list(range(8)))
    ra = res_a.results
    kf = [np.zeros((NCH, 128, S), BFnp) for _ in range(B)]
    vf = [np.zeros((S, D), BFnp) for _ in range(B)]
    for c in range(8):
        b = c // 2
        kfm = ra[c]["o_kfm"].view(BFnp) if ra[c]["o_kfm"].dtype != BFnp else ra[c]["o_kfm"]
        vtm = ra[c]["o_vtm"].view(BFnp) if ra[c]["o_vtm"].dtype != BFnp else ra[c]["o_vtm"]
        for sl, g in enumerate(_tiles_of(c)):
            kf[b][:, :, g * TILE:(g + 1) * TILE] = kfm[:, :, sl, :]
            vf[b][g * TILE:(g + 1) * TILE, :] = vtm[sl]
    tri = np.where(np.arange(128)[None, :] <= np.arange(128)[:, None], 0.0, NEG).astype(np.float32)
    zer = np.zeros((128, 128), np.float32)
    alln = np.full((128, 128), NEG, np.float32)
    nc_b = build_l1b()
    in_maps = []
    for c in range(8):
        b = c // 2
        par = c % 2
        mp = np.stack([zer, tri]) if par == 0 else np.stack([tri, alln])
        in_maps.append(dict(
            q=np.ascontiguousarray(ra[c]["o_xfm"]),
            kf=np.ascontiguousarray(kf[b]),
            vf=np.ascontiguousarray(vf[b]),
            xtm=np.ascontiguousarray(ra[c]["o_xtm"]),
            x=x_own[c],
            maskp=np.ascontiguousarray(mp),
            identb=identb, constv=constv,
        ))
    res_b = run_bass_kernel_spmd(nc_b, in_maps, core_ids=list(range(8)))
    out = np.empty((B, NT, TILE, D), np.float32)
    for c in range(8):
        b = c // 2
        for sl, g in enumerate(_tiles_of(c)):
            out[b, g] = res_b.results[c]["ostream"][sl]
    return out


def _shard_stream(full_t):
    stream_own, prow_own = [], []
    for c in range(8):
        b = c // 2
        tiles = _tiles_of(c)
        stream_own.append(np.ascontiguousarray(full_t[b, tiles]))
        pr = np.zeros((OWN, D), np.float32)
        for j, g in enumerate(tiles):
            if g > 0:
                pr[j] = full_t[b, g - 1, TILE - 1]
        prow_own.append(pr)
    return stream_own, prow_own


def _assemble(per_core):
    out = np.empty((B, NT, TILE, D), np.float32)
    for c in range(8):
        b = c // 2
        for j, g in enumerate(_tiles_of(c)):
            out[b, g] = per_core[c][j]
    return out


def run_l3_launches(stream_t, Wk, Wv, Wr, Wo, td, tf, tmk, tmv, tmr):
    identb, constv = _consts()
    identf = np.eye(128, dtype=np.float32)
    stream_own, prow_own = _shard_stream(stream_t)
    wkT = np.ascontiguousarray(np.asarray(Wk, np.float32).T)
    wvT = np.ascontiguousarray(np.asarray(Wv, np.float32).T)
    wrT = np.ascontiguousarray(np.asarray(Wr, np.float32).T)
    woT = np.ascontiguousarray(np.asarray(Wo, np.float32).T)
    tmkf = _flatmu(np.asarray(tmk).reshape(-1))
    tmvf = _flatmu(np.asarray(tmv).reshape(-1))
    tmrf = _flatmu(np.asarray(tmr).reshape(-1))
    td_col = _colsof(td)
    tf_col = _colsof(tf)
    nc_a = build_l3a()
    res_a = run_bass_kernel_spmd(
        nc_a,
        [dict(stream=stream_own[c], prow=prow_own[c], wkT=wkT, wvT=wvT, wrT=wrT,
              tmkf=tmkf, tmvf=tmvf, tmrf=tmrf, td_col=td_col,
              identb=identb, identf=identf, constv=constv) for c in range(8)],
        core_ids=list(range(8)))
    ra = res_a.results
    tot_g = np.zeros((B, NT, 2, 128, 8), np.float32)
    for c in range(8):
        b = c // 2
        for j, g in enumerate(_tiles_of(c)):
            tot_g[b, g] = ra[c]["tot_o"][j]
    in_maps = []
    for c in range(8):
        b = c // 2
        par = c % 2
        sh = np.zeros((NT, 2, 128, 8), np.float32)
        if par == 0:
            sh[:] = tot_g[b]
        else:
            sh[1:] = tot_g[b][:-1]
        tot_sh = np.ascontiguousarray(sh.transpose(2, 1, 3, 0))
        in_maps.append(dict(
            stream=stream_own[c], z_sp=ra[c]["z_sp"], t1_sp=ra[c]["t1_sp"],
            r_sp=ra[c]["r_sp"], tot_sh=tot_sh, td_col=td_col, tf_col=tf_col,
            woT=woT, constv=constv))
    nc_c = build_l3c()
    res_c = run_bass_kernel_spmd(nc_c, in_maps, core_ids=list(range(8)))
    return _assemble([res_c.results[c]["ostream"] for c in range(8)])


def _fused_inmaps(inputs, debug=False):
    x = np.asarray(inputs["L0_attn_x"], np.float32).reshape(B, NT, TILE, D)
    identb, constv = _consts()
    identf = np.eye(128, dtype=np.float32)
    tri = np.where(np.arange(128)[None, :] <= np.arange(128)[:, None], 0.0, NEG).astype(np.float32)
    zer = np.zeros((128, 128), np.float32)
    alln = np.full((128, 128), NEG, np.float32)
    wkvT = np.ascontiguousarray(np.asarray(inputs["L0_attn_W"], np.float32).T)
    wT2 = np.ascontiguousarray(np.stack([
        np.asarray(inputs["L0_ffn_W_r"], np.float32).T,
        np.asarray(inputs["L0_ffn_W_k"], np.float32).T,
        np.asarray(inputs["L0_ffn_W_v"], np.float32).T]))
    wT3 = np.ascontiguousarray(np.stack([
        np.asarray(inputs["L1_attn_Wk"], np.float32).T,
        np.asarray(inputs["L1_attn_Wv"], np.float32).T,
        np.asarray(inputs["L1_attn_Wr"], np.float32).T]))
    woT3 = np.ascontiguousarray(np.asarray(inputs["L1_attn_Wo"], np.float32).T)
    wT4 = np.ascontiguousarray(np.stack([
        np.asarray(inputs["L1_ffn_W_r"], np.float32).T,
        np.asarray(inputs["L1_ffn_W_k"], np.float32).T,
        np.asarray(inputs["L1_ffn_W_v"], np.float32).T]))
    mu2f = np.ascontiguousarray(np.stack([
        _flatmu(inputs["L0_ffn_mu_r"]), _flatmu(inputs["L0_ffn_mu_k"])]))
    tm3f = np.ascontiguousarray(np.stack([
        _flatmu(np.asarray(inputs["L1_attn_time_mix_k"]).reshape(-1)),
        _flatmu(np.asarray(inputs["L1_attn_time_mix_v"]).reshape(-1)),
        _flatmu(np.asarray(inputs["L1_attn_time_mix_r"]).reshape(-1))]))
    mu4f = np.ascontiguousarray(np.stack([
        _flatmu(inputs["L1_ffn_mu_r"]), _flatmu(inputs["L1_ffn_mu_k"])]))
    td_col = _colsof(inputs["L1_attn_time_decay"])
    tf_col = _colsof(inputs["L1_attn_time_first"])
    in_maps = []
    for c in range(8):
        b = c // 2
        par = c % 2
        delta = 1 - par
        tiles = _tiles_of(c)
        mp = np.ascontiguousarray(np.stack([zer, tri]) if par == 0 else np.stack([tri, alln]))
        idxb = np.zeros((32, 1), np.uint32)
        zpm = np.ones((32, 1), np.float32)
        for j in range(OWN):
            t = 2 * j + delta - 1
            if t < 0:
                idxb[j, 0] = 0
                zpm[j, 0] = 0.0
            else:
                owner = 2 * b + 1 - (t % 2)
                idxb[j, 0] = owner * 32 + t // 2
        idxs = np.zeros((128, 8), np.uint32)
        for cc in range(8):
            for r in range(128):
                q = r // 64
                m = r % 64
                t = m + delta - 1
                if t < 0:
                    idxs[r, cc] = 0
                else:
                    owner = 2 * b + 1 - (t % 2)
                    idxs[r, cc] = owner * 512 + (t // 2) * 16 + q * 8 + cc
        tzm = np.full((128, 2), 1.0 if par == 0 else 0.0, np.float32)
        in_maps.append(dict(
            xfull=np.ascontiguousarray(x[b]),
            x_own=np.ascontiguousarray(x[b, tiles]),
            wkvT=wkvT, wT2=wT2, wT3=wT3, woT3=woT3, wT4=wT4,
            mu2f=mu2f, tm3f=tm3f, mu4f=mu4f,
            td_col=td_col, tf_col=tf_col, maskp=mp,
            identb=identb, identf=identf, constv=constv,
            idxb=idxb, idxs=idxs, zpm=zpm, tzm=tzm,
        ))
    return in_maps


def kernel_fused(debug=False, **inputs):
    nc = build_fused(debug=debug)
    in_maps = _fused_inmaps(inputs, debug=debug)
    res = run_bass_kernel_spmd(nc, in_maps, core_ids=list(range(8)))
    out = _assemble([res.results[c]["ostream"] for c in range(8)])
    if debug:
        return out, res
    return np.ascontiguousarray(out.reshape(B, S, D))


def kernel(**inputs):
    return kernel_fused(debug=False, **inputs)


# version 05
# speedup vs baseline: 1.0341x; 1.0341x over previous
import numpy as np
import ml_dtypes
from contextlib import ExitStack

import concourse.bass as bass
import concourse.mybir as mybir
from concourse.bass_utils import run_bass_kernel_spmd
from concourse.bass import IndirectOffsetOnAxis

F32 = mybir.dt.float32
BF16 = mybir.dt.bfloat16
AF = mybir.ActivationFunctionType
ALU = mybir.AluOpType
AX = mybir.AxisListType
BFnp = ml_dtypes.bfloat16

B, S, D, TILE = 4, 8192, 1024, 128
NT = S // TILE
OWN = NT // 2
NCH = D // 128
EPS = 1e-5
NEG = -30000.0

_CACHE = {}


def build_ffn():
    if "ffn" in _CACHE:
        return _CACHE["ffn"]
    nc = bass.Bass("TRN2", target_bir_lowering=False)
    stream = nc.dram_tensor("stream", [OWN, TILE, D], F32, kind="ExternalInput").ap()
    prow = nc.dram_tensor("prow", [OWN, D], F32, kind="ExternalInput").ap()
    wrT = nc.dram_tensor("wrT", [D, D], F32, kind="ExternalInput").ap()
    wkT = nc.dram_tensor("wkT", [D, D], F32, kind="ExternalInput").ap()
    wvT = nc.dram_tensor("wvT", [D, D], F32, kind="ExternalInput").ap()
    murf = nc.dram_tensor("murf", [128, D], F32, kind="ExternalInput").ap()
    mukf = nc.dram_tensor("mukf", [128, D], F32, kind="ExternalInput").ap()
    identb = nc.dram_tensor("identb", [128, 128], BF16, kind="ExternalInput").ap()
    constv = nc.dram_tensor("constv", [128, 2], F32, kind="ExternalInput").ap()
    ostream = nc.dram_tensor("ostream", [OWN, TILE, D], F32, kind="ExternalOutput").ap()

    with ExitStack() as ctx:
        def sb(name, shape, dt):
            return ctx.enter_context(nc.sbuf_tensor(name, shape, dt))

        def psr(name, shape, dt):
            return ctx.enter_context(nc.psum_tensor(name, shape, dt))

        def sem(name):
            return ctx.enter_context(nc.semaphore(name))

        xbuf = [sb("xbuf0", [128, D], F32), sb("xbuf1", [128, D], F32)]
        xb4 = xbuf + [sb("xbuf2", [128, D], F32), sb("xbuf3", [128, D], F32)]
        obuf = [sb("obuf0", [128, D], F32), sb("obuf1", [128, D], F32)]
        sqscr = sb("sqscr", [128, D], F32)
        xln = sb("xln", [128, D], BF16)
        fmx = sb("fmx", [128, D], BF16)
        fmp = sb("fmp", [128, D], BF16)
        dmix = sb("dmix", [128, D], BF16)
        xrf = sb("xrf", [128, D], BF16)
        xkf = sb("xkf", [128, D], BF16)
        e1t = sb("e1t", [128, D], F32)
        rt = sb("rt", [128, D], F32)
        krelu = sb("krelu", [128, D], BF16)
        kactb = sb("kactb", [128, D], BF16)
        kactf = sb("kactf", [128, D], BF16)
        ot = sb("ot", [128, D], F32)
        lnout = sb("lnout", [128, D], F32)
        wr_s = sb("wr_s", [128, NCH * D], BF16)
        wk_s = sb("wk_s", [128, NCH * D], BF16)
        wv_s = sb("wv_s", [128, NCH * D], BF16)
        mur_s = sb("mur_s", [128, D], BF16)
        muk_s = sb("muk_s", [128, D], BF16)
        idb = sb("idb", [128, 128], BF16)
        prraw = sb("prraw", [128, D], F32)
        prln = sb("prln", [128, D], BF16)
        prfm = sb("prfm", [128, NCH * OWN], BF16)
        stat = sb("stat", [128, 32], F32)
        s_cst = sb("s_cst", [128, 2], F32)
        nc.const_aps.register(F32, EPS, s_cst[:, 0:1])
        nc.const_aps.register(F32, 1.0, s_cst[:, 1:2])
        SUM, M, Q, VAR, SSQ, MSQ, LNV, RSTD, NB = range(9)

        def col(k, st=0):
            return stat[:, st * 16 + k:st * 16 + k + 1]

        p_tr = psr("p_tr", [128, 1024], BF16)
        p_r = psr("p_r", [128, 1024], F32)
        p_k = psr("p_k", [128, 1024], F32)
        p_v = psr("p_v", [128, 1024], F32)

        dIn = sem("dIn"); dPre = sem("dPre"); dW = sem("dW"); dOut = sem("dOut")
        vS = sem("vS"); aS = sem("aS"); vT = sem("vT"); aR = sem("aR")
        vN = sem("vN"); aX = sem("aX"); tT = sem("tT"); vEv = sem("vEv")
        vM = sem("vM"); tR = sem("tR"); tK = sem("tK"); aE1 = sem("aE1")
        aSig = sem("aSig"); aRel = sem("aRel"); aK = sem("aK"); tT2 = sem("tT2")
        vEv2 = sem("vEv2"); tV = sem("tV"); vO = sem("vO")
        aS2 = sem("aS2"); vT2 = sem("vT2"); aR2 = sem("aR2")
        vN2 = sem("vN2"); aX2 = sem("aX2"); vFin = sem("vFin")
        vPre = sem("vPre"); tPre = sem("tPre")
        vC = sem("vC"); aC = sem("aC")

        block = ctx.enter_context(nc.Block())

        @block.sync
        def _(sync):
            sync.dma_start(out=idb[:], in_=identb).then_inc(dPre, 16)
            sync.dma_start(out=prraw[0:OWN, :], in_=prow).then_inc(dPre, 16)
            sync.dma_start(out=s_cst[:], in_=constv).then_inc(dPre, 16)
            for i in range(OWN):
                if i >= 2:
                    sync.wait_ge(vFin, i - 1)
                sync.dma_start(out=xbuf[i % 2][:], in_=stream[i]).then_inc(dIn, 16)
            sync.wait_ge(dOut, 16 * OWN)

        @block.gpsimd
        def _(g):
            for w_dram, w_sb in ((wrT, wr_s), (wkT, wk_s), (wvT, wv_s)):
                g.dma_start(out=w_sb[:].rearrange("p (c f) -> p c f", c=NCH),
                            in_=w_dram.rearrange("(c p) f -> p c f", p=128)).then_inc(dW, 16)
            g.dma_start(out=mur_s[:], in_=murf).then_inc(dW, 16)
            g.dma_start(out=muk_s[:], in_=mukf).then_inc(dW, 16)
            for i in range(OWN):
                g.wait_ge(vFin, i + 1)
                g.dma_start(out=ostream[i], in_=obuf[i % 2][:]).then_inc(dOut, 16)
            g.wait_ge(dOut, 16 * OWN)

        @block.vector
        def _(v):
            _vn = [0]

            def ch(ins, cross=None, tgt=None):
                if cross is None:
                    _vn[0] += 1
                    ins.then_inc(vC, 1)
                    v.wait_ge(vC, _vn[0])
                else:
                    ins.then_inc(cross, 1)
                    v.wait_ge(cross, tgt)
                return ins

            v.wait_ge(dPre, 48)
            pr = prraw[0:OWN, :]
            st = stat[0:OWN, :]
            ch(v.reduce_sum(st[:, 0:1], pr, axis=AX.X))
            ch(v.tensor_scalar_mul(st[:, 1:2], st[:, 0:1], 1.0 / D))
            v.wait_ge(aS, 1)
            ch(v.tensor_scalar_mul(st[:, 2:3], st[:, 4:5], 1.0 / D))
            ch(v.tensor_mul(st[:, 5:6], st[:, 1:2], st[:, 1:2]))
            ch(v.tensor_sub(st[:, 3:4], st[:, 2:3], st[:, 5:6]), vT, 1)
            v.wait_ge(aR, 1)
            ch(v.tensor_scalar(st[:, 6:7], st[:, 1:2], st[:, 7:8], -1.0,
                               ALU.mult, ALU.mult), vN, 1)
            v.wait_ge(tPre, 1)
            ch(v.tensor_copy(prfm[:], p_tr[:, 0:NCH * OWN]), vPre, 1)

            for i in range(OWN):
                n = i + 1
                v.wait_ge(dIn, 16 * n)
                ch(v.reduce_sum(col(SUM), xbuf[i % 2][:], axis=AX.X))
                ch(v.tensor_scalar_mul(col(M), col(SUM), 1.0 / D))
                v.wait_ge(aS, n + 1)
                ch(v.tensor_scalar_mul(col(Q), col(SSQ), 1.0 / D))
                ch(v.tensor_mul(col(MSQ), col(M), col(M)))
                ch(v.tensor_sub(col(VAR), col(Q), col(MSQ)), vT, n + 1)
                v.wait_ge(aR, n + 1)
                ch(v.tensor_scalar(col(NB), col(M), col(RSTD), -1.0,
                                   ALU.mult, ALU.mult), vN, n + 1)
                v.wait_ge(tT, n)
                if i > 0:
                    v.wait_ge(tK, i)
                ch(v.tensor_copy(fmx[:], p_tr[:]), vEv, n)
                v.wait_ge(vPre, 1)
                for cc in range(NCH):
                    ch(v.tensor_copy(fmp[:, cc * 128 + 1:cc * 128 + 128],
                                     fmx[:, cc * 128:cc * 128 + 127]))
                    ch(v.tensor_copy(fmp[:, cc * 128:cc * 128 + 1],
                                     prfm[:, cc * OWN + i:cc * OWN + i + 1]))
                v.wait_ge(dW, 80)
                ch(v.tensor_sub(dmix[:], fmx[:], fmp[:]))
                ch(v.tensor_mul(xrf[:], dmix[:], mur_s[:]))
                ch(v.tensor_add(xrf[:], xrf[:], fmp[:]))
                ch(v.tensor_mul(xkf[:], dmix[:], muk_s[:]))
                ch(v.tensor_add(xkf[:], xkf[:], fmp[:]), vM, n)
                v.wait_ge(tT2, n)
                ch(v.tensor_copy(kactf[:], p_tr[:]), vEv2, n)
                v.wait_ge(tV, n)
                v.wait_ge(aSig, n)
                ch(v.tensor_mul(ot[:], rt[:], p_v[:]), vO, n)
                ch(v.reduce_sum(col(SUM, 1), ot[:], axis=AX.X))
                ch(v.tensor_scalar_mul(col(M, 1), col(SUM, 1), 1.0 / D))
                v.wait_ge(aS2, n)
                ch(v.tensor_scalar_mul(col(Q, 1), col(SSQ, 1), 1.0 / D))
                ch(v.tensor_mul(col(MSQ, 1), col(M, 1), col(M, 1)))
                ch(v.tensor_sub(col(VAR, 1), col(Q, 1), col(MSQ, 1)), vT2, n)
                v.wait_ge(aR2, n)
                ch(v.tensor_scalar(col(NB, 1), col(M, 1), col(RSTD, 1), -1.0,
                                   ALU.mult, ALU.mult), vN2, n)
                v.wait_ge(aX2, n)
                if i >= 2:
                    v.wait_ge(dOut, 16 * (i - 1))
                ch(v.tensor_add(obuf[i % 2][:], xbuf[i % 2][:], lnout[:]), vFin, n)

        @block.scalar
        def _(a):
            _an = [0]

            def ca(ins, cross=None, tgt=None):
                if cross is None:
                    _an[0] += 1
                    ins.then_inc(aC, 1)
                    a.wait_ge(aC, _an[0])
                else:
                    ins.then_inc(cross, 1)
                    a.wait_ge(cross, tgt)
                return ins

            a.wait_ge(dPre, 48)
            pr = prraw[0:OWN, :]
            st = stat[0:OWN, :]
            ca(a.activation(sqscr[0:OWN, :], pr, AF.Square,
                            accum_out=st[:, 4:5]), aS, 1)
            a.wait_ge(vT, 1)
            ca(a.activation(st[:, 8:9], st[:, 3:4], AF.Ln, bias=EPS))
            ca(a.activation(st[:, 7:8], st[:, 8:9], AF.Exp, scale=-0.5), aR, 1)
            a.wait_ge(vN, 1)
            ca(a.activation(prln[0:OWN, :], pr, AF.Identity, scale=st[:, 7:8],
                            bias=st[:, 6:7]), aX, 1)

            for i in range(OWN):
                n = i + 1
                a.wait_ge(dIn, 16 * n)
                ca(a.activation(sqscr[:], xbuf[i % 2][:], AF.Square,
                                accum_out=col(SSQ)), aS, n + 1)
                a.wait_ge(vT, n + 1)
                ca(a.activation(col(LNV), col(VAR), AF.Ln, bias=EPS))
                ca(a.activation(col(RSTD), col(LNV), AF.Exp, scale=-0.5), aR, n + 1)
                a.wait_ge(vN, n + 1)
                ca(a.activation(xln[:], xbuf[i % 2][:], AF.Identity, scale=col(RSTD),
                                bias=col(NB)), aX, n + 1)
                a.wait_ge(tR, n)
                ca(a.activation(e1t[:], p_r[:], AF.Exp, scale=-1.0), aE1, n)
                ca(a.activation(rt[:], e1t[:], AF.Ln, bias=1.0))
                ca(a.activation(rt[:], rt[:], AF.Exp, scale=-1.0), aSig, n)
                a.wait_ge(tK, n)
                ca(a.activation(krelu[:], p_k[:], AF.Relu), aRel, n)
                ca(a.activation(kactb[:], krelu[:], AF.Square), aK, n)
                a.wait_ge(vO, n)
                ca(a.activation(sqscr[:], ot[:], AF.Square,
                                accum_out=col(SSQ, 1)), aS2, n)
                a.wait_ge(vT2, n)
                ca(a.activation(col(LNV, 1), col(VAR, 1), AF.Ln, bias=EPS))
                ca(a.activation(col(RSTD, 1), col(LNV, 1), AF.Exp, scale=-0.5), aR2, n)
                a.wait_ge(vN2, n)
                ca(a.activation(lnout[:], ot[:], AF.Identity, scale=col(RSTD, 1),
                                bias=col(NB, 1)), aX2, n)

        @block.tensor
        def _(t):
            t.wait_ge(aX, 1)
            t.wait_ge(dPre, 48)
            for cc in range(NCH):
                ins = t.transpose(p_tr[:, cc * OWN:(cc + 1) * OWN],
                                  prln[0:OWN, cc * 128:(cc + 1) * 128],
                                  idb[0:OWN, 0:OWN])
            ins.then_inc(tPre, 1)

            for i in range(OWN):
                n = i + 1
                t.wait_ge(aX, n + 1)
                t.wait_ge(vPre, 1)
                if i > 0:
                    t.wait_ge(vEv2, i)
                for cc in range(NCH):
                    ins = t.transpose(p_tr[:, cc * 128:(cc + 1) * 128],
                                      xln[:, cc * 128:(cc + 1) * 128], idb[:])
                ins.then_inc(tT, 1)
                t.wait_ge(vM, n)
                t.wait_ge(dW, 80)
                if i > 0:
                    t.wait_ge(aE1, i)
                for h in range(2):
                    for kc in range(NCH):
                        ins = t.matmul(p_r[:, h * 512:(h + 1) * 512],
                                       xrf[:, kc * 128:(kc + 1) * 128],
                                       wr_s[:, kc * D + h * 512:kc * D + (h + 1) * 512],
                                       start=(kc == 0), stop=(kc == NCH - 1))
                ins.then_inc(tR, 1)
                if i > 0:
                    t.wait_ge(aRel, i)
                for h in range(2):
                    for kc in range(NCH):
                        ins = t.matmul(p_k[:, h * 512:(h + 1) * 512],
                                       xkf[:, kc * 128:(kc + 1) * 128],
                                       wk_s[:, kc * D + h * 512:kc * D + (h + 1) * 512],
                                       start=(kc == 0), stop=(kc == NCH - 1))
                ins.then_inc(tK, 1)
                t.wait_ge(aK, n)
                t.wait_ge(vEv, n)
                for cc in range(NCH):
                    ins = t.transpose(p_tr[:, cc * 128:(cc + 1) * 128],
                                      kactb[:, cc * 128:(cc + 1) * 128], idb[:])
                ins.then_inc(tT2, 1)
                t.wait_ge(vEv2, n)
                if i > 0:
                    t.wait_ge(vO, i)
                for h in range(2):
                    for kc in range(NCH):
                        ins = t.matmul(p_v[:, h * 512:(h + 1) * 512],
                                       kactf[:, kc * 128:(kc + 1) * 128],
                                       wv_s[:, kc * D + h * 512:kc * D + (h + 1) * 512],
                                       start=(kc == 0), stop=(kc == NCH - 1))
                ins.then_inc(tV, 1)

    _CACHE["ffn"] = nc
    return nc



def build_l1a():
    if "l1a" in _CACHE:
        return _CACHE["l1a"]
    nc = bass.Bass("TRN2", target_bir_lowering=False)
    x_d = nc.dram_tensor("x", [OWN, TILE, D], F32, kind="ExternalInput").ap()
    wkvT = nc.dram_tensor("wkvT", [D, 2 * D], F32, kind="ExternalInput").ap()
    identb = nc.dram_tensor("identb", [128, 128], BF16, kind="ExternalInput").ap()
    constv = nc.dram_tensor("constv", [128, 2], F32, kind="ExternalInput").ap()
    o_xtm = nc.dram_tensor("o_xtm", [OWN, TILE, D], F32, kind="ExternalOutput").ap()
    o_xfm = nc.dram_tensor("o_xfm", [OWN, NCH, 128, 128], BF16, kind="ExternalOutput").ap()
    o_kfm = nc.dram_tensor("o_kfm", [NCH, 128, OWN, 128], BF16, kind="ExternalOutput").ap()
    o_vtm = nc.dram_tensor("o_vtm", [OWN, TILE, D], BF16, kind="ExternalOutput").ap()

    with ExitStack() as ctx:
        def sb(name, shape, dt):
            return ctx.enter_context(nc.sbuf_tensor(name, shape, dt))

        def psr(name, shape, dt):
            return ctx.enter_context(nc.psum_tensor(name, shape, dt))

        def sem(name):
            return ctx.enter_context(nc.semaphore(name))

        xbuf = [sb("xbuf0", [128, D], F32), sb("xbuf1", [128, D], F32)]
        xb4 = xbuf + [sb("xbuf2", [128, D], F32), sb("xbuf3", [128, D], F32)]
        xq32 = [sb("xq320", [128, D], F32), sb("xq321", [128, D], F32)]
        fmx = [sb("fmx0", [128, D], BF16), sb("fmx1", [128, D], BF16)]
        vbf = [sb("vbf0", [128, D], BF16), sb("vbf1", [128, D], BF16)]
        kfms = [sb("kfms0", [128, D], BF16), sb("kfms1", [128, D], BF16)]
        xln = sb("xln", [128, D], BF16)
        ktm = sb("ktm", [128, D], BF16)
        sqscr = sb("sqscr", [128, D], F32)
        wkv_s = sb("wkv_s", [128, NCH * 2 * D], BF16)
        idb = sb("idb", [128, 128], BF16)
        stat = sb("stat", [128, 16], F32)
        s_cst = sb("s_cst", [128, 2], F32)
        nc.const_aps.register(F32, EPS, s_cst[:, 0:1])
        SUM, M, Q, VAR, SSQ, MSQ, LNV, RSTD, NB = range(9)

        def col(k):
            return stat[:, k:k + 1]

        p_tr = psr("p_tr", [128, 1024], BF16)
        p_kv = psr("p_kv", [128, 2048], F32)

        dIn = sem("dIn"); dPre = sem("dPre"); dW = sem("dW")
        dO1 = sem("dO1"); dO2 = sem("dO2")
        aS = sem("aS"); vT = sem("vT"); aR = sem("aR"); vN = sem("vN")
        aX = sem("aX"); aX2 = sem("aX2"); tT = sem("tT"); vEv = sem("vEv")
        tKV = sem("tKV"); aKC = sem("aKC"); vVC = sem("vVC"); tT2 = sem("tT2")
        vKE = sem("vKE")
        vC = sem("vC"); aC = sem("aC")

        block = ctx.enter_context(nc.Block())

        @block.sync
        def _(sync):
            sync.dma_start(out=idb[:], in_=identb).then_inc(dPre, 16)
            sync.dma_start(out=s_cst[:], in_=constv).then_inc(dPre, 16)
            for i in range(OWN + 1):
                if i < OWN:
                    if i >= 2:
                        sync.wait_ge(aX2, i - 1)
                    sync.dma_start(out=xbuf[i % 2][:], in_=x_d[i]).then_inc(dIn, 16)
                if i >= 1:
                    k = i - 1
                    sync.wait_ge(aX2, k + 1)
                    sync.dma_start(out=o_xtm[k], in_=xq32[k % 2][:]).then_inc(dO1, 16)
                    sync.wait_ge(vEv, k + 1)
                    sync.dma_start(out=o_xfm[k].rearrange("c p t -> p c t"),
                                   in_=fmx[k % 2][:].rearrange("p (c t) -> p c t", c=NCH)).then_inc(dO1, 16)
            sync.wait_ge(dO1, 32 * OWN)
            sync.wait_ge(dO2, 32 * OWN)

        @block.gpsimd
        def _(g):
            g.dma_start(out=wkv_s[:].rearrange("p (c f) -> p c f", c=NCH),
                        in_=wkvT.rearrange("(c p) f -> p c f", p=128)).then_inc(dW, 16)
            for i in range(OWN):
                g.wait_ge(vVC, i + 1)
                g.dma_start(out=o_vtm[i], in_=vbf[i % 2][:]).then_inc(dO2, 16)
                g.wait_ge(vKE, i + 1)
                g.dma_start(out=o_kfm[:, :, i, :].rearrange("c p t -> p c t"),
                            in_=kfms[i % 2][:].rearrange("p (c t) -> p c t", c=NCH)).then_inc(dO2, 16)
            g.wait_ge(dO2, 32 * OWN)

        @block.vector
        def _(v):
            _vn = [0]

            def ch(ins, cross=None, tgt=None):
                if cross is None:
                    _vn[0] += 1
                    ins.then_inc(vC, 1)
                    v.wait_ge(vC, _vn[0])
                else:
                    ins.then_inc(cross, 1)
                    v.wait_ge(cross, tgt)
                return ins

            for i in range(OWN):
                n = i + 1
                v.wait_ge(dIn, 16 * n)
                ch(v.reduce_sum(col(SUM), xbuf[i % 2][:], axis=AX.X))
                ch(v.tensor_scalar_mul(col(M), col(SUM), 1.0 / D))
                v.wait_ge(aS, n)
                ch(v.tensor_scalar_mul(col(Q), col(SSQ), 1.0 / D))
                ch(v.tensor_mul(col(MSQ), col(M), col(M)))
                ch(v.tensor_sub(col(VAR), col(Q), col(MSQ)), vT, n)
                v.wait_ge(aR, n)
                ch(v.tensor_scalar(col(NB), col(M), col(RSTD), -1.0,
                                   ALU.mult, ALU.mult), vN, n)
                v.wait_ge(tT, n)
                if i >= 2:
                    v.wait_ge(dO1, 32 * (i - 1))
                ch(v.tensor_copy(fmx[i % 2][:], p_tr[:]), vEv, n)
                v.wait_ge(tKV, n)
                if i >= 2:
                    v.wait_ge(dO2, 32 * (i - 1))
                ch(v.tensor_copy(vbf[i % 2][:], p_kv[:, 1024:2048]), vVC, n)
                v.wait_ge(tT2, n)
                ch(v.tensor_copy(kfms[i % 2][:], p_tr[:]), vKE, n)

        @block.scalar
        def _(a):
            _an = [0]

            def ca(ins, cross=None, tgt=None):
                if cross is None:
                    _an[0] += 1
                    ins.then_inc(aC, 1)
                    a.wait_ge(aC, _an[0])
                else:
                    ins.then_inc(cross, 1)
                    a.wait_ge(cross, tgt)
                return ins

            a.wait_ge(dPre, 32)
            for i in range(OWN):
                n = i + 1
                a.wait_ge(dIn, 16 * n)
                ca(a.activation(sqscr[:], xbuf[i % 2][:], AF.Square,
                                accum_out=col(SSQ)), aS, n)
                a.wait_ge(vT, n)
                ca(a.activation(col(LNV), col(VAR), AF.Ln, bias=EPS))
                ca(a.activation(col(RSTD), col(LNV), AF.Exp, scale=-0.5), aR, n)
                a.wait_ge(vN, n)
                ca(a.activation(xln[:], xbuf[i % 2][:], AF.Identity, scale=col(RSTD),
                                bias=col(NB)), aX, n)
                if i >= 2:
                    a.wait_ge(dO1, 32 * (i - 1) - 16)
                ca(a.activation(xq32[i % 2][:], xbuf[i % 2][:], AF.Identity,
                                scale=col(RSTD), bias=col(NB)), aX2, n)
                a.wait_ge(tKV, n)
                ca(a.activation(ktm[:], p_kv[:, 0:1024], AF.Identity), aKC, n)

        @block.tensor
        def _(t):
            t.wait_ge(dPre, 32)
            for i in range(OWN):
                n = i + 1
                t.wait_ge(aX, n)
                if i > 0:
                    t.wait_ge(vEv, i)
                for cc in range(NCH):
                    ins = t.transpose(p_tr[:, cc * 128:(cc + 1) * 128],
                                      xln[:, cc * 128:(cc + 1) * 128], idb[:])
                ins.then_inc(tT, 1)
                t.wait_ge(tT, n)
                t.wait_ge(vEv, n)
                t.wait_ge(dW, 16)
                if i > 0:
                    t.wait_ge(aKC, i)
                    t.wait_ge(vVC, i)
                for h in range(4):
                    for kc in range(NCH):
                        ins = t.matmul(p_kv[:, h * 512:(h + 1) * 512],
                                       fmx[i % 2][:, kc * 128:(kc + 1) * 128],
                                       wkv_s[:, kc * 2048 + h * 512:kc * 2048 + (h + 1) * 512],
                                       start=(kc == 0), stop=(kc == NCH - 1))
                ins.then_inc(tKV, 1)
                t.wait_ge(aKC, n)
                t.wait_ge(vEv, n)
                if i > 0:
                    t.wait_ge(vKE, i)
                for cc in range(NCH):
                    ins = t.transpose(p_tr[:, cc * 128:(cc + 1) * 128],
                                      ktm[:, cc * 128:(cc + 1) * 128], idb[:])
                ins.then_inc(tT2, 1)

    _CACHE["l1a"] = nc
    return nc


def _l1b_schedule():
    visits = []
    blocks = []
    fins = []
    for p, slots in ((0, range(0, 16)), (1, range(16, 32))):
        nkb = 4 if p == 0 else 8
        for kb in range(nkb):
            act = [j for j in slots if 2 * j + 2 > 8 * kb]
            blocks.append(dict(p=p, kb=kb, first=len(visits), nact=len(act)))
            for j in act:
                W = min(128 * (2 * j + 2) - 1024 * kb, 1024)
                visits.append(dict(p=p, kb=kb, j=j, W=W,
                                   mask=(kb == j // 4), fin=False))
            for j in act:
                if (2 * j + 2) <= 8 * (kb + 1):
                    fins.append(dict(j=j, after=len(visits) - 1))
    for f_i, f in enumerate(fins):
        f["fi"] = f_i
    return visits, blocks, fins


def build_l1b():
    if "l1b" in _CACHE:
        return _CACHE["l1b"]
    visits, blocks, fins = _l1b_schedule()
    NV = len(visits)
    nc = bass.Bass("TRN2", target_bir_lowering=False)
    q_d = nc.dram_tensor("q", [OWN, NCH, 128, 128], BF16, kind="ExternalInput").ap()
    kf_d = nc.dram_tensor("kf", [NCH, 128, S], BF16, kind="ExternalInput").ap()
    vf_d = nc.dram_tensor("vf", [S, D], BF16, kind="ExternalInput").ap()
    xtm_d = nc.dram_tensor("xtm", [OWN, TILE, D], F32, kind="ExternalInput").ap()
    x_d = nc.dram_tensor("x", [OWN, TILE, D], F32, kind="ExternalInput").ap()
    maskp = nc.dram_tensor("maskp", [2, 128, 128], F32, kind="ExternalInput").ap()
    identb = nc.dram_tensor("identb", [128, 128], BF16, kind="ExternalInput").ap()
    constv = nc.dram_tensor("constv", [128, 2], F32, kind="ExternalInput").ap()
    ostream = nc.dram_tensor("ostream", [OWN, TILE, D], F32, kind="ExternalOutput").ap()

    with ExitStack() as ctx:
        def sb(name, shape, dt):
            return ctx.enter_context(nc.sbuf_tensor(name, shape, dt))

        def psr(name, shape, dt):
            return ctx.enter_context(nc.psum_tensor(name, shape, dt))

        def sem(name):
            return ctx.enter_context(nc.semaphore(name))

        q_sb = sb("q_sb", [128, 16 * 1024], BF16)
        acc = sb("acc", [128, 16 * 1024], F32)
        kbuf = [sb("kbuf0", [128, NCH * 1024], BF16), sb("kbuf1", [128, NCH * 1024], BF16)]
        vbuf = [sb("vbuf0", [128, NCH * 1024], BF16), sb("vbuf1", [128, NCH * 1024], BF16)]
        P_sb = sb("P_sb", [128, 1024], BF16)
        pt_sb = sb("pt_sb", [128, 1024], BF16)
        msk = sb("msk", [128, 256], F32)
        negm = sb("negm", [128, 16], F32)
        lcol = sb("lcol", [128, 16], F32)
        alpha = sb("alpha", [128, 16], F32)
        rsum = sb("rsum", [128, 16], F32)
        tmpa = sb("tmpa", [128, 16], F32)
        xtb = [sb("xtb0", [128, D], F32), sb("xtb1", [128, D], F32)]
        xrb = [sb("xrb0", [128, D], F32), sb("xrb1", [128, D], F32)]
        obuf = [sb("obuf0", [128, D], F32), sb("obuf1", [128, D], F32)]
        otb = sb("otb", [128, D], F32)
        lnout = sb("lnout", [128, D], F32)
        sqscr = sb("sqscr", [128, D], F32)
        stat = sb("stat", [128, 16], F32)
        s_cst = sb("s_cst", [128, 2], F32)
        idb = sb("idb", [128, 128], BF16)
        nc.const_aps.register(F32, EPS, s_cst[:, 0:1])
        SUM, M, Q, VAR, SSQ, MSQ, LNV, RSTD, NB = range(9)

        def col(k):
            return stat[:, k:k + 1]

        p_sc = psr("p_sc", [128, 1024], F32)
        p_pv = psr("p_pv", [128, 1024], F32)
        p_pt = psr("p_pt", [128, 1024], BF16)

        dPre = sem("dPre"); dQ = sem("dQ"); dKV = sem("dKV"); dFin = sem("dFin")
        dOut = sem("dOut")
        tQK = sem("tQK"); vStat = sem("vStat"); aExp = sem("aExp")
        tPT = sem("tPT"); vPTe = sem("vPTe"); tPV = sem("tPV"); vAcc = sem("vAcc")
        vMS = sem("vMS"); vOt = sem("vOt")
        aS2 = sem("aS2"); vT2 = sem("vT2"); aR2 = sem("aR2"); vN2 = sem("vN2")
        aX2 = sem("aX2"); vFin = sem("vFin")
        vC = sem("vC"); aC = sem("aC")

        block = ctx.enter_context(nc.Block())

        p1_first = next(b["first"] for b in blocks if b["p"] == 1)

        blk_end = []
        for bi, b in enumerate(blocks):
            blk_end.append(b["first"] + b["nact"])

        @block.sync
        def _(sync):
            sync.dma_start(out=msk[:, 0:128], in_=maskp[0]).then_inc(dPre, 16)
            sync.dma_start(out=msk[:, 128:256], in_=maskp[1]).then_inc(dPre, 16)
            sync.dma_start(out=s_cst[:], in_=constv).then_inc(dPre, 16)
            sync.dma_start(out=idb[:], in_=identb).then_inc(dPre, 16)
            for s in range(16):
                sync.dma_start(out=q_sb[:].rearrange("p (s c t) -> p s c t", s=16, c=NCH)[:, s],
                               in_=q_d[s].rearrange("c p t -> p c t")).then_inc(dQ, 16)
            for bi, b in enumerate(blocks):
                if b["p"] == 1 and b["kb"] == 0:
                    sync.wait_ge(vFin, 16)
                    for s in range(16):
                        sync.dma_start(
                            out=q_sb[:].rearrange("p (s c t) -> p s c t", s=16, c=NCH)[:, s],
                            in_=q_d[16 + s].rearrange("c p t -> p c t")).then_inc(dQ, 16)
                if bi >= 2:
                    sync.wait_ge(tPV, blk_end[bi - 2])
                kb = b["kb"]
                for dc in range(NCH):
                    sync.dma_start(
                        out=kbuf[bi % 2][:, dc * 1024:(dc + 1) * 1024],
                        in_=kf_d[dc, :, kb * 1024:(kb + 1) * 1024]).then_inc(dKV, 16)
                for sub in range(NCH):
                    sync.dma_start(
                        out=vbuf[bi % 2][:, sub * 1024:(sub + 1) * 1024],
                        in_=vf_d[kb * 1024 + sub * 128:kb * 1024 + (sub + 1) * 128, :]).then_inc(dKV, 16)
                for f in [f for f in fins if visits[f["after"]]["kb"] == kb and visits[f["after"]]["p"] == b["p"]]:
                    fi = f["fi"]
                    if fi >= 2:
                        sync.wait_ge(vFin, fi - 1)
                    sync.dma_start(out=xtb[fi % 2][:], in_=xtm_d[f["j"]]).then_inc(dFin, 16)
                    sync.dma_start(out=xrb[fi % 2][:], in_=x_d[f["j"]]).then_inc(dFin, 16)
            sync.wait_ge(dOut, 16 * OWN)

        @block.gpsimd
        def _(g):
            g.memset(acc[:], 0.0).then_inc(vMS, 1)
            g.memset(lcol[:], 0.0).then_inc(vMS, 1)
            g.memset(negm[:], 3.0e38).then_inc(vMS, 1)
            for f in fins[:16]:
                fi = f["fi"]
                g.wait_ge(vFin, fi + 1)
                g.dma_start(out=ostream[f["j"]], in_=obuf[fi % 2][:]).then_inc(dOut, 16)
            g.memset(acc[:], 0.0).then_inc(vMS, 1)
            g.memset(lcol[:], 0.0).then_inc(vMS, 1)
            g.memset(negm[:], 3.0e38).then_inc(vMS, 1)
            for f in fins[16:]:
                fi = f["fi"]
                g.wait_ge(vFin, fi + 1)
                g.dma_start(out=ostream[f["j"]], in_=obuf[fi % 2][:]).then_inc(dOut, 16)
            g.wait_ge(dOut, 16 * OWN)

        @block.vector
        def _(v):
            _vn = [0]

            def ch(ins, cross=None, tgt=None):
                if cross is None:
                    _vn[0] += 1
                    ins.then_inc(vC, 1)
                    v.wait_ge(vC, _vn[0])
                else:
                    ins.then_inc(cross, 1)
                    v.wait_ge(cross, tgt)
                return ins

            v.wait_ge(dPre, 64)
            fin_iter = iter(fins)
            nextf = next(fin_iter, None)
            for u, vis in enumerate(visits):
                j = vis["j"]; s = j - 16 * vis["p"]; W = vis["W"]
                if vis["p"] == 0 and u == 0:
                    v.wait_ge(vMS, 3)
                if vis["p"] == 1 and u == p1_first:
                    v.wait_ge(vMS, 6)
                v.wait_ge(tQK, u + 1)
                if vis["mask"]:
                    r0 = 2 * j - 8 * vis["kb"]
                    ch(v.tensor_add(p_sc[:, r0 * 128:(r0 + 1) * 128],
                                    p_sc[:, r0 * 128:(r0 + 1) * 128], msk[:, 0:128]))
                    ch(v.tensor_add(p_sc[:, (r0 + 1) * 128:(r0 + 2) * 128],
                                    p_sc[:, (r0 + 1) * 128:(r0 + 2) * 128], msk[:, 128:256]))
                ch(v.reduce_max(tmpa[:, 0:1], p_sc[:, 0:W], axis=AX.X, negate=True))
                ch(v.tensor_tensor(tmpa[:, 1:2], negm[:, s:s + 1], tmpa[:, 0:1], ALU.min))
                ch(v.tensor_sub(tmpa[:, 2:3], tmpa[:, 1:2], negm[:, s:s + 1]))
                ch(v.tensor_copy(negm[:, s:s + 1], tmpa[:, 1:2]), vStat, u + 1)
                v.wait_ge(tPT, u + 1)
                ch(v.tensor_copy(pt_sb[:, 0:W], p_pt[:, 0:W]), vPTe, u + 1)
                v.wait_ge(tPV, u + 1)
                v.wait_ge(aExp, 2 * (u + 1))
                ch(v.scalar_tensor_tensor(lcol[:, s:s + 1], lcol[:, s:s + 1],
                                          alpha[:, s:s + 1], rsum[:, s:s + 1],
                                          ALU.mult, ALU.add))
                ch(v.scalar_tensor_tensor(acc[:, s * 1024:(s + 1) * 1024],
                                          acc[:, s * 1024:(s + 1) * 1024],
                                          alpha[:, s:s + 1], p_pv[:],
                                          ALU.mult, ALU.add), vAcc, u + 1)
                while nextf is not None and nextf["after"] == u:
                    f = nextf; fi = f["fi"]; fs = f["j"] - 16 * vis["p"]
                    fn = fi + 1
                    v.wait_ge(dFin, 32 * fn)
                    ch(v.reciprocal(tmpa[:, 3:4], lcol[:, fs:fs + 1]))
                    ch(v.scalar_tensor_tensor(otb[:], acc[:, fs * 1024:(fs + 1) * 1024],
                                              tmpa[:, 3:4], xtb[fi % 2][:],
                                              ALU.mult, ALU.add), vOt, fn)
                    ch(v.reduce_sum(col(SUM), otb[:], axis=AX.X))
                    ch(v.tensor_scalar_mul(col(M), col(SUM), 1.0 / D))
                    v.wait_ge(aS2, fn)
                    ch(v.tensor_scalar_mul(col(Q), col(SSQ), 1.0 / D))
                    ch(v.tensor_mul(col(MSQ), col(M), col(M)))
                    ch(v.tensor_sub(col(VAR), col(Q), col(MSQ)), vT2, fn)
                    v.wait_ge(aR2, fn)
                    ch(v.tensor_scalar(col(NB), col(M), col(RSTD), -1.0,
                                       ALU.mult, ALU.mult), vN2, fn)
                    v.wait_ge(aX2, fn)
                    if fi >= 2:
                        v.wait_ge(dOut, 16 * (fi - 1))
                    ch(v.tensor_add(obuf[fi % 2][:], xrb[fi % 2][:], lnout[:]), vFin, fn)
                    nextf = next(fin_iter, None)

        @block.scalar
        def _(a):
            _an = [0]

            def ca(ins, cross=None, tgt=None):
                if cross is None:
                    _an[0] += 1
                    ins.then_inc(aC, 1)
                    a.wait_ge(aC, _an[0])
                else:
                    ins.then_inc(cross, 1)
                    a.wait_ge(cross, tgt)
                return ins

            a.wait_ge(dPre, 64)
            fin_iter = iter(fins)
            nextf = next(fin_iter, None)
            for u, vis in enumerate(visits):
                j = vis["j"]; s = j - 16 * vis["p"]; W = vis["W"]
                a.wait_ge(vStat, u + 1)
                ca(a.activation(alpha[:, s:s + 1], tmpa[:, 2:3], AF.Exp), aExp, 2 * u + 1)
                ca(a.activation(P_sb[:, 0:W], p_sc[:, 0:W], AF.Exp,
                                bias=negm[:, s:s + 1],
                                accum_out=rsum[:, s:s + 1]), aExp, 2 * u + 2)
                while nextf is not None and nextf["after"] == u:
                    f = nextf; fi = f["fi"]; fn = fi + 1
                    a.wait_ge(vOt, fn)
                    ca(a.activation(sqscr[:], otb[:], AF.Square,
                                    accum_out=col(SSQ)), aS2, fn)
                    a.wait_ge(vT2, fn)
                    ca(a.activation(col(LNV), col(VAR), AF.Ln, bias=EPS))
                    ca(a.activation(col(RSTD), col(LNV), AF.Exp, scale=-0.5), aR2, fn)
                    a.wait_ge(vN2, fn)
                    ca(a.activation(lnout[:], otb[:], AF.Identity, scale=col(RSTD),
                                    bias=col(NB)), aX2, fn)
                    nextf = next(fin_iter, None)

        @block.tensor
        def _(t):
            t.wait_ge(dPre, 64)
            for u, vis in enumerate(visits):
                j = vis["j"]; s = j - 16 * vis["p"]; W = vis["W"]; kb = vis["kb"]
                bi = next(i for i, b in enumerate(blocks)
                          if b["p"] == vis["p"] and b["kb"] == kb)
                t.wait_ge(dQ, 256 * (vis["p"] + 1))
                t.wait_ge(dKV, 256 * (bi + 1))
                if u > 0:
                    t.wait_ge(aExp, 2 * u)
                nh = (W + 511) // 512
                for h in range(nh):
                    w_h = min(W - h * 512, 512)
                    for kc in range(NCH):
                        ins = t.matmul(
                            p_sc[:, h * 512:h * 512 + w_h],
                            q_sb[:, s * 1024 + kc * 128:s * 1024 + (kc + 1) * 128],
                            kbuf[bi % 2][:, kc * 1024 + h * 512:kc * 1024 + h * 512 + w_h],
                            start=(kc == 0), stop=(kc == NCH - 1))
                ins.then_inc(tQK, 1)
                t.wait_ge(aExp, 2 * u + 2)
                if u > 0:
                    t.wait_ge(vPTe, u)
                nsub = W // 128
                for sub in range(nsub):
                    ins = t.transpose(p_pt[:, sub * 128:(sub + 1) * 128],
                                      P_sb[:, sub * 128:(sub + 1) * 128], idb[:])
                ins.then_inc(tPT, 1)
                t.wait_ge(vPTe, u + 1)
                if u > 0:
                    t.wait_ge(vAcc, u)
                for sub in range(nsub):
                    for h in range(2):
                        ins = t.matmul(p_pv[:, h * 512:(h + 1) * 512],
                                       pt_sb[:, sub * 128:(sub + 1) * 128],
                                       vbuf[bi % 2][:, sub * 1024 + h * 512:sub * 1024 + (h + 1) * 512],
                                       start=(sub == 0), stop=(sub == nsub - 1))
                ins.then_inc(tPV, 1)

    _CACHE["l1b"] = nc
    return nc



def build_l3a():
    if "l3a" in _CACHE:
        return _CACHE["l3a"]
    nc = bass.Bass("TRN2", target_bir_lowering=False)
    stream = nc.dram_tensor("stream", [OWN, TILE, D], F32, kind="ExternalInput").ap()
    prow = nc.dram_tensor("prow", [OWN, D], F32, kind="ExternalInput").ap()
    wkT = nc.dram_tensor("wkT", [D, D], F32, kind="ExternalInput").ap()
    wvT = nc.dram_tensor("wvT", [D, D], F32, kind="ExternalInput").ap()
    wrT = nc.dram_tensor("wrT", [D, D], F32, kind="ExternalInput").ap()
    tmkf = nc.dram_tensor("tmkf", [128, D], F32, kind="ExternalInput").ap()
    tmvf = nc.dram_tensor("tmvf", [128, D], F32, kind="ExternalInput").ap()
    tmrf = nc.dram_tensor("tmrf", [128, D], F32, kind="ExternalInput").ap()
    td_col = nc.dram_tensor("td_col", [128, 8], F32, kind="ExternalInput").ap()
    identb = nc.dram_tensor("identb", [128, 128], BF16, kind="ExternalInput").ap()
    identf = nc.dram_tensor("identf", [128, 128], F32, kind="ExternalInput").ap()
    constv = nc.dram_tensor("constv", [128, 2], F32, kind="ExternalInput").ap()
    z_sp = nc.dram_tensor("z_sp", [OWN, 128, D], F32, kind="ExternalOutput").ap()
    t1_sp = nc.dram_tensor("t1_sp", [OWN, 128, D], F32, kind="ExternalOutput").ap()
    r_sp = nc.dram_tensor("r_sp", [OWN, 128, D], BF16, kind="ExternalOutput").ap()
    tot_o = nc.dram_tensor("tot_o", [OWN, 2, 128, 8], F32, kind="ExternalOutput").ap()

    with ExitStack() as ctx:
        def sb(name, shape, dt):
            return ctx.enter_context(nc.sbuf_tensor(name, shape, dt))

        def psr(name, shape, dt):
            return ctx.enter_context(nc.psum_tensor(name, shape, dt))

        def sem(name):
            return ctx.enter_context(nc.semaphore(name))

        xbuf = [sb("xbuf0", [128, D], F32), sb("xbuf1", [128, D], F32)]
        xb4 = xbuf + [sb("xbuf2", [128, D], F32), sb("xbuf3", [128, D], F32)]
        sqscr = sb("sqscr", [128, D], F32)
        xln = sb("xln", [128, D], BF16)
        fmx = sb("fmx", [128, D], BF16)
        fmp = sb("fmp", [128, D], BF16)
        dmix = sb("dmix", [128, D], BF16)
        xkf = sb("xkf", [128, D], BF16)
        xvf = sb("xvf", [128, D], BF16)
        xrf = sb("xrf", [128, D], BF16)
        ktm = sb("ktm", [128, D], F32)
        vtm_ = sb("vtm_", [128, D], F32)
        kfm = sb("kfm", [128, D], F32)
        vfm = sb("vfm", [128, D], F32)
        e1t = sb("e1t", [128, D], F32)
        rtm = sb("rtm", [128, D], BF16)
        t1t = [sb("t1t0", [128, D], F32), sb("t1t1", [128, D], F32)]
        zt = [sb("zt0", [128, D], F32), sb("zt1", [128, D], F32)]
        rfm = [sb("rfm0", [128, D], BF16), sb("rfm1", [128, D], BF16)]
        Sl = sb("Sl", [128, D], F32)
        Tl = sb("Tl", [128, D], F32)
        tots = [sb("tots0", [128, 16], F32), sb("tots1", [128, 16], F32)]
        wk_s = sb("wk_s", [128, NCH * D], BF16)
        wv_s = sb("wv_s", [128, NCH * D], BF16)
        wr_s = sb("wr_s", [128, NCH * D], BF16)
        tmk_s = sb("tmk_s", [128, D], BF16)
        tmv_s = sb("tmv_s", [128, D], BF16)
        tmr_s = sb("tmr_s", [128, D], BF16)
        idb = sb("idb", [128, 128], BF16)
        idf = sb("idf", [128, 128], F32)
        prraw = sb("prraw", [128, D], F32)
        prln = sb("prln", [128, D], BF16)
        prfm = sb("prfm", [128, NCH * OWN], BF16)
        EWf = sb("EWf", [128, D], F32)
        onef = sb("onef", [128, 128], F32)
        wcol = sb("wcol", [128, 8], F32)
        ewcol = sb("ewcol", [128, 8], F32)
        stat = sb("stat", [128, 32], F32)
        s_cst = sb("s_cst", [128, 2], F32)
        nc.const_aps.register(F32, EPS, s_cst[:, 0:1])
        nc.const_aps.register(F32, 1.0, s_cst[:, 1:2])
        SUM, M, Q, VAR, SSQ, MSQ, LNV, RSTD, NB = range(9)

        def col(k):
            return stat[:, k:k + 1]

        p_tr = psr("p_tr", [128, 1024], BF16)
        p_k = psr("p_k", [128, 1024], F32)
        p_v = psr("p_v", [128, 1024], F32)
        p_r = psr("p_r", [128, 1024], F32)

        dIn = sem("dIn"); dPre = sem("dPre"); dW = sem("dW"); dSp = sem("dSp")
        aS = sem("aS"); vT = sem("vT"); aR = sem("aR"); vN = sem("vN"); aX = sem("aX")
        tT = sem("tT"); vEv = sem("vEv"); vM = sem("vM")
        tP1 = sem("tP1"); tP2 = sem("tP2"); tP3 = sem("tP3")
        vKC = sem("vKC"); vVC = sem("vVC"); tTK = sem("tTK"); tTV = sem("tTV")
        vKF = sem("vKF"); vVF = sem("vVF"); aT1 = sem("aT1"); vZ = sem("vZ")
        vScan = sem("vScan"); vTot = sem("vTot"); aE1 = sem("aE1"); aSig = sem("aSig")
        tTR = sem("tTR"); vRE = sem("vRE")
        vPre = sem("vPre"); tPre = sem("tPre"); vEW = sem("vEW")
        vC = sem("vC"); aC = sem("aC")

        block = ctx.enter_context(nc.Block())

        @block.sync
        def _(sync):
            sync.dma_start(out=idb[:], in_=identb).then_inc(dPre, 16)
            sync.dma_start(out=idf[:], in_=identf).then_inc(dPre, 16)
            sync.dma_start(out=s_cst[:], in_=constv).then_inc(dPre, 16)
            sync.dma_start(out=prraw[0:OWN, :], in_=prow).then_inc(dPre, 16)
            sync.dma_start(out=wcol[:], in_=td_col).then_inc(dPre, 16)
            for i in range(OWN):
                if i >= 2:
                    sync.wait_ge(vM, i - 1)
                sync.dma_start(out=xbuf[i % 2][:], in_=stream[i]).then_inc(dIn, 16)
                if i >= 1:
                    k = i - 1
                    sync.wait_ge(vZ, k + 1)
                    sync.dma_start(out=z_sp[k], in_=zt[k % 2][:]).then_inc(dSp, 16)
                    sync.dma_start(out=t1_sp[k], in_=t1t[k % 2][:]).then_inc(dSp, 16)
                    sync.wait_ge(vRE, k + 1)
                    sync.dma_start(out=r_sp[k], in_=rfm[k % 2][:]).then_inc(dSp, 16)
                    sync.wait_ge(vTot, k + 1)
                    sync.dma_start(out=tot_o[k].rearrange("q p c -> p q c"),
                                   in_=tots[k % 2][:].rearrange("p (q c) -> p q c", q=2)).then_inc(dSp, 16)
            k = OWN - 1
            sync.wait_ge(vZ, k + 1)
            sync.dma_start(out=z_sp[k], in_=zt[k % 2][:]).then_inc(dSp, 16)
            sync.dma_start(out=t1_sp[k], in_=t1t[k % 2][:]).then_inc(dSp, 16)
            sync.wait_ge(vRE, k + 1)
            sync.dma_start(out=r_sp[k], in_=rfm[k % 2][:]).then_inc(dSp, 16)
            sync.wait_ge(vTot, k + 1)
            sync.dma_start(out=tot_o[k].rearrange("q p c -> p q c"),
                           in_=tots[k % 2][:].rearrange("p (q c) -> p q c", q=2)).then_inc(dSp, 16)
            sync.wait_ge(dSp, 64 * OWN)

        @block.gpsimd
        def _(g):
            for w_dram, w_sb in ((wkT, wk_s), (wvT, wv_s), (wrT, wr_s)):
                g.dma_start(out=w_sb[:].rearrange("p (c f) -> p c f", c=NCH),
                            in_=w_dram.rearrange("(c p) f -> p c f", p=128)).then_inc(dW, 16)
            g.dma_start(out=tmk_s[:], in_=tmkf).then_inc(dW, 16)
            g.dma_start(out=tmv_s[:], in_=tmvf).then_inc(dW, 16)
            g.dma_start(out=tmr_s[:], in_=tmrf).then_inc(dW, 16)
            g.memset(onef[:], 1.0).then_inc(dW, 16)

        @block.vector
        def _(v):
            _vn = [0]

            def ch(ins, cross=None, tgt=None):
                if cross is None:
                    _vn[0] += 1
                    ins.then_inc(vC, 1)
                    v.wait_ge(vC, _vn[0])
                else:
                    ins.then_inc(cross, 1)
                    v.wait_ge(cross, tgt)
                return ins

            v.wait_ge(dPre, 80)
            v.wait_ge(dW, 112)
            v.wait_ge(aC, 2)
            for cc in range(NCH):
                ch(v.tensor_scalar_mul(EWf[:, cc * 128:(cc + 1) * 128], onef[:],
                                       ewcol[:, cc:cc + 1]))
            v.nop().then_inc(vEW, 1)
            pr = prraw[0:OWN, :]
            st = stat[0:OWN, :]
            ch(v.reduce_sum(st[:, 0:1], pr, axis=AX.X))
            ch(v.tensor_scalar_mul(st[:, 1:2], st[:, 0:1], 1.0 / D))
            v.wait_ge(aS, 1)
            ch(v.tensor_scalar_mul(st[:, 2:3], st[:, 4:5], 1.0 / D))
            ch(v.tensor_mul(st[:, 5:6], st[:, 1:2], st[:, 1:2]))
            ch(v.tensor_sub(st[:, 3:4], st[:, 2:3], st[:, 5:6]), vT, 1)
            v.wait_ge(aR, 1)
            ch(v.tensor_scalar(st[:, 6:7], st[:, 1:2], st[:, 7:8], -1.0,
                               ALU.mult, ALU.mult), vN, 1)
            v.wait_ge(tPre, 1)
            ch(v.tensor_copy(prfm[:], p_tr[:, 0:NCH * OWN]), vPre, 1)

            for i in range(OWN):
                n = i + 1
                v.wait_ge(dIn, 16 * n)
                ch(v.reduce_sum(col(SUM), xbuf[i % 2][:], axis=AX.X))
                ch(v.tensor_scalar_mul(col(M), col(SUM), 1.0 / D))
                v.wait_ge(aS, n + 1)
                ch(v.tensor_scalar_mul(col(Q), col(SSQ), 1.0 / D))
                ch(v.tensor_mul(col(MSQ), col(M), col(M)))
                ch(v.tensor_sub(col(VAR), col(Q), col(MSQ)), vT, n + 1)
                v.wait_ge(aR, n + 1)
                ch(v.tensor_scalar(col(NB), col(M), col(RSTD), -1.0,
                                   ALU.mult, ALU.mult), vN, n + 1)
                v.wait_ge(tT, n)
                if i > 0:
                    v.wait_ge(tP3, i)
                ch(v.tensor_copy(fmx[:], p_tr[:]), vEv, n)
                v.wait_ge(vPre, 1)
                for cc in range(NCH):
                    ch(v.tensor_copy(fmp[:, cc * 128 + 1:cc * 128 + 128],
                                     fmx[:, cc * 128:cc * 128 + 127]))
                    ch(v.tensor_copy(fmp[:, cc * 128:cc * 128 + 1],
                                     prfm[:, cc * OWN + i:cc * OWN + i + 1]))
                ch(v.tensor_sub(dmix[:], fmx[:], fmp[:]))
                ch(v.tensor_mul(xkf[:], dmix[:], tmk_s[:]))
                ch(v.tensor_add(xkf[:], xkf[:], fmp[:]))
                ch(v.tensor_mul(xvf[:], dmix[:], tmv_s[:]))
                ch(v.tensor_add(xvf[:], xvf[:], fmp[:]))
                ch(v.tensor_mul(xrf[:], dmix[:], tmr_s[:]))
                ch(v.tensor_add(xrf[:], xrf[:], fmp[:]), vM, n)
                v.wait_ge(tP1, n)
                ch(v.tensor_copy(ktm[:], p_k[:]), vKC, n)
                v.wait_ge(tP2, n)
                ch(v.tensor_copy(vtm_[:], p_v[:]), vVC, n)
                v.wait_ge(tTK, n)
                ch(v.tensor_copy(kfm[:], p_k[:]), vKF, n)
                v.wait_ge(tTV, n)
                ch(v.tensor_copy(vfm[:], p_v[:]), vVF, n)
                v.wait_ge(aT1, n)
                if i >= 2:
                    v.wait_ge(dSp, 64 * (i - 1))
                ch(v.tensor_mul(zt[i % 2][:], t1t[i % 2][:], vfm[:]), vZ, n)
                v.wait_ge(vEW, 1)
                for cc in range(NCH):
                    ch(v.tensor_tensor_scan(Sl[:, cc * 128:(cc + 1) * 128],
                                            EWf[:, cc * 128:(cc + 1) * 128],
                                            zt[i % 2][:, cc * 128:(cc + 1) * 128],
                                            0.0, ALU.mult, ALU.add))
                    ch(v.tensor_tensor_scan(Tl[:, cc * 128:(cc + 1) * 128],
                                            EWf[:, cc * 128:(cc + 1) * 128],
                                            t1t[i % 2][:, cc * 128:(cc + 1) * 128],
                                            0.0, ALU.mult, ALU.add))
                v.nop().then_inc(vScan, 1)
                v.wait_ge(vScan, n)
                ch(v.tensor_copy(tots[i % 2][:, 0:8],
                                 Sl[:].rearrange("p (c t) -> p c t", c=NCH)[:, :, 127]))
                ch(v.tensor_copy(tots[i % 2][:, 8:16],
                                 Tl[:].rearrange("p (c t) -> p c t", c=NCH)[:, :, 127]), vTot, n)
                v.wait_ge(tTR, n)
                ch(v.tensor_copy(rfm[i % 2][:], p_tr[:]), vRE, n)

        @block.scalar
        def _(a):
            _an = [0]

            def ca(ins, cross=None, tgt=None):
                if cross is None:
                    _an[0] += 1
                    ins.then_inc(aC, 1)
                    a.wait_ge(aC, _an[0])
                else:
                    ins.then_inc(cross, 1)
                    a.wait_ge(cross, tgt)
                return ins

            a.wait_ge(dPre, 80)
            ca(a.activation(ewcol[:], wcol[:], AF.Exp))
            ca(a.activation(ewcol[:], ewcol[:], AF.Exp, scale=-1.0))
            pr = prraw[0:OWN, :]
            st = stat[0:OWN, :]
            ca(a.activation(sqscr[0:OWN, :], pr, AF.Square, accum_out=st[:, 4:5]), aS, 1)
            a.wait_ge(vT, 1)
            ca(a.activation(st[:, 8:9], st[:, 3:4], AF.Ln, bias=EPS))
            ca(a.activation(st[:, 7:8], st[:, 8:9], AF.Exp, scale=-0.5), aR, 1)
            a.wait_ge(vN, 1)
            ca(a.activation(prln[0:OWN, :], pr, AF.Identity, scale=st[:, 7:8],
                            bias=st[:, 6:7]), aX, 1)

            for i in range(OWN):
                n = i + 1
                a.wait_ge(dIn, 16 * n)
                ca(a.activation(sqscr[:], xbuf[i % 2][:], AF.Square,
                                accum_out=col(SSQ)), aS, n + 1)
                a.wait_ge(vT, n + 1)
                ca(a.activation(col(LNV), col(VAR), AF.Ln, bias=EPS))
                ca(a.activation(col(RSTD), col(LNV), AF.Exp, scale=-0.5), aR, n + 1)
                a.wait_ge(vN, n + 1)
                ca(a.activation(xln[:], xbuf[i % 2][:], AF.Identity, scale=col(RSTD),
                                bias=col(NB)), aX, n + 1)
                a.wait_ge(vKF, n)
                if i >= 2:
                    a.wait_ge(dSp, 64 * (i - 1))
                ca(a.activation(t1t[i % 2][:], kfm[:], AF.Exp), aT1, n)
                a.wait_ge(tP3, n)
                ca(a.activation(e1t[:], p_r[:], AF.Exp, scale=-1.0), aE1, n)
                ca(a.activation(rtm[:], e1t[:], AF.Ln, bias=1.0))
                ca(a.activation(rtm[:], rtm[:], AF.Exp, scale=-1.0), aSig, n)

        @block.tensor
        def _(t):
            t.wait_ge(dPre, 80)
            t.wait_ge(aX, 1)
            for cc in range(NCH):
                ins = t.transpose(p_tr[:, cc * OWN:(cc + 1) * OWN],
                                  prln[0:OWN, cc * 128:(cc + 1) * 128],
                                  idb[0:OWN, 0:OWN])
            ins.then_inc(tPre, 1)
            t.wait_ge(dW, 112)
            for i in range(OWN):
                n = i + 1
                t.wait_ge(aX, n + 1)
                t.wait_ge(vPre, 1)
                if i > 0:
                    t.wait_ge(vRE, i)
                for cc in range(NCH):
                    ins = t.transpose(p_tr[:, cc * 128:(cc + 1) * 128],
                                      xln[:, cc * 128:(cc + 1) * 128], idb[:])
                ins.then_inc(tT, 1)
                t.wait_ge(vM, n)
                if i > 0:
                    t.wait_ge(vKF, i)
                for h in range(2):
                    for kc in range(NCH):
                        ins = t.matmul(p_k[:, h * 512:(h + 1) * 512],
                                       xkf[:, kc * 128:(kc + 1) * 128],
                                       wk_s[:, kc * D + h * 512:kc * D + (h + 1) * 512],
                                       start=(kc == 0), stop=(kc == NCH - 1))
                ins.then_inc(tP1, 1)
                if i > 0:
                    t.wait_ge(vVF, i)
                for h in range(2):
                    for kc in range(NCH):
                        ins = t.matmul(p_v[:, h * 512:(h + 1) * 512],
                                       xvf[:, kc * 128:(kc + 1) * 128],
                                       wv_s[:, kc * D + h * 512:kc * D + (h + 1) * 512],
                                       start=(kc == 0), stop=(kc == NCH - 1))
                ins.then_inc(tP2, 1)
                if i > 0:
                    t.wait_ge(aE1, i)
                for h in range(2):
                    for kc in range(NCH):
                        ins = t.matmul(p_r[:, h * 512:(h + 1) * 512],
                                       xrf[:, kc * 128:(kc + 1) * 128],
                                       wr_s[:, kc * D + h * 512:kc * D + (h + 1) * 512],
                                       start=(kc == 0), stop=(kc == NCH - 1))
                ins.then_inc(tP3, 1)
                t.wait_ge(vKC, n)
                for cc in range(NCH):
                    ins = t.transpose(p_k[:, cc * 128:(cc + 1) * 128],
                                      ktm[:, cc * 128:(cc + 1) * 128], idf[:])
                ins.then_inc(tTK, 1)
                t.wait_ge(vVC, n)
                for cc in range(NCH):
                    ins = t.transpose(p_v[:, cc * 128:(cc + 1) * 128],
                                      vtm_[:, cc * 128:(cc + 1) * 128], idf[:])
                ins.then_inc(tTV, 1)
                t.wait_ge(aSig, n)
                t.wait_ge(vEv, n)
                for cc in range(NCH):
                    ins = t.transpose(p_tr[:, cc * 128:(cc + 1) * 128],
                                      rtm[:, cc * 128:(cc + 1) * 128], idb[:])
                ins.then_inc(tTR, 1)

    _CACHE["l3a"] = nc
    return nc


def build_l3c(debug=False):
    key = "l3c_dbg" if debug else "l3c"
    if key in _CACHE:
        return _CACHE[key]
    nc = bass.Bass("TRN2", target_bir_lowering=False)
    stream = nc.dram_tensor("stream", [OWN, TILE, D], F32, kind="ExternalInput").ap()
    z_sp = nc.dram_tensor("z_sp", [OWN, 128, D], F32, kind="ExternalInput").ap()
    t1_sp = nc.dram_tensor("t1_sp", [OWN, 128, D], F32, kind="ExternalInput").ap()
    r_sp = nc.dram_tensor("r_sp", [OWN, 128, D], BF16, kind="ExternalInput").ap()
    tot_sh = nc.dram_tensor("tot_sh", [128, 2, NCH, NT], F32, kind="ExternalInput").ap()
    td_col = nc.dram_tensor("td_col", [128, 8], F32, kind="ExternalInput").ap()
    tf_col = nc.dram_tensor("tf_col", [128, 8], F32, kind="ExternalInput").ap()
    woT = nc.dram_tensor("woT", [D, D], F32, kind="ExternalInput").ap()
    constv = nc.dram_tensor("constv", [128, 2], F32, kind="ExternalInput").ap()
    ostream = nc.dram_tensor("ostream", [OWN, TILE, D], F32, kind="ExternalOutput").ap()
    dbg = {}
    if debug:
        for nm, shp in (("d_carr", [128, 2 * NCH * NT]), ("d_sf", [128, D]),
                        ("d_tf", [128, D]), ("d_num", [128, D]), ("d_den", [128, D]),
                        ("d_ivd", [128, D]), ("d_yt", [128, D]), ("d_ot", [128, D]),
                        ("d_cols", [128, 64]), ("d_totsb", [128, 2 * NCH * NT])):
            dbg[nm] = nc.dram_tensor(nm, shp, F32, kind="ExternalOutput").ap()

    with ExitStack() as ctx:
        def sb(name, shape, dt):
            return ctx.enter_context(nc.sbuf_tensor(name, shape, dt))

        def psr(name, shape, dt):
            return ctx.enter_context(nc.psum_tensor(name, shape, dt))

        def sem(name):
            return ctx.enter_context(nc.semaphore(name))

        xbuf = [sb("xbuf0", [128, D], F32), sb("xbuf1", [128, D], F32)]
        xb4 = xbuf + [sb("xbuf2", [128, D], F32), sb("xbuf3", [128, D], F32)]
        ztb = [sb("ztb0", [128, D], F32), sb("ztb1", [128, D], F32)]
        t1b = [sb("t1b0", [128, D], F32), sb("t1b1", [128, D], F32)]
        rfb = [sb("rfb0", [128, D], F32), sb("rfb1", [128, D], F32)]
        obuf = [sb("obuf0", [128, D], F32), sb("obuf1", [128, D], F32)]
        Sf = sb("Sf", [128, D], F32)
        Tf = sb("Tf", [128, D], F32)
        u1 = sb("u1", [128, D], F32)
        numt = sb("numt", [128, D], F32)
        dent = sb("dent", [128, D], F32)
        ivdt = sb("ivdt", [128, D], F32)
        yt = sb("yt", [128, D], F32)
        ryb = sb("ryb", [128, D], BF16)
        ot = sb("ot", [128, D], F32)
        lnout = sb("lnout", [128, D], F32)
        sqscr = sb("sqscr", [128, D], F32)
        totsb = sb("totsb", [128, 2 * NCH * NT], F32)
        Carr = sb("Carr", [128, 2 * NCH * NT], F32)
        EWf = sb("EWf", [128, D], F32)
        EL64 = sb("EL64", [128, NCH * NT], F32)
        onef = sb("onef", [128, 128], F32)
        wcol = sb("wcol", [128, 8], F32)
        ewcol = sb("ewcol", [128, 8], F32)
        elcol = sb("elcol", [128, 8], F32)
        eucol = sb("eucol", [128, 8], F32)
        gcol = sb("gcol", [128, 8], F32)
        ewinv = sb("ewinv", [128, 8], F32)
        wo_s = sb("wo_s", [128, NCH * D], BF16)
        stat = sb("stat", [128, 16], F32)
        s_cst = sb("s_cst", [128, 2], F32)
        nc.const_aps.register(F32, EPS, s_cst[:, 0:1])
        SUM, M, Q, VAR, SSQ, MSQ, LNV, RSTD, NB = range(9)

        def col(k):
            return stat[:, k:k + 1]

        p_o = psr("p_o", [128, 1024], F32)

        dIn = sem("dIn"); dPre = sem("dPre"); dW = sem("dW"); dOut = sem("dOut")
        vPro = sem("vPro"); aPro = sem("aPro")
        vScan = sem("vScan"); vNum = sem("vNum"); aDiv = sem("aDiv"); vRy = sem("vRy")
        tWo = sem("tWo"); vOt = sem("vOt")
        aS2 = sem("aS2"); vT2 = sem("vT2"); aR2 = sem("aR2"); vN2 = sem("vN2")
        aX2 = sem("aX2"); vFin = sem("vFin")
        vC = sem("vC"); aC = sem("aC")

        block = ctx.enter_context(nc.Block())

        @block.sync
        def _(sync):
            sync.dma_start(out=s_cst[:], in_=constv).then_inc(dPre, 16)
            sync.dma_start(out=wcol[:], in_=td_col).then_inc(dPre, 16)
            sync.dma_start(out=eucol[:], in_=tf_col).then_inc(dPre, 16)
            sync.dma_start(out=totsb[:].rearrange("p (q c g) -> p q c g", q=2, c=NCH),
                           in_=tot_sh.rearrange("p q c g -> p q c g")).then_inc(dPre, 16)
            for i in range(OWN):
                if i >= 2:
                    sync.wait_ge(vFin, i - 1)
                sync.dma_start(out=xbuf[i % 2][:], in_=stream[i]).then_inc(dIn, 16)
                sync.dma_start(out=ztb[i % 2][:], in_=z_sp[i]).then_inc(dIn, 16)
                sync.dma_start(out=t1b[i % 2][:], in_=t1_sp[i]).then_inc(dIn, 16)
            sync.wait_ge(dOut, 16 * OWN)
            if debug:
                for nm, srcap in (("d_carr", Carr[:]), ("d_sf", Sf[:]), ("d_tf", Tf[:]),
                                  ("d_num", numt[:]), ("d_den", dent[:]),
                                  ("d_ivd", ivdt[:]), ("d_yt", yt[:]), ("d_ot", ot[:]),
                                  ("d_totsb", totsb[:])):
                    sync.dma_start(out=dbg[nm], in_=srcap).then_inc(dPre, 16)
                sync.dma_start(out=dbg["d_cols"][:, 0:8], in_=ewcol[:]).then_inc(dPre, 16)
                sync.dma_start(out=dbg["d_cols"][:, 8:16], in_=elcol[:]).then_inc(dPre, 16)
                sync.dma_start(out=dbg["d_cols"][:, 16:24], in_=eucol[:]).then_inc(dPre, 16)
                sync.dma_start(out=dbg["d_cols"][:, 24:32], in_=gcol[:]).then_inc(dPre, 16)

        @block.gpsimd
        def _(g):
            g.dma_start(out=wo_s[:].rearrange("p (c f) -> p c f", c=NCH),
                        in_=woT.rearrange("(c p) f -> p c f", p=128)).then_inc(dW, 16)
            g.memset(onef[:], 1.0).then_inc(dW, 16)
            for i in range(OWN + 1):
                if i < OWN:
                    if i >= 2:
                        g.wait_ge(vRy, i - 1)
                    g.dma_start(out=rfb[i % 2][:], in_=r_sp[i]).then_inc(dIn, 16)
                if i >= 1:
                    g.wait_ge(vFin, i)
                    g.dma_start(out=ostream[i - 1], in_=obuf[(i - 1) % 2][:]).then_inc(dOut, 16)
            g.wait_ge(dOut, 16 * OWN)

        @block.vector
        def _(v):
            _vn = [0]

            def ch(ins, cross=None, tgt=None):
                if cross is None:
                    _vn[0] += 1
                    ins.then_inc(vC, 1)
                    v.wait_ge(vC, _vn[0])
                else:
                    ins.then_inc(cross, 1)
                    v.wait_ge(cross, tgt)
                return ins

            v.wait_ge(dPre, 64)
            v.wait_ge(dW, 32)
            v.wait_ge(aPro, 1)
            ch(v.tensor_mul(gcol[:], eucol[:], ewinv[:]))
            for cc in range(NCH):
                ch(v.tensor_scalar_mul(EWf[:, cc * 128:(cc + 1) * 128], onef[:],
                                       ewcol[:, cc:cc + 1]))
                ch(v.tensor_scalar_mul(EL64[:, cc * NT:(cc + 1) * NT], onef[:, 0:NT],
                                       elcol[:, cc:cc + 1]))
            for q in range(2):
                for cc in range(NCH):
                    off = (q * NCH + cc) * NT
                    ch(v.tensor_tensor_scan(Carr[:, off:off + NT],
                                            EL64[:, cc * NT:(cc + 1) * NT],
                                            totsb[:, off:off + NT],
                                            0.0, ALU.mult, ALU.add))
            v.nop().then_inc(vPro, 1)

            for i in range(OWN):
                n = i + 1
                v.wait_ge(dIn, 64 * n)
                for cc in range(NCH):
                    offS = (0 * NCH + cc) * NT
                    offT = (1 * NCH + cc) * NT
                    ch(v.tensor_tensor_scan(Sf[:, cc * 128:(cc + 1) * 128],
                                            EWf[:, cc * 128:(cc + 1) * 128],
                                            ztb[i % 2][:, cc * 128:(cc + 1) * 128],
                                            Carr[:, offS + 2 * i:offS + 2 * i + 1],
                                            ALU.mult, ALU.add))
                    ch(v.tensor_tensor_scan(Tf[:, cc * 128:(cc + 1) * 128],
                                            EWf[:, cc * 128:(cc + 1) * 128],
                                            t1b[i % 2][:, cc * 128:(cc + 1) * 128],
                                            Carr[:, offT + 2 * i:offT + 2 * i + 1],
                                            ALU.mult, ALU.add))
                v.nop().then_inc(vScan, 1)
                ch(v.tensor_sub(u1[:], Sf[:], ztb[i % 2][:]))
                for cc in range(NCH):
                    ch(v.scalar_tensor_tensor(numt[:, cc * 128:(cc + 1) * 128],
                                              u1[:, cc * 128:(cc + 1) * 128],
                                              gcol[:, cc:cc + 1],
                                              ztb[i % 2][:, cc * 128:(cc + 1) * 128],
                                              ALU.mult, ALU.add))
                ch(v.tensor_sub(u1[:], Tf[:], t1b[i % 2][:]))
                for cc in range(NCH):
                    ins = v.scalar_tensor_tensor(dent[:, cc * 128:(cc + 1) * 128],
                                                 u1[:, cc * 128:(cc + 1) * 128],
                                                 gcol[:, cc:cc + 1],
                                                 t1b[i % 2][:, cc * 128:(cc + 1) * 128],
                                                 ALU.mult, ALU.add)
                    if cc == NCH - 1:
                        ch(ins, vNum, n)
                    else:
                        ch(ins)
                v.wait_ge(aDiv, n)
                ch(v.tensor_mul(yt[:], numt[:], ivdt[:]))
                ch(v.tensor_mul(ryb[:], yt[:], rfb[i % 2][:]), vRy, n)
                v.wait_ge(tWo, n)
                ch(v.tensor_copy(ot[:], p_o[:]), vOt, n)
                ch(v.reduce_sum(col(SUM), ot[:], axis=AX.X))
                ch(v.tensor_scalar_mul(col(M), col(SUM), 1.0 / D))
                v.wait_ge(aS2, n)
                ch(v.tensor_scalar_mul(col(Q), col(SSQ), 1.0 / D))
                ch(v.tensor_mul(col(MSQ), col(M), col(M)))
                ch(v.tensor_sub(col(VAR), col(Q), col(MSQ)), vT2, n)
                v.wait_ge(aR2, n)
                ch(v.tensor_scalar(col(NB), col(M), col(RSTD), -1.0,
                                   ALU.mult, ALU.mult), vN2, n)
                v.wait_ge(aX2, n)
                if i >= 2:
                    v.wait_ge(dOut, 16 * (i - 1))
                ch(v.tensor_add(obuf[i % 2][:], xbuf[i % 2][:], lnout[:]), vFin, n)

        @block.scalar
        def _(a):
            _an = [0]

            def ca(ins, cross=None, tgt=None):
                if cross is None:
                    _an[0] += 1
                    ins.then_inc(aC, 1)
                    a.wait_ge(aC, _an[0])
                else:
                    ins.then_inc(cross, 1)
                    a.wait_ge(cross, tgt)
                return ins

            a.wait_ge(dPre, 64)
            ca(a.activation(ewcol[:], wcol[:], AF.Exp))
            ca(a.activation(elcol[:], ewcol[:], AF.Exp, scale=-128.0))
            ca(a.activation(ewinv[:], ewcol[:], AF.Exp))
            ca(a.activation(ewcol[:], ewcol[:], AF.Exp, scale=-1.0))
            ca(a.activation(eucol[:], eucol[:], AF.Exp), aPro, 1)

            for i in range(OWN):
                n = i + 1
                a.wait_ge(vNum, n)
                ca(a.activation(ivdt[:], dent[:], AF.Ln))
                ca(a.activation(ivdt[:], ivdt[:], AF.Exp, scale=-1.0), aDiv, n)
                a.wait_ge(vOt, n)
                ca(a.activation(sqscr[:], ot[:], AF.Square, accum_out=col(SSQ)), aS2, n)
                a.wait_ge(vT2, n)
                ca(a.activation(col(LNV), col(VAR), AF.Ln, bias=EPS))
                ca(a.activation(col(RSTD), col(LNV), AF.Exp, scale=-0.5), aR2, n)
                a.wait_ge(vN2, n)
                ca(a.activation(lnout[:], ot[:], AF.Identity, scale=col(RSTD),
                                bias=col(NB)), aX2, n)

        @block.tensor
        def _(t):
            t.wait_ge(dW, 32)
            for i in range(OWN):
                n = i + 1
                t.wait_ge(vRy, n)
                if i > 0:
                    t.wait_ge(vOt, i)
                for h in range(2):
                    for kc in range(NCH):
                        ins = t.matmul(p_o[:, h * 512:(h + 1) * 512],
                                       ryb[:, kc * 128:(kc + 1) * 128],
                                       wo_s[:, kc * D + h * 512:kc * D + (h + 1) * 512],
                                       start=(kc == 0), stop=(kc == NCH - 1))
                ins.then_inc(tWo, 1)

    _CACHE[key] = nc
    return nc



U32 = mybir.dt.uint32


def _fused_schedule():
    visits = []
    blocks = []
    fins = []
    for p in range(4):
        slots = range(8 * p, 8 * p + 8)
        nkb = 2 * p + 2
        for kb in range(nkb):
            act = [j for j in slots if 2 * j + 2 > 8 * kb]
            blocks.append(dict(p=p, kb=kb, first=len(visits), nact=len(act)))
            for j in act:
                W = min(128 * (2 * j + 2) - 1024 * kb, 1024)
                visits.append(dict(p=p, kb=kb, j=j, W=W, mask=(kb == j // 4)))
            for j in act:
                if (2 * j + 2) <= 8 * (kb + 1):
                    fins.append(dict(j=j, after=len(visits) - 1, p=p))
    for f_i, f in enumerate(fins):
        f["fi"] = f_i
    return visits, blocks, fins


def build_fused(debug=False):
    key = "fused_dbg" if debug else "fused"
    if key in _CACHE:
        return _CACHE[key]
    visits, blocks, fins = _fused_schedule()
    nc = bass.Bass("TRN2", target_bir_lowering=False)
    xfull = nc.dram_tensor("xfull", [NT, TILE, D], F32, kind="ExternalInput").ap()
    x_own = nc.dram_tensor("x_own", [OWN, TILE, D], F32, kind="ExternalInput").ap()
    wkvT = nc.dram_tensor("wkvT", [D, 2 * D], F32, kind="ExternalInput").ap()
    wT2 = nc.dram_tensor("wT2", [3, D, D], F32, kind="ExternalInput").ap()
    wT3 = nc.dram_tensor("wT3", [3, D, D], F32, kind="ExternalInput").ap()
    woT3 = nc.dram_tensor("woT3", [D, D], F32, kind="ExternalInput").ap()
    wT4 = nc.dram_tensor("wT4", [3, D, D], F32, kind="ExternalInput").ap()
    mu2f = nc.dram_tensor("mu2f", [2, 128, D], F32, kind="ExternalInput").ap()
    tm3f = nc.dram_tensor("tm3f", [3, 128, D], F32, kind="ExternalInput").ap()
    mu4f = nc.dram_tensor("mu4f", [2, 128, D], F32, kind="ExternalInput").ap()
    td_colD = nc.dram_tensor("td_col", [128, 8], F32, kind="ExternalInput").ap()
    tf_colD = nc.dram_tensor("tf_col", [128, 8], F32, kind="ExternalInput").ap()
    maskp = nc.dram_tensor("maskp", [2, 128, 128], F32, kind="ExternalInput").ap()
    identb = nc.dram_tensor("identb", [128, 128], BF16, kind="ExternalInput").ap()
    identf = nc.dram_tensor("identf", [128, 128], F32, kind="ExternalInput").ap()
    constv = nc.dram_tensor("constv", [128, 2], F32, kind="ExternalInput").ap()
    idxb_d = nc.dram_tensor("idxb", [32, 1], U32, kind="ExternalInput").ap()
    idxs_d = nc.dram_tensor("idxs", [128, 8], U32, kind="ExternalInput").ap()
    zpm_d = nc.dram_tensor("zpm", [32, 1], F32, kind="ExternalInput").ap()
    tzm_d = nc.dram_tensor("tzm", [128, 2], F32, kind="ExternalInput").ap()
    ostream = nc.dram_tensor("ostream", [OWN, TILE, D], F32, kind="ExternalOutput").ap()
    skind0 = "ExternalOutput" if debug else "Internal"
    kf_scr = nc.dram_tensor("kf_scr", [NCH, 128, S], BF16, kind=skind0).ap()
    vf_scr = nc.dram_tensor("vf_scr", [S, D], BF16, kind=skind0).ap()
    z_scr = nc.dram_tensor("z_scr", [OWN, 128, D], F32, kind="Internal").ap()
    t1_scr = nc.dram_tensor("t1_scr", [OWN, 128, D], F32, kind="Internal").ap()
    r_scr = nc.dram_tensor("r_scr", [OWN, 128, D], BF16, kind="Internal").ap()
    skind = "ExternalOutput" if debug else "Internal"
    s1_scr = nc.dram_tensor("s1_scr", [OWN, TILE, D], F32, kind=skind).ap()
    s2_scr = nc.dram_tensor("s2_scr", [OWN, TILE, D], F32, kind=skind).ap()
    s3_scr = nc.dram_tensor("s3_scr", [OWN, TILE, D], F32, kind=skind).ap()
    bd_in = nc.dram_tensor("bd_in", [32, D], F32, kind="Internal")
    bd_out = nc.dram_tensor("bd_out", [256, D], F32, kind="Internal")
    tt_in = nc.dram_tensor("tt_in", [512, 128], F32, kind="Internal")
    tt_out = nc.dram_tensor("tt_out", [4096, 128], F32, kind="Internal")

    with ExitStack() as ctx:
        def sb(name, shape, dt):
            return ctx.enter_context(nc.sbuf_tensor(name, shape, dt))

        def psr(name, shape, dt):
            return ctx.enter_context(nc.psum_tensor(name, shape, dt))

        def sem(name):
            return ctx.enter_context(nc.semaphore(name))

        bigA = sb("bigA", [128, 16384], F32)
        bigB = sb("bigB", [128, 16384], BF16)
        wpool = sb("wpool", [128, 24576], BF16)
        xbuf = [sb("xbuf0", [128, D], F32), sb("xbuf1", [128, D], F32)]
        xb4 = xbuf + [sb("xbuf2", [128, D], F32), sb("xbuf3", [128, D], F32)]
        obuf = [sb("obuf0", [128, D], F32), sb("obuf1", [128, D], F32)]
        P_sb = sb("P_sb", [128, 1024], BF16)
        pt_sb = sb("pt_sb", [128, 1024], BF16)
        prraw = sb("prraw", [128, D], F32)
        prln = sb("prln", [128, D], BF16)
        prfm = sb("prfm", [128, NCH * OWN], BF16)
        totsb = sb("totsb", [128, 1024], F32)
        CarrS = sb("CarrS", [128, 1024], F32)
        EWf = sb("EWf", [128, D], F32)
        EL64 = sb("EL64", [128, NCH * NT], F32)
        onef = sb("onef", [128, 128], F32)
        flatp = sb("flatp", [128, 3072], BF16)
        statsP = sb("statsP", [128, 24], F32)
        stat = sb("stat", [128, 32], F32)
        negm = sb("negm", [128, 8], F32)
        lcol = sb("lcol", [128, 8], F32)
        alpha = sb("alpha", [128, 8], F32)
        rsum = sb("rsum", [128, 8], F32)
        tmpa = sb("tmpa", [128, 16], F32)
        wcol = sb("wcol", [128, 8], F32)
        ewcol = sb("ewcol", [128, 8], F32)
        elcol = sb("elcol", [128, 8], F32)
        ewinv = sb("ewinv", [128, 8], F32)
        eucol = sb("eucol", [128, 8], F32)
        gcol = sb("gcol", [128, 8], F32)
        msk = sb("msk", [128, 256], F32)
        idb = sb("idb", [128, 128], BF16)
        idf = sb("idf", [128, 128], F32)
        s_cst = sb("s_cst", [128, 2], F32)
        sb_ib = sb("sb_ib", [32, 1], U32)
        sb_is = sb("sb_is", [128, 8], U32)
        sb_zm = sb("sb_zm", [32, 1], F32)
        sb_tz = sb("sb_tz", [128, 2], F32)
        nc.const_aps.register(F32, EPS, s_cst[:, 0:1])
        nc.const_aps.register(F32, 1.0, s_cst[:, 1:2])
        SUM, M, Q, VAR, SSQ, MSQ, LNV, RSTD, NB = range(9)

        def col(k, st=0):
            return stat[:, st * 16 + k:st * 16 + k + 1]

        P1 = psr("P1", [128, 2048], F32)
        P2 = psr("P2", [128, 1024], F32)
        P3 = psr("P3", [128, 1024], BF16)
        P4 = psr("P4", [128, 1024], BF16)
        P1a = P1[:, 0:1024]
        P1b = P1[:, 1024:2048]

        PH = sem("PH"); dPre = sem("dPre")
        paIn = sem("paIn"); paS = sem("paS"); paT = sem("paT"); paR = sem("paR")
        paN = sem("paN"); paX = sem("paX"); paTT = sem("paTT"); paEv = sem("paEv")
        paKV = sem("paKV"); paKC = sem("paKC"); paVC = sem("paVC"); paT2 = sem("paT2")
        paKE = sem("paKE"); paSp = sem("paSp"); paW = sem("paW")
        pbQd = sem("pbQd"); pbQs = sem("pbQs"); pbQt = sem("pbQt"); pbQe = sem("pbQe")
        pbQr = sem("pbQr"); pbQn = sem("pbQn"); pbQx = sem("pbQx"); pbQT = sem("pbQT")
        pbKV = sem("pbKV"); pbQK = sem("pbQK"); pbSt = sem("pbSt")
        pbPT = sem("pbPT"); pbPe = sem("pbPe"); pbPV = sem("pbPV"); pbAc = sem("pbAc")
        pbMS = sem("pbMS"); pbFl = sem("pbFl"); pbOt = sem("pbOt"); pbVo = sem("pbVo")
        pbS2 = sem("pbS2"); pbT2 = sem("pbT2"); pbR2 = sem("pbR2"); pbN2 = sem("pbN2")
        pbX2 = sem("pbX2"); pbFn = sem("pbFn"); pbSo = sem("pbSo")
        exD = sem("exD"); exC = sem("exC"); exG = sem("exG")
        prS = sem("prS"); prT = sem("prT"); prR = sem("prR"); prN = sem("prN")
        prX = sem("prX"); prTT = sem("prTT"); prEv = sem("prEv")
        fIn = sem("fIn"); fS = sem("fS"); fT = sem("fT"); fR = sem("fR")
        fN = sem("fN"); fX = sem("fX"); fTT = sem("fTT"); fEv = sem("fEv")
        fM = sem("fM"); fPR = sem("fPR"); fPK = sem("fPK"); fE1 = sem("fE1")
        fSig = sem("fSig"); fRel = sem("fRel"); fK = sem("fK"); fT2 = sem("fT2")
        fEv2 = sem("fEv2"); fPV = sem("fPV"); fO = sem("fO")
        fS2 = sem("fS2"); fV2 = sem("fV2"); fR2 = sem("fR2"); fN2 = sem("fN2")
        fX2 = sem("fX2"); fFin = sem("fFin"); fW = sem("fW"); fSo = sem("fSo")
        dS_ = paS; dT_ = paT; dR_ = paR; dN_ = paN; dX_ = paX
        dTT = paTT; dEv = paEv; dP1 = paKV; dKC = paKC; dVC = paVC
        dTK = paT2; dKF = paKE; dIn_ = paIn; dSp_ = paSp; dW_ = paW
        dM = sem("dM"); dP2 = sem("dP2"); dP3 = sem("dP3")
        dTV = sem("dTV"); dVF = sem("dVF"); dT1 = sem("dT1"); dZ = sem("dZ")
        dScan = sem("dScan"); dE1 = sem("dE1"); dSig = sem("dSig")
        dTR = sem("dTR"); dRE = sem("dRE")
        dTB = sem("dTB"); dTE = sem("dTE")
        eIn = pbFl; eFin = pbFn; eS2 = pbS2; eT2 = pbT2; eR2 = pbR2
        eN2 = pbN2; eX2 = pbX2; eSo = pbSo; eOt = pbVo; eRy = pbAc; eWo = pbQK
        eNum = sem("eNum"); eDiv = sem("eDiv"); eW_ = sem("eW_"); ePro = sem("ePro")
        dEW = ePro
        eTz = dTB
        eSg = dTE
        vC = sem("vC"); aC = sem("aC")

        block = ctx.enter_context(nc.Block())

        p_firsts = {p: next(b["first"] for b in blocks if b["p"] == p and b["kb"] == 0)
                    for p in range(4)}
        blk_end = [b["first"] + b["nact"] for b in blocks]
        NV = len(visits)
        B_LN = NT
        B_IN = 16 * (NT + 2)
        B_SP = 32 * NT
        B_W = 16
        B_FL = 16 * 32
        B_FN = 32
        B_S2 = 32
        B_X2 = 64
        B_SO = 16 * 32
        B_VO = 32
        B_AC = len(visits)
        B_QK = len(visits)
        fins_by_pass = {p: [f for f in fins if f["p"] == p] for p in range(4)}

        f_e1t = bigA[:, 0:1024]
        f_rt = [bigA[:, 1024:2048], bigA[:, 5120:6144]]
        f_ot = [bigA[:, 2048:3072], bigA[:, 6144:7168]]
        f_ln = bigA[:, 3072:4096]
        f_sq = bigA[:, 4096:5120]
        f_xln = [bigB[:, 0:1024], bigB[:, 10240:11264]]
        f_fmx = bigB[:, 1024:2048]
        f_fmp = bigB[:, 2048:3072]
        f_dmx = bigB[:, 3072:4096]
        f_xrf = [bigB[:, 4096:5120], bigB[:, 11264:12288]]
        f_xkf = [bigB[:, 5120:6144], bigB[:, 12288:13312]]
        f_krl = bigB[:, 6144:7168]
        f_kab = [bigB[:, 7168:8192], bigB[:, 13312:14336]]
        f_kaf = bigB[:, 8192:9216]
        d_ktm = bigA[:, 0:1024]; d_vtm = bigA[:, 1024:2048]
        d_kfm = bigA[:, 2048:3072]; d_vfm = bigA[:, 3072:4096]
        d_Sl = bigA[:, 4096:5120]; d_Tl = bigA[:, 5120:6144]; d_e1t = bigA[:, 6144:7168]
        d_zt = [bigA[:, 7168:8192], bigA[:, 8192:9216]]
        d_t1t = [bigA[:, 9216:10240], bigA[:, 10240:11264]]
        d_sq = bigA[:, 11264:12288]
        d_totc = bigA[:, 12288:12800]
        d_tote = bigA[:, 12800:13312]
        d_sgr = bigA[:, 13312:14336]
        d_xln = bigB[:, 0:1024]; d_fmx = bigB[:, 1024:2048]; d_fmp = bigB[:, 2048:3072]
        d_dmx = bigB[:, 3072:4096]; d_xkf = bigB[:, 4096:5120]; d_xvf = bigB[:, 5120:6144]
        d_xrf = bigB[:, 6144:7168]; d_rtm = bigB[:, 7168:8192]
        d_rfm = [bigB[:, 8192:9216], bigB[:, 9216:10240]]
        e_Sf = bigA[:, 0:1024]; e_Tf = bigA[:, 1024:2048]; e_u1 = bigA[:, 2048:3072]
        e_num = bigA[:, 3072:4096]; e_den = bigA[:, 4096:5120]; e_ivd = bigA[:, 5120:6144]
        e_yt = bigA[:, 6144:7168]; e_ot = bigA[:, 7168:8192]; e_ln = bigA[:, 8192:9216]
        e_sq = bigA[:, 9216:10240]
        e_ztb = [bigA[:, 10240:11264], bigA[:, 11264:12288]]
        e_t1b = [bigA[:, 12288:13312], bigA[:, 13312:14336]]
        e_rfb = [bigA[:, 14336:15360], bigA[:, 15360:16384]]
        e_ryb = bigB[:, 0:1024]
        a_xln = bigB[:, 0:1024]
        a_fmx = [bigB[:, 1024:2048], bigB[:, 2048:3072]]
        a_ktm = bigB[:, 3072:4096]
        a_kfm = [bigB[:, 4096:5120], bigB[:, 5120:6144]]
        a_vbf = [bigB[:, 6144:7168], bigB[:, 7168:8192]]
        a_sq = bigA[:, 0:1024]
        b_acc = bigA[:, 0:8192]
        pbx = bigA[:, 9216:11264]
        b_q = bigB[:, 0:8192]
        b_kv = [(wpool[:, 0:8192], wpool[:, 8192:16384]),
                (wpool[:, 16384:24576], bigB[:, 8192:16384])]

        def wslot(k):
            return wpool[:, k * 8192:(k + 1) * 8192]

        EXG_T = {0: 16, 1: 32, "sig": 160, 2: 176}

        def ffn_vector(v, ch, base):
            bo = base * OWN
            pb = 1 + base * 2
            v.wait_ge(fW, 80 * (base + 1))
            v.wait_ge(exG, EXG_T[0] if base == 0 else EXG_T[2])
            ch(v.tensor_scalar_mul(prraw[0:32, :], prraw[0:32, :], sb_zm[0:32, 0:1]))
            pr = prraw[0:OWN, :]
            st_ = stat[0:OWN, :]
            ch(v.reduce_sum(st_[:, 0:1], pr, axis=AX.X))
            ch(v.tensor_scalar_mul(st_[:, 1:2], st_[:, 0:1], 1.0 / D))
            v.wait_ge(prS, pb)
            ch(v.tensor_scalar_mul(st_[:, 2:3], st_[:, 4:5], 1.0 / D))
            ch(v.tensor_mul(st_[:, 5:6], st_[:, 1:2], st_[:, 1:2]))
            ch(v.tensor_sub(st_[:, 3:4], st_[:, 2:3], st_[:, 5:6]), prT, pb)
            v.wait_ge(prR, pb)
            ch(v.tensor_scalar(st_[:, 6:7], st_[:, 1:2], st_[:, 7:8], -1.0,
                               ALU.mult, ALU.mult), prN, pb)
            v.wait_ge(prTT, pb)
            ch(v.tensor_copy(prfm[:], P3[:, 0:NCH * OWN]), prEv, pb)

            def F(i):
                n = bo + i + 1
                v.wait_ge(fIn, 16 * n)
                ch(v.reduce_sum(col(SUM), xb4[i % 4][:], axis=AX.X))
                ch(v.tensor_scalar_mul(col(M), col(SUM), 1.0 / D))
                v.wait_ge(fS, n)
                ch(v.tensor_scalar_mul(col(Q), col(SSQ), 1.0 / D))
                ch(v.tensor_mul(col(MSQ), col(M), col(M)))
                ch(v.tensor_sub(col(VAR), col(Q), col(MSQ)), fT, n)
                v.wait_ge(fR, n)
                ch(v.tensor_scalar(col(NB), col(M), col(RSTD), -1.0,
                                   ALU.mult, ALU.mult), fN, n)
                v.wait_ge(fTT, n)
                if n > 2:
                    v.wait_ge(fPK, n - 2)
                ch(v.tensor_copy(f_fmx, P3[:]), fEv, n)
                for cc in range(NCH):
                    ch(v.tensor_copy(f_fmp[:, cc * 128 + 1:cc * 128 + 128],
                                     f_fmx[:, cc * 128:cc * 128 + 127]))
                    ch(v.tensor_copy(f_fmp[:, cc * 128:cc * 128 + 1],
                                     prfm[:, cc * OWN + i:cc * OWN + i + 1]))
                ch(v.tensor_sub(f_dmx, f_fmx, f_fmp))
                ch(v.tensor_mul(f_xrf[i % 2], f_dmx, flatp[:, 0:1024]))
                ch(v.tensor_add(f_xrf[i % 2], f_xrf[i % 2], f_fmp))
                ch(v.tensor_mul(f_xkf[i % 2], f_dmx, flatp[:, 1024:2048]))
                ch(v.tensor_add(f_xkf[i % 2], f_xkf[i % 2], f_fmp), fM, n)

            def Bk(i):
                n = bo + i + 1
                v.wait_ge(fT2, n)
                ch(v.tensor_copy(f_kaf, P4[:]), fEv2, n)
                v.wait_ge(fPV, n)
                v.wait_ge(fSig, n)
                ch(v.tensor_mul(f_ot[i % 2], f_rt[i % 2], P2[:]), fO, n)
                ch(v.reduce_sum(col(SUM, 1), f_ot[i % 2], axis=AX.X))
                ch(v.tensor_scalar_mul(col(M, 1), col(SUM, 1), 1.0 / D))
                v.wait_ge(fS2, n)
                ch(v.tensor_scalar_mul(col(Q, 1), col(SSQ, 1), 1.0 / D))
                ch(v.tensor_mul(col(MSQ, 1), col(M, 1), col(M, 1)))
                ch(v.tensor_sub(col(VAR, 1), col(Q, 1), col(MSQ, 1)), fV2, n)
                v.wait_ge(fR2, n)
                ch(v.tensor_scalar(col(NB, 1), col(M, 1), col(RSTD, 1), -1.0,
                                   ALU.mult, ALU.mult), fN2, n)
                v.wait_ge(fX2, n)
                if i >= 2:
                    v.wait_ge(fSo, 16 * (n - 2))
                ch(v.tensor_add(obuf[i % 2][:], xb4[i % 4][:], f_ln), fFin, n)

            for t_ in range(OWN + 2):
                if t_ < OWN:
                    F(t_)
                if t_ >= 2:
                    Bk(t_ - 2)

        def ffn_scalar(a, ca, base):
            bo = base * OWN
            pb = 1 + base * 2
            pr = prraw[0:OWN, :]
            st_ = stat[0:OWN, :]
            a.wait_ge(exG, EXG_T[0] if base == 0 else EXG_T[2])
            ca(a.activation(f_sq[0:OWN, :], pr, AF.Square, accum_out=st_[:, 4:5]),
               prS, pb)
            a.wait_ge(prT, pb)
            ca(a.activation(st_[:, 8:9], st_[:, 3:4], AF.Ln, bias=EPS))
            ca(a.activation(st_[:, 7:8], st_[:, 8:9], AF.Exp, scale=-0.5), prR, pb)
            a.wait_ge(prN, pb)
            ca(a.activation(prln[0:OWN, :], pr, AF.Identity, scale=st_[:, 7:8],
                            bias=st_[:, 6:7]), prX, pb)

            def F(i):
                n = bo + i + 1
                a.wait_ge(fIn, 16 * n)
                ca(a.activation(f_sq, xb4[i % 4][:], AF.Square,
                                accum_out=col(SSQ)), fS, n)
                a.wait_ge(fT, n)
                ca(a.activation(col(LNV), col(VAR), AF.Ln, bias=EPS))
                ca(a.activation(col(RSTD), col(LNV), AF.Exp, scale=-0.5), fR, n)
                a.wait_ge(fN, n)
                if n > 2:
                    a.wait_ge(fTT, n - 2)
                ca(a.activation(f_xln[i % 2], xb4[i % 4][:], AF.Identity,
                                scale=col(RSTD), bias=col(NB)), fX, n)

            def Md(i):
                n = bo + i + 1
                a.wait_ge(fPR, n)
                ca(a.activation(f_e1t, P1a, AF.Exp, scale=-1.0), fE1, n)
                if n > 2:
                    a.wait_ge(fO, n - 2)
                ca(a.activation(f_rt[i % 2], f_e1t, AF.Ln, bias=1.0))
                ca(a.activation(f_rt[i % 2], f_rt[i % 2], AF.Exp, scale=-1.0), fSig, n)
                a.wait_ge(fPK, n)
                ca(a.activation(f_krl, P1b, AF.Relu), fRel, n)
                ca(a.activation(f_kab[i % 2], f_krl, AF.Square), fK, n)

            def Bk(i):
                n = bo + i + 1
                a.wait_ge(fO, n)
                ca(a.activation(f_sq, f_ot[i % 2], AF.Square, accum_out=col(SSQ, 1)), fS2, n)
                a.wait_ge(fV2, n)
                ca(a.activation(col(LNV, 1), col(VAR, 1), AF.Ln, bias=EPS))
                ca(a.activation(col(RSTD, 1), col(LNV, 1), AF.Exp, scale=-0.5), fR2, n)
                a.wait_ge(fN2, n)
                ca(a.activation(f_ln, f_ot[i % 2], AF.Identity, scale=col(RSTD, 1),
                                bias=col(NB, 1)), fX2, n)

            for t_ in range(OWN + 2):
                if t_ < OWN:
                    F(t_)
                if 1 <= t_ <= OWN:
                    Md(t_ - 1)
                if t_ >= 2:
                    Bk(t_ - 2)

        def ffn_tensor(t, base):
            bo = base * OWN
            pb = 1 + base * 2
            t.wait_ge(prX, pb)
            for cc in range(NCH):
                ins = t.transpose(P3[:, cc * OWN:(cc + 1) * OWN],
                                  prln[0:OWN, cc * 128:(cc + 1) * 128],
                                  idb[0:OWN, 0:OWN])
            ins.then_inc(prTT, 1)
            t.wait_ge(fW, 80 * (base + 1))

            def F(i):
                n = bo + i + 1
                t.wait_ge(fX, n)
                t.wait_ge(prEv, pb)
                if n > 1:
                    t.wait_ge(fEv, n - 1)
                for cc in range(NCH):
                    ins = t.transpose(P3[:, cc * 128:(cc + 1) * 128],
                                      f_xln[i % 2][:, cc * 128:(cc + 1) * 128], idb[:])
                ins.then_inc(fTT, 1)

            def Md(i):
                n = bo + i + 1
                t.wait_ge(fM, n)
                if n > 1:
                    t.wait_ge(fE1, n - 1)
                for h in range(2):
                    for kc in range(NCH):
                        ins = t.matmul(P1a[:, h * 512:(h + 1) * 512],
                                       f_xrf[i % 2][:, kc * 128:(kc + 1) * 128],
                                       wslot(0)[:, kc * D + h * 512:kc * D + (h + 1) * 512],
                                       start=(kc == 0), stop=(kc == NCH - 1))
                ins.then_inc(fPR, 1)
                if n > 1:
                    t.wait_ge(fRel, n - 1)
                for h in range(2):
                    for kc in range(NCH):
                        ins = t.matmul(P1b[:, h * 512:(h + 1) * 512],
                                       f_xkf[i % 2][:, kc * 128:(kc + 1) * 128],
                                       wslot(1)[:, kc * D + h * 512:kc * D + (h + 1) * 512],
                                       start=(kc == 0), stop=(kc == NCH - 1))
                ins.then_inc(fPK, 1)

            def Bk(i):
                n = bo + i + 1
                t.wait_ge(fK, n)
                if n > 1:
                    t.wait_ge(fEv2, n - 1)
                for cc in range(NCH):
                    ins = t.transpose(P4[:, cc * 128:(cc + 1) * 128],
                                      f_kab[i % 2][:, cc * 128:(cc + 1) * 128], idb[:])
                ins.then_inc(fT2, 1)
                t.wait_ge(fEv2, n)
                if n > 1:
                    t.wait_ge(fO, n - 1)
                for h in range(2):
                    for kc in range(NCH):
                        ins = t.matmul(P2[:, h * 512:(h + 1) * 512],
                                       f_kaf[:, kc * 128:(kc + 1) * 128],
                                       wslot(2)[:, kc * D + h * 512:kc * D + (h + 1) * 512],
                                       start=(kc == 0), stop=(kc == NCH - 1))
                ins.then_inc(fPV, 1)

            for t_ in range(OWN + 2):
                if t_ < OWN:
                    F(t_)
                if 1 <= t_ <= OWN:
                    Md(t_ - 1)
                if t_ >= 2:
                    Bk(t_ - 2)

        @block.sync
        def _(sync):
            ph = [0]

            def bar():
                ph[0] += 5
                sync.nop().then_inc(PH, 1)
                sync.wait_ge(PH, ph[0])

            for _rep in range(2):
                for dst, src_ in ((idb[:], identb), (idf[:], identf), (s_cst[:], constv),
                                  (msk[:, 0:128], maskp[0]), (msk[:, 128:256], maskp[1]),
                                  (wcol[:], td_colD), (eucol[:], tf_colD),
                                  (sb_ib[:], idxb_d), (sb_is[:], idxs_d),
                                  (sb_zm[:], zpm_d), (sb_tz[:], tzm_d)):
                    sync.dma_start(out=dst, in_=src_).then_inc(dPre, 16)
            for g in range(NT + 1):
                if g < NT:
                    if g >= 2:
                        sync.wait_ge(paX, g - 1)
                    sync.dma_start(out=xbuf[g % 2][:], in_=xfull[g]).then_inc(paIn, 16)
                    if g < 2:
                        sync.dma_start(out=xbuf[g % 2][:], in_=xfull[g]).then_inc(paIn, 16)
                if g >= 1:
                    k = g - 1
                    sync.wait_ge(paVC, k + 1)
                    sync.dma_start(out=vf_scr[k * 128:(k + 1) * 128, :],
                                   in_=a_vbf[k % 2]).then_inc(paSp, 16)
                    sync.wait_ge(paKE, k + 1)
                    sync.dma_start(out=kf_scr[:, :, k * 128:(k + 1) * 128].rearrange("c p t -> p c t"),
                                   in_=a_kfm[k % 2].rearrange("p (c t) -> p c t", c=NCH)).then_inc(paSp, 16)
            bar()
            sync.wait_ge(paSp, 32 * NT)
            for bi, b in enumerate(blocks):
                p, kb = b["p"], b["kb"]
                if kb == 0:
                    if p > 0:
                        sync.wait_ge(pbFn, 8 * p)
                    for ss in range(8):
                        jn = 8 * p + ss
                        if jn >= 2:
                            sync.wait_ge(pbQx, jn - 1)
                        sync.dma_start(out=xbuf[ss % 2][:], in_=x_own[jn]).then_inc(pbQd, 16)
                if bi >= 2:
                    sync.wait_ge(pbPV, blk_end[bi - 2])
                for dc in range(NCH):
                    sync.dma_start(out=b_kv[bi % 2][0][:, dc * 1024:(dc + 1) * 1024],
                                   in_=kf_scr[dc, :, kb * 1024:(kb + 1) * 1024]).then_inc(pbKV, 16)
                for sub in range(NCH):
                    sync.dma_start(out=b_kv[bi % 2][1][:, sub * 1024:(sub + 1) * 1024],
                                   in_=vf_scr[kb * 1024 + sub * 128:kb * 1024 + (sub + 1) * 128, :]).then_inc(pbKV, 16)
            bar()
            sync.wait_ge(pbSo, 16 * OWN)
            sync.dma_start(out=bd_in[:, :],
                           in_=s1_scr.rearrange("j p d -> (j p) d")[127::128, :]).then_inc(exD, 16)
            for i in range(OWN):
                if i >= 4:
                    sync.wait_ge(fFin, i - 3)
                sync.dma_start(out=xb4[i % 4][:], in_=s1_scr[i]).then_inc(fIn, 16)
            bar()
            sync.wait_ge(fSo, 16 * OWN)
            sync.dma_start(out=bd_in[:, :],
                           in_=s2_scr.rearrange("j p d -> (j p) d")[127::128, :]).then_inc(exD, 16)
            for i in range(OWN + 1):
                if i < OWN:
                    if i >= 2:
                        sync.wait_ge(dM, i - 1)
                    sync.dma_start(out=xbuf[i % 2][:], in_=s2_scr[i]).then_inc(dIn_, 16)
                if i >= 1:
                    k = i - 1
                    sync.wait_ge(dZ, k + 1)
                    sync.dma_start(out=z_scr[k], in_=d_zt[k % 2]).then_inc(dSp_, 16)
                    sync.dma_start(out=t1_scr[k], in_=d_t1t[k % 2]).then_inc(dSp_, 16)
                    sync.wait_ge(dRE, k + 1)
                    sync.dma_start(out=r_scr[k], in_=d_rfm[k % 2]).then_inc(dSp_, 16)
            sync.wait_ge(dTE, 4)
            for q4 in range(4):
                sync.dma_start(out=tt_in[q4 * 128:(q4 + 1) * 128, :],
                               in_=d_tote[:, q4 * 128:(q4 + 1) * 128]).then_inc(exD, 16)
            bar()
            sync.wait_ge(dSp_, B_SP + 48 * OWN)
            sync.wait_ge(eSg, 5)
            for i in range(OWN + 1):
                if i < OWN:
                    if i >= 2:
                        sync.wait_ge(eFin, B_FN + i - 1)
                    sync.dma_start(out=xbuf[i % 2][:], in_=s2_scr[i]).then_inc(eIn, 16)
                    sync.dma_start(out=e_ztb[i % 2], in_=z_scr[i]).then_inc(eIn, 16)
                    sync.dma_start(out=e_t1b[i % 2], in_=t1_scr[i]).then_inc(eIn, 16)
                if i >= 1:
                    k = i - 1
                    sync.wait_ge(eFin, B_FN + k + 1)
                    sync.dma_start(out=s3_scr[k], in_=obuf[k % 2][:]).then_inc(eSo, 16)
            bar()
            sync.wait_ge(eSo, B_SO + 16 * OWN)
            sync.dma_start(out=bd_in[:, :],
                           in_=s3_scr.rearrange("j p d -> (j p) d")[127::128, :]).then_inc(exD, 16)
            for i in range(OWN):
                if i >= 4:
                    sync.wait_ge(fFin, OWN + i - 3)
                sync.dma_start(out=xb4[i % 4][:], in_=s3_scr[i]).then_inc(fIn, 16)
            sync.wait_ge(fSo, 16 * 2 * OWN)

        @block.gpsimd
        def _(g):
            ph = [0]

            def bar():
                ph[0] += 5
                g.nop().then_inc(PH, 1)
                g.wait_ge(PH, ph[0])

            g.dma_start(out=wpool[:, 0:16384].rearrange("p (c f) -> p c f", c=NCH),
                        in_=wkvT.rearrange("(c p) f -> p c f", p=128)).then_inc(paW, 16)
            bar()
            for p in range(4):
                fl = fins_by_pass[p]
                for k in range(len(fl) + 1):
                    if k < len(fl):
                        f = fl[k]; fi = f["fi"]
                        if fi >= 2:
                            g.wait_ge(pbFn, fi - 1)
                        g.wait_ge(pbQe, 8 * (p + 1))
                        g.dma_start(out=xbuf[fi % 2][:], in_=x_own[f["j"]]).then_inc(pbFl, 16)
                    if k >= 1:
                        f = fl[k - 1]; fi = f["fi"]
                        g.wait_ge(pbFn, fi + 1)
                        g.dma_start(out=s1_scr[f["j"]], in_=obuf[fi % 2][:]).then_inc(pbSo, 16)
            bar()
            g.wait_ge(exD, 16)
            g.collective_compute("AllGather", ALU.bypass, replica_groups=[list(range(8))],
                                 ins=[bd_in.ap().opt()], outs=[bd_out.ap().opt()]).then_inc(exC)
            g.wait_ge(exC, 1)
            g.indirect_dma_start(out=prraw[0:32, :], out_offset=None, in_=bd_out.ap(),
                                 in_offset=IndirectOffsetOnAxis(ap=sb_ib[:, 0:1], axis=0)).then_inc(exG, 16)
            for kk in range(3):
                g.dma_start(out=wslot(kk).rearrange("p (c f) -> p c f", c=NCH),
                            in_=wT2[kk].rearrange("(c p) f -> p c f", p=128)).then_inc(fW, 16)
            g.dma_start(out=flatp[:, 0:1024], in_=mu2f[0]).then_inc(fW, 16)
            g.dma_start(out=flatp[:, 1024:2048], in_=mu2f[1]).then_inc(fW, 16)
            for k in range(OWN):
                g.wait_ge(fFin, k + 1)
                g.dma_start(out=s2_scr[k], in_=obuf[k % 2][:]).then_inc(fSo, 16)
            bar()
            g.wait_ge(exD, 32)
            g.collective_compute("AllGather", ALU.bypass, replica_groups=[list(range(8))],
                                 ins=[bd_in.ap().opt()], outs=[bd_out.ap().opt()]).then_inc(exC)
            g.wait_ge(exC, 2)
            g.indirect_dma_start(out=prraw[0:32, :], out_offset=None, in_=bd_out.ap(),
                                 in_offset=IndirectOffsetOnAxis(ap=sb_ib[:, 0:1], axis=0)).then_inc(exG, 16)
            for kk in range(3):
                g.dma_start(out=wslot(kk).rearrange("p (c f) -> p c f", c=NCH),
                            in_=wT3[kk].rearrange("(c p) f -> p c f", p=128)).then_inc(dW_, 16)
            for kk in range(3):
                g.dma_start(out=flatp[:, kk * 1024:(kk + 1) * 1024], in_=tm3f[kk]).then_inc(dW_, 16)
            g.wait_ge(exD, 96)
            g.collective_compute("AllGather", ALU.bypass, replica_groups=[list(range(8))],
                                 ins=[tt_in.ap().opt()], outs=[tt_out.ap().opt()]).then_inc(exC)
            g.wait_ge(exC, 3)
            for kk in range(8):
                g.indirect_dma_start(out=d_sgr[:, kk * 128:(kk + 1) * 128], out_offset=None,
                                     in_=tt_out.ap(),
                                     in_offset=IndirectOffsetOnAxis(ap=sb_is[:, kk:kk + 1], axis=0)).then_inc(exG, 16)
            bar()
            g.dma_start(out=wslot(0).rearrange("p (c f) -> p c f", c=NCH),
                        in_=woT3.rearrange("(c p) f -> p c f", p=128)).then_inc(eW_, 16)
            for i in range(OWN):
                if i >= 2:
                    g.wait_ge(eRy, B_AC + i - 1)
                g.dma_start(out=e_rfb[i % 2], in_=r_scr[i]).then_inc(eIn, 16)
            bar()
            g.wait_ge(exD, 112)
            g.collective_compute("AllGather", ALU.bypass, replica_groups=[list(range(8))],
                                 ins=[bd_in.ap().opt()], outs=[bd_out.ap().opt()]).then_inc(exC)
            g.wait_ge(exC, 4)
            g.indirect_dma_start(out=prraw[0:32, :], out_offset=None, in_=bd_out.ap(),
                                 in_offset=IndirectOffsetOnAxis(ap=sb_ib[:, 0:1], axis=0)).then_inc(exG, 16)
            for kk in range(3):
                g.dma_start(out=wslot(kk).rearrange("p (c f) -> p c f", c=NCH),
                            in_=wT4[kk].rearrange("(c p) f -> p c f", p=128)).then_inc(fW, 16)
            g.dma_start(out=flatp[:, 0:1024], in_=mu4f[0]).then_inc(fW, 16)
            g.dma_start(out=flatp[:, 1024:2048], in_=mu4f[1]).then_inc(fW, 16)
            for i in range(OWN):
                g.wait_ge(fFin, OWN + i + 1)
                g.dma_start(out=ostream[i], in_=obuf[i % 2][:]).then_inc(fSo, 16)
            g.wait_ge(fSo, 16 * 2 * OWN)

        @block.vector
        def _(v):
            _vn = [0]

            def ch(ins, cross=None, tgt=None):
                if cross is None:
                    _vn[0] += 1
                    ins.then_inc(vC, 1)
                    v.wait_ge(vC, _vn[0])
                else:
                    ins.then_inc(cross, 1)
                    v.wait_ge(cross, tgt)
                return ins

            ph = [0]

            def bar():
                ph[0] += 5
                v.nop().then_inc(PH, 1)
                v.wait_ge(PH, ph[0])

            v.wait_ge(dPre, 352)
            for i in range(NT):
                n = i + 1
                v.wait_ge(paIn, 16 * (n + min(n, 2)))
                if i == 0:
                    ch(v.reduce_sum(col(SUM), xbuf[i % 2][:], axis=AX.X))
                ch(v.reduce_sum(col(SUM), xbuf[i % 2][:], axis=AX.X))
                ch(v.tensor_scalar_mul(col(M), col(SUM), 1.0 / D))
                v.wait_ge(paS, n)
                ch(v.tensor_scalar_mul(col(Q), col(SSQ), 1.0 / D))
                ch(v.tensor_mul(col(MSQ), col(M), col(M)))
                ch(v.tensor_sub(col(VAR), col(Q), col(MSQ)), paT, n)
                v.wait_ge(paR, n)
                ch(v.tensor_scalar(col(NB), col(M), col(RSTD), -1.0,
                                   ALU.mult, ALU.mult), paN, n)
                v.wait_ge(paTT, n)
                if i >= 2:
                    v.wait_ge(paSp, 32 * (i - 1))
                ch(v.tensor_copy(a_fmx[i % 2], P3[:]), paEv, n)
                v.wait_ge(paKV, n)
                ch(v.tensor_copy(a_vbf[i % 2], P1b), paVC, n)
                v.wait_ge(paT2, n)
                ch(v.tensor_copy(a_kfm[i % 2], P4[:]), paKE, n)
            bar()
            fin_iter = iter(fins)
            nextf = next(fin_iter, None)
            for u, vis in enumerate(visits):
                j = vis["j"]; p = vis["p"]; s = j - 8 * p; W = vis["W"]
                if u == p_firsts[p]:
                    ch(v.memset(b_acc, 0.0))
                    ch(v.memset(lcol[:], 0.0))
                    ch(v.memset(negm[:], 3.0e38))
                    for ss in range(8):
                        jn = 8 * p + ss + 1
                        v.wait_ge(pbQd, 16 * jn)
                        ch(v.reduce_sum(col(SUM), xbuf[ss % 2][:], axis=AX.X))
                        ch(v.tensor_scalar_mul(statsP[:, 3 * ss:3 * ss + 1],
                                               col(SUM), 1.0 / D))
                        v.wait_ge(pbQs, jn)
                        ch(v.tensor_scalar_mul(col(Q), col(SSQ), 1.0 / D))
                        ch(v.tensor_mul(col(MSQ), statsP[:, 3 * ss:3 * ss + 1],
                                        statsP[:, 3 * ss:3 * ss + 1]))
                        ch(v.tensor_sub(col(VAR), col(Q), col(MSQ)), pbQt, jn)
                        v.wait_ge(pbQr, jn)
                        ch(v.tensor_scalar(statsP[:, 3 * ss + 2:3 * ss + 3],
                                           statsP[:, 3 * ss:3 * ss + 1],
                                           statsP[:, 3 * ss + 1:3 * ss + 2], -1.0,
                                           ALU.mult, ALU.mult), pbQn, jn)
                        v.wait_ge(pbQT, jn)
                        ch(v.tensor_copy(b_q[:, ss * 1024:(ss + 1) * 1024], P3[:]),
                           pbQe, jn)
                v.wait_ge(pbQK, u + 1)
                SCu = P1a if u % 2 == 0 else P2
                if vis["mask"]:
                    r0 = 2 * j - 8 * vis["kb"]
                    ch(v.tensor_add(SCu[:, r0 * 128:(r0 + 1) * 128],
                                    SCu[:, r0 * 128:(r0 + 1) * 128], msk[:, 0:128]))
                    ch(v.tensor_add(SCu[:, (r0 + 1) * 128:(r0 + 2) * 128],
                                    SCu[:, (r0 + 1) * 128:(r0 + 2) * 128], msk[:, 128:256]))
                ch(v.reduce_max(tmpa[:, 0:1], SCu[:, 0:W], axis=AX.X, negate=True))
                ch(v.tensor_tensor(tmpa[:, 1:2], negm[:, s:s + 1], tmpa[:, 0:1], ALU.min))
                ch(v.tensor_sub(tmpa[:, 2:3], tmpa[:, 1:2], negm[:, s:s + 1]))
                ch(v.tensor_copy(negm[:, s:s + 1], tmpa[:, 1:2]), pbSt, u + 1)
                v.wait_ge(pbPT, u + 1)
                ch(v.tensor_copy(pt_sb[:, 0:W], P3[:, 0:W]), pbPe, u + 1)
                v.wait_ge(pbPV, u + 1)
                v.wait_ge(pbOt, 2 * (u + 1))
                ch(v.scalar_tensor_tensor(lcol[:, s:s + 1], lcol[:, s:s + 1],
                                          alpha[:, s:s + 1], rsum[:, s:s + 1],
                                          ALU.mult, ALU.add))
                ch(v.scalar_tensor_tensor(b_acc[:, s * 1024:(s + 1) * 1024],
                                          b_acc[:, s * 1024:(s + 1) * 1024],
                                          alpha[:, s:s + 1], P1b,
                                          ALU.mult, ALU.add), pbAc, u + 1)
                while nextf is not None and nextf["after"] == u:
                    f = nextf; fi = f["fi"]; fs = f["j"] - 8 * p; fn = fi + 1
                    v.wait_ge(pbFl, 16 * fn)
                    ch(v.reciprocal(tmpa[:, 3:4], lcol[:, fs:fs + 1]))
                    v.wait_ge(pbX2, 2 * fn - 1)
                    ch(v.scalar_tensor_tensor(pbx[:, 1024:2048],
                                              b_acc[:, fs * 1024:(fs + 1) * 1024],
                                              tmpa[:, 3:4], pbx[:, 0:1024],
                                              ALU.mult, ALU.add), pbVo, fn)
                    ch(v.reduce_sum(col(SUM, 1), pbx[:, 1024:2048], axis=AX.X))
                    ch(v.tensor_scalar_mul(col(M, 1), col(SUM, 1), 1.0 / D))
                    v.wait_ge(pbS2, fn)
                    ch(v.tensor_scalar_mul(col(Q, 1), col(SSQ, 1), 1.0 / D))
                    ch(v.tensor_mul(col(MSQ, 1), col(M, 1), col(M, 1)))
                    ch(v.tensor_sub(col(VAR, 1), col(Q, 1), col(MSQ, 1)), pbT2, fn)
                    v.wait_ge(pbR2, fn)
                    ch(v.tensor_scalar(col(NB, 1), col(M, 1), col(RSTD, 1), -1.0,
                                       ALU.mult, ALU.mult), pbN2, fn)
                    v.wait_ge(pbX2, 2 * fn)
                    if fi >= 2:
                        v.wait_ge(pbSo, 16 * (fi - 1))
                    ch(v.tensor_add(obuf[fi % 2][:], xbuf[fi % 2][:], pbx[:, 0:1024]),
                       pbFn, fn)
                    nextf = next(fin_iter, None)
            bar()
            ffn_vector(v, ch, base=0)
            bar()
            v.wait_ge(dW_, B_W + 96)
            v.wait_ge(dEW, 1)
            ch(v.memset(onef[:], 1.0))
            for cc in range(NCH):
                ch(v.tensor_scalar_mul(EWf[:, cc * 128:(cc + 1) * 128], onef[:],
                                       ewcol[:, cc:cc + 1]))
            v.wait_ge(exG, EXG_T[1])
            ch(v.tensor_scalar_mul(prraw[0:32, :], prraw[0:32, :], sb_zm[0:32, 0:1]))
            pr = prraw[0:OWN, :]
            st_ = stat[0:OWN, :]
            ch(v.reduce_sum(st_[:, 0:1], pr, axis=AX.X))
            ch(v.tensor_scalar_mul(st_[:, 1:2], st_[:, 0:1], 1.0 / D))
            v.wait_ge(prS, 2)
            ch(v.tensor_scalar_mul(st_[:, 2:3], st_[:, 4:5], 1.0 / D))
            ch(v.tensor_mul(st_[:, 5:6], st_[:, 1:2], st_[:, 1:2]))
            ch(v.tensor_sub(st_[:, 3:4], st_[:, 2:3], st_[:, 5:6]), prT, 2)
            v.wait_ge(prR, 2)
            ch(v.tensor_scalar(st_[:, 6:7], st_[:, 1:2], st_[:, 7:8], -1.0,
                               ALU.mult, ALU.mult), prN, 2)
            v.wait_ge(prTT, 2)
            ch(v.tensor_copy(prfm[:], P3[:, 0:NCH * OWN]), prEv, 2)
            for i in range(OWN):
                n = i + 1
                v.wait_ge(dIn_, B_IN + 16 * n)
                ch(v.reduce_sum(col(SUM), xbuf[i % 2][:], axis=AX.X))
                ch(v.tensor_scalar_mul(col(M), col(SUM), 1.0 / D))
                v.wait_ge(dS_, B_LN + n)
                ch(v.tensor_scalar_mul(col(Q), col(SSQ), 1.0 / D))
                ch(v.tensor_mul(col(MSQ), col(M), col(M)))
                ch(v.tensor_sub(col(VAR), col(Q), col(MSQ)), dT_, B_LN + n)
                v.wait_ge(dR_, B_LN + n)
                ch(v.tensor_scalar(col(NB), col(M), col(RSTD), -1.0,
                                   ALU.mult, ALU.mult), dN_, B_LN + n)
                v.wait_ge(dTT, B_LN + n)
                if i > 0:
                    v.wait_ge(dP3, i)
                ch(v.tensor_copy(d_fmx, P3[:]), dEv, B_LN + n)
                for cc in range(NCH):
                    ch(v.tensor_copy(d_fmp[:, cc * 128 + 1:cc * 128 + 128],
                                     d_fmx[:, cc * 128:cc * 128 + 127]))
                    ch(v.tensor_copy(d_fmp[:, cc * 128:cc * 128 + 1],
                                     prfm[:, cc * OWN + i:cc * OWN + i + 1]))
                ch(v.tensor_sub(d_dmx, d_fmx, d_fmp))
                ch(v.tensor_mul(d_xkf, d_dmx, flatp[:, 0:1024]))
                ch(v.tensor_add(d_xkf, d_xkf, d_fmp))
                ch(v.tensor_mul(d_xvf, d_dmx, flatp[:, 1024:2048]))
                ch(v.tensor_add(d_xvf, d_xvf, d_fmp))
                ch(v.tensor_mul(d_xrf, d_dmx, flatp[:, 2048:3072]))
                ch(v.tensor_add(d_xrf, d_xrf, d_fmp), dM, n)
                v.wait_ge(dP1, B_LN + n)
                ch(v.tensor_copy(d_ktm, P1a), dKC, B_LN + n)
                v.wait_ge(dP2, n)
                ch(v.tensor_copy(d_vtm, P1b), dVC, B_LN + n)
                v.wait_ge(dTK, B_LN + n)
                ch(v.tensor_copy(d_kfm, P1a), dKF, B_LN + n)
                v.wait_ge(dTV, n)
                ch(v.tensor_copy(d_vfm, P1b), dVF, n)
                v.wait_ge(dT1, n)
                if i >= 2:
                    v.wait_ge(dSp_, B_SP + 48 * (i - 1))
                ch(v.tensor_mul(d_zt[i % 2], d_t1t[i % 2], d_vfm), dZ, n)
                for cc in range(NCH):
                    ch(v.tensor_tensor_scan(d_Sl[:, cc * 128:(cc + 1) * 128],
                                            EWf[:, cc * 128:(cc + 1) * 128],
                                            d_zt[i % 2][:, cc * 128:(cc + 1) * 128],
                                            0.0, ALU.mult, ALU.add))
                    ch(v.tensor_tensor_scan(d_Tl[:, cc * 128:(cc + 1) * 128],
                                            EWf[:, cc * 128:(cc + 1) * 128],
                                            d_t1t[i % 2][:, cc * 128:(cc + 1) * 128],
                                            0.0, ALU.mult, ALU.add))
                ch(v.tensor_copy(d_totc[:, i * 16:i * 16 + 8],
                                 d_Sl.rearrange("p (c t) -> p c t", c=NCH)[:, :, 127]))
                ch(v.tensor_copy(d_totc[:, i * 16 + 8:i * 16 + 16],
                                 d_Tl.rearrange("p (c t) -> p c t", c=NCH)[:, :, 127]),
                   dScan, n)
                v.wait_ge(dTR, n)
                ch(v.tensor_copy(d_rfm[i % 2], P4[:]), dRE, n)
            v.wait_ge(dTB, 1)
            for q4 in range(4):
                ch(v.tensor_copy(d_tote[:, q4 * 128:(q4 + 1) * 128],
                                 P2[:, q4 * 128:(q4 + 1) * 128]), dTE, q4 + 1)
            bar()
            v.wait_ge(eW_, 16)
            v.wait_ge(ePro, 2)
            ch(v.tensor_mul(gcol[:], eucol[:], ewinv[:]))
            for cc in range(NCH):
                ch(v.tensor_scalar_mul(EL64[:, cc * NT:(cc + 1) * NT], onef[:, 0:NT],
                                       elcol[:, cc:cc + 1]))
            v.wait_ge(eTz, 2)
            for kk in range(NCH):
                ins = v.tensor_copy(totsb[:, kk * 128:(kk + 1) * 128],
                                    P2[:, kk * 128:(kk + 1) * 128])
                if kk == NCH - 1:
                    ch(ins, eSg, 5)
                else:
                    ch(ins)
            for kk in range(NCH):
                ch(v.tensor_scalar_mul(totsb[:, kk * 128:kk * 128 + 1],
                                       totsb[:, kk * 128:kk * 128 + 1], sb_tz[:, 0:1]))
                ch(v.tensor_scalar_mul(totsb[:, kk * 128 + 64:kk * 128 + 65],
                                       totsb[:, kk * 128 + 64:kk * 128 + 65], sb_tz[:, 0:1]))
            for cc in range(NCH):
                for half in range(2):
                    off = cc * 128 + half * 64
                    ch(v.tensor_tensor_scan(CarrS[:, off:off + 64],
                                            EL64[:, cc * NT:cc * NT + 64],
                                            totsb[:, off:off + 64],
                                            0.0, ALU.mult, ALU.add))
            for i in range(OWN):
                n = i + 1
                v.wait_ge(eIn, B_FL + 64 * n)
                for cc in range(NCH):
                    ch(v.tensor_tensor_scan(e_Sf[:, cc * 128:(cc + 1) * 128],
                                            EWf[:, cc * 128:(cc + 1) * 128],
                                            e_ztb[i % 2][:, cc * 128:(cc + 1) * 128],
                                            CarrS[:, cc * 128 + 2 * i:cc * 128 + 2 * i + 1],
                                            ALU.mult, ALU.add))
                    ch(v.tensor_tensor_scan(e_Tf[:, cc * 128:(cc + 1) * 128],
                                            EWf[:, cc * 128:(cc + 1) * 128],
                                            e_t1b[i % 2][:, cc * 128:(cc + 1) * 128],
                                            CarrS[:, cc * 128 + 64 + 2 * i:cc * 128 + 64 + 2 * i + 1],
                                            ALU.mult, ALU.add))
                ch(v.tensor_sub(e_u1, e_Sf, e_ztb[i % 2]))
                for cc in range(NCH):
                    ch(v.scalar_tensor_tensor(e_num[:, cc * 128:(cc + 1) * 128],
                                              e_u1[:, cc * 128:(cc + 1) * 128],
                                              gcol[:, cc:cc + 1],
                                              e_ztb[i % 2][:, cc * 128:(cc + 1) * 128],
                                              ALU.mult, ALU.add))
                ch(v.tensor_sub(e_u1, e_Tf, e_t1b[i % 2]))
                for cc in range(NCH):
                    ins = v.scalar_tensor_tensor(e_den[:, cc * 128:(cc + 1) * 128],
                                                 e_u1[:, cc * 128:(cc + 1) * 128],
                                                 gcol[:, cc:cc + 1],
                                                 e_t1b[i % 2][:, cc * 128:(cc + 1) * 128],
                                                 ALU.mult, ALU.add)
                    if cc == NCH - 1:
                        ch(ins, eNum, n)
                    else:
                        ch(ins)
                v.wait_ge(eDiv, n)
                ch(v.tensor_mul(e_yt, e_num, e_ivd))
                ch(v.tensor_mul(e_ryb, e_yt, e_rfb[i % 2]), eRy, B_AC + n)
                v.wait_ge(eWo, B_QK + n)
                ch(v.tensor_copy(e_ot, P2[:]), eOt, B_VO + n)
                ch(v.reduce_sum(col(SUM), e_ot, axis=AX.X))
                ch(v.tensor_scalar_mul(col(M), col(SUM), 1.0 / D))
                v.wait_ge(eS2, B_S2 + n)
                ch(v.tensor_scalar_mul(col(Q), col(SSQ), 1.0 / D))
                ch(v.tensor_mul(col(MSQ), col(M), col(M)))
                ch(v.tensor_sub(col(VAR), col(Q), col(MSQ)), eT2, B_S2 + n)
                v.wait_ge(eR2, B_S2 + n)
                ch(v.tensor_scalar(col(NB), col(M), col(RSTD), -1.0,
                                   ALU.mult, ALU.mult), eN2, B_S2 + n)
                v.wait_ge(eX2, B_X2 + n)
                if i >= 2:
                    v.wait_ge(eSo, B_SO + 16 * (i - 1))
                ch(v.tensor_add(obuf[i % 2][:], xbuf[i % 2][:], e_ln), eFin, B_FN + n)
            bar()
            ffn_vector(v, ch, base=1)

        @block.scalar
        def _(a):
            _an = [0]

            def ca(ins, cross=None, tgt=None):
                if cross is None:
                    _an[0] += 1
                    ins.then_inc(aC, 1)
                    a.wait_ge(aC, _an[0])
                else:
                    ins.then_inc(cross, 1)
                    a.wait_ge(cross, tgt)
                return ins

            ph = [0]

            def bar():
                ph[0] += 5
                a.nop().then_inc(PH, 1)
                a.wait_ge(PH, ph[0])

            a.wait_ge(dPre, 352)
            for i in range(NT):
                n = i + 1
                a.wait_ge(paIn, 16 * (n + min(n, 2)))
                if i == 0:
                    ca(a.activation(a_sq, xbuf[i % 2][:], AF.Square,
                                    accum_out=col(SSQ)))
                ca(a.activation(a_sq, xbuf[i % 2][:], AF.Square,
                                accum_out=col(SSQ)), paS, n)
                a.wait_ge(paT, n)
                ca(a.activation(col(LNV), col(VAR), AF.Ln, bias=EPS))
                ca(a.activation(col(RSTD), col(LNV), AF.Exp, scale=-0.5), paR, n)
                a.wait_ge(paN, n)
                ca(a.activation(a_xln, xbuf[i % 2][:], AF.Identity, scale=col(RSTD),
                                bias=col(NB)), paX, n)
                a.wait_ge(paKV, n)
                ca(a.activation(a_ktm, P1a, AF.Identity), paKC, n)
            bar()
            fin_iter = iter(fins)
            nextf = next(fin_iter, None)
            for u, vis in enumerate(visits):
                j = vis["j"]; p = vis["p"]; s = j - 8 * p; W = vis["W"]
                if u == p_firsts[p]:
                    for ss in range(8):
                        jn = 8 * p + ss + 1
                        a.wait_ge(pbQd, 16 * jn)
                        ca(a.activation(pbx[:, 0:1024], xbuf[ss % 2][:], AF.Square,
                                        accum_out=col(SSQ)), pbQs, jn)
                        a.wait_ge(pbQt, jn)
                        ca(a.activation(col(LNV), col(VAR), AF.Ln, bias=EPS))
                        ca(a.activation(statsP[:, 3 * ss + 1:3 * ss + 2], col(LNV),
                                        AF.Exp, scale=-0.5), pbQr, jn)
                        a.wait_ge(pbQn, jn)
                        if jn > 1:
                            a.wait_ge(pbQT, jn - 1)
                        ca(a.activation(P_sb[:], xbuf[ss % 2][:], AF.Identity,
                                        scale=statsP[:, 3 * ss + 1:3 * ss + 2],
                                        bias=statsP[:, 3 * ss + 2:3 * ss + 3]), pbQx, jn)
                a.wait_ge(pbSt, u + 1)
                SCu = P1a if u % 2 == 0 else P2
                ca(a.activation(alpha[:, s:s + 1], tmpa[:, 2:3], AF.Exp), pbOt, 2 * u + 1)
                ca(a.activation(P_sb[:, 0:W], SCu[:, 0:W], AF.Exp,
                                bias=negm[:, s:s + 1],
                                accum_out=rsum[:, s:s + 1]), pbOt, 2 * u + 2)
                while nextf is not None and nextf["after"] == u:
                    f = nextf; fi = f["fi"]; fs = f["j"] - 8 * p; fn = fi + 1
                    a.wait_ge(pbFl, 16 * fn)
                    if fi > 0:
                        a.wait_ge(pbFn, fi)
                    ca(a.activation(pbx[:, 0:1024], xbuf[fi % 2][:], AF.Identity,
                                    scale=statsP[:, 3 * fs + 1:3 * fs + 2],
                                    bias=statsP[:, 3 * fs + 2:3 * fs + 3]), pbX2, 2 * fn - 1)
                    a.wait_ge(pbVo, fn)
                    ca(a.activation(bigA[:, 8192:9216], pbx[:, 1024:2048], AF.Square,
                                    accum_out=col(SSQ, 1)), pbS2, fn)
                    a.wait_ge(pbT2, fn)
                    ca(a.activation(col(LNV, 1), col(VAR, 1), AF.Ln, bias=EPS))
                    ca(a.activation(col(RSTD, 1), col(LNV, 1), AF.Exp, scale=-0.5),
                       pbR2, fn)
                    a.wait_ge(pbN2, fn)
                    ca(a.activation(pbx[:, 0:1024], pbx[:, 1024:2048], AF.Identity,
                                    scale=col(RSTD, 1), bias=col(NB, 1)), pbX2, 2 * fn)
                    nextf = next(fin_iter, None)
            bar()
            ffn_scalar(a, ca, base=0)
            bar()
            ca(a.activation(ewcol[:], wcol[:], AF.Exp))
            ca(a.activation(elcol[:], ewcol[:], AF.Exp, scale=-128.0))
            ca(a.activation(ewinv[:], ewcol[:], AF.Exp))
            ca(a.activation(ewcol[:], ewcol[:], AF.Exp, scale=-1.0))
            ca(a.activation(eucol[:], eucol[:], AF.Exp), dEW, 1)
            pr = prraw[0:OWN, :]
            st_ = stat[0:OWN, :]
            a.wait_ge(exG, EXG_T[1])
            ca(a.activation(d_sq[0:OWN, :], pr, AF.Square, accum_out=st_[:, 4:5]), prS, 2)
            a.wait_ge(prT, 2)
            ca(a.activation(st_[:, 8:9], st_[:, 3:4], AF.Ln, bias=EPS))
            ca(a.activation(st_[:, 7:8], st_[:, 8:9], AF.Exp, scale=-0.5), prR, 2)
            a.wait_ge(prN, 2)
            ca(a.activation(prln[0:OWN, :], pr, AF.Identity, scale=st_[:, 7:8],
                            bias=st_[:, 6:7]), prX, 2)
            for i in range(OWN):
                n = i + 1
                a.wait_ge(dIn_, B_IN + 16 * n)
                ca(a.activation(d_sq, xbuf[i % 2][:], AF.Square,
                                accum_out=col(SSQ)), dS_, B_LN + n)
                a.wait_ge(dT_, B_LN + n)
                ca(a.activation(col(LNV), col(VAR), AF.Ln, bias=EPS))
                ca(a.activation(col(RSTD), col(LNV), AF.Exp, scale=-0.5), dR_, B_LN + n)
                a.wait_ge(dN_, B_LN + n)
                ca(a.activation(d_xln, xbuf[i % 2][:], AF.Identity, scale=col(RSTD),
                                bias=col(NB)), dX_, B_LN + n)
                a.wait_ge(dKF, B_LN + n)
                if i >= 2:
                    a.wait_ge(dSp_, B_SP + 48 * (i - 1))
                ca(a.activation(d_t1t[i % 2], d_kfm, AF.Exp), dT1, n)
                a.wait_ge(dP3, n)
                ca(a.activation(d_e1t, P2[:], AF.Exp, scale=-1.0), dE1, n)
                ca(a.activation(d_rtm, d_e1t, AF.Ln, bias=1.0))
                ca(a.activation(d_rtm, d_rtm, AF.Exp, scale=-1.0), dSig, n)
            a.nop().then_inc(ePro, 1)
            bar()
            for i in range(OWN):
                n = i + 1
                a.wait_ge(eNum, n)
                ca(a.activation(e_ivd, e_den, AF.Ln))
                ca(a.activation(e_ivd, e_ivd, AF.Exp, scale=-1.0), eDiv, n)
                a.wait_ge(eOt, B_VO + n)
                ca(a.activation(e_sq, e_ot, AF.Square, accum_out=col(SSQ)), eS2, B_S2 + n)
                a.wait_ge(eT2, B_S2 + n)
                ca(a.activation(col(LNV), col(VAR), AF.Ln, bias=EPS))
                ca(a.activation(col(RSTD), col(LNV), AF.Exp, scale=-0.5), eR2, B_S2 + n)
                a.wait_ge(eN2, B_S2 + n)
                ca(a.activation(e_ln, e_ot, AF.Identity, scale=col(RSTD),
                                bias=col(NB)), eX2, B_X2 + n)
            bar()
            ffn_scalar(a, ca, base=1)

        @block.tensor
        def _(t):
            ph = [0]

            def bar():
                ph[0] += 5
                t.nop().then_inc(PH, 1)
                t.wait_ge(PH, ph[0])

            t.wait_ge(dPre, 352)
            t.wait_ge(paW, 16)

            def _pa_kt():
                for cc in range(NCH):
                    ins = t.transpose(P4[:, cc * 128:(cc + 1) * 128],
                                      a_ktm[:, cc * 128:(cc + 1) * 128], idb[:])
                ins.then_inc(paT2, 1)

            for i in range(NT):
                n = i + 1
                if i > 0:
                    t.wait_ge(paKC, i)
                    if i > 1:
                        t.wait_ge(paKE, i - 1)
                    _pa_kt()
                t.wait_ge(paX, n)
                if i > 0:
                    t.wait_ge(paEv, i)
                for cc in range(NCH):
                    ins = t.transpose(P3[:, cc * 128:(cc + 1) * 128],
                                      a_xln[:, cc * 128:(cc + 1) * 128], idb[:])
                ins.then_inc(paTT, 1)
                t.wait_ge(paEv, n)
                if i > 0:
                    t.wait_ge(paVC, i)
                for h in range(4):
                    for kc in range(NCH):
                        ins = t.matmul(P1[:, h * 512:(h + 1) * 512],
                                       a_fmx[i % 2][:, kc * 128:(kc + 1) * 128],
                                       wpool[:, kc * 2048 + h * 512:kc * 2048 + (h + 1) * 512],
                                       start=(kc == 0), stop=(kc == NCH - 1))
                ins.then_inc(paKV, 1)
            t.wait_ge(paKC, NT)
            t.wait_ge(paKE, NT - 1)
            _pa_kt()
            bar()
            def _tpv(v):
                vv = visits[v]
                Wv = vv["W"]
                biv = next(iB for iB, b in enumerate(blocks)
                           if b["p"] == vv["p"] and b["kb"] == vv["kb"])
                t.wait_ge(pbOt, 2 * v + 2)
                if v > 0:
                    t.wait_ge(pbPe, v)
                nsub = Wv // 128
                for sub in range(nsub):
                    ins = t.transpose(P3[:, sub * 128:(sub + 1) * 128],
                                      P_sb[:, sub * 128:(sub + 1) * 128], idb[:])
                ins.then_inc(pbPT, 1)
                t.wait_ge(pbPe, v + 1)
                if v > 0:
                    t.wait_ge(pbAc, v)
                for sub in range(nsub):
                    for h in range(2):
                        ins = t.matmul(P1b[:, h * 512:(h + 1) * 512],
                                       pt_sb[:, sub * 128:(sub + 1) * 128],
                                       b_kv[biv % 2][1][:, sub * 1024 + h * 512:sub * 1024 + (h + 1) * 512],
                                       start=(sub == 0), stop=(sub == nsub - 1))
                ins.then_inc(pbPV, 1)

            emitted = 0
            for u, vis in enumerate(visits):
                j = vis["j"]; p = vis["p"]; s = j - 8 * p; W = vis["W"]; kb = vis["kb"]
                bi = next(iB for iB, b in enumerate(blocks)
                          if b["p"] == p and b["kb"] == kb)
                if u == p_firsts[p]:
                    while emitted < u:
                        _tpv(emitted)
                        emitted += 1
                    for ss in range(8):
                        jn = 8 * p + ss + 1
                        t.wait_ge(pbQx, jn)
                        if jn > 1:
                            t.wait_ge(pbQe, jn - 1)
                        if p > 0 and ss == 0:
                            t.wait_ge(pbPe, p_firsts[p])
                        for cc in range(NCH):
                            ins = t.transpose(P3[:, cc * 128:(cc + 1) * 128],
                                              P_sb[:, cc * 128:(cc + 1) * 128], idb[:])
                        ins.then_inc(pbQT, 1)
                t.wait_ge(pbQe, 8 * (p + 1))
                t.wait_ge(pbKV, 256 * (bi + 1))
                if u >= 2:
                    t.wait_ge(pbOt, 2 * u - 2)
                SCu = P1a if u % 2 == 0 else P2
                nh = (W + 511) // 512
                for h in range(nh):
                    w_h = min(W - h * 512, 512)
                    for kc in range(NCH):
                        ins = t.matmul(
                            SCu[:, h * 512:h * 512 + w_h],
                            b_q[:, s * 1024 + kc * 128:s * 1024 + (kc + 1) * 128],
                            b_kv[bi % 2][0][:, kc * 1024 + h * 512:kc * 1024 + h * 512 + w_h],
                            start=(kc == 0), stop=(kc == NCH - 1))
                ins.then_inc(pbQK, 1)
                if u >= 1 and emitted == u - 1:
                    _tpv(u - 1)
                    emitted = u
            while emitted < NV:
                _tpv(emitted)
                emitted += 1
            bar()
            ffn_tensor(t, base=0)
            bar()
            t.wait_ge(prX, 2)
            for cc in range(NCH):
                ins = t.transpose(P3[:, cc * OWN:(cc + 1) * OWN],
                                  prln[0:OWN, cc * 128:(cc + 1) * 128],
                                  idb[0:OWN, 0:OWN])
            ins.then_inc(prTT, 1)
            t.wait_ge(dW_, B_W + 96)
            for i in range(OWN):
                n = i + 1
                t.wait_ge(dX_, B_LN + n)
                t.wait_ge(prEv, 2)
                if i > 0:
                    t.wait_ge(dEv, B_LN + i)
                for cc in range(NCH):
                    ins = t.transpose(P3[:, cc * 128:(cc + 1) * 128],
                                      d_xln[:, cc * 128:(cc + 1) * 128], idb[:])
                ins.then_inc(dTT, 1)
                t.wait_ge(dM, n)
                if i > 0:
                    t.wait_ge(dKF, B_LN + i)
                for h in range(2):
                    for kc in range(NCH):
                        ins = t.matmul(P1a[:, h * 512:(h + 1) * 512],
                                       d_xkf[:, kc * 128:(kc + 1) * 128],
                                       wslot(0)[:, kc * D + h * 512:kc * D + (h + 1) * 512],
                                       start=(kc == 0), stop=(kc == NCH - 1))
                ins.then_inc(dP1, 1)
                if i > 0:
                    t.wait_ge(dVF, i)
                for h in range(2):
                    for kc in range(NCH):
                        ins = t.matmul(P1b[:, h * 512:(h + 1) * 512],
                                       d_xvf[:, kc * 128:(kc + 1) * 128],
                                       wslot(1)[:, kc * D + h * 512:kc * D + (h + 1) * 512],
                                       start=(kc == 0), stop=(kc == NCH - 1))
                ins.then_inc(dP2, 1)
                if i > 0:
                    t.wait_ge(dE1, i)
                for h in range(2):
                    for kc in range(NCH):
                        ins = t.matmul(P2[:, h * 512:(h + 1) * 512],
                                       d_xrf[:, kc * 128:(kc + 1) * 128],
                                       wslot(2)[:, kc * D + h * 512:kc * D + (h + 1) * 512],
                                       start=(kc == 0), stop=(kc == NCH - 1))
                ins.then_inc(dP3, 1)
                t.wait_ge(dKC, B_LN + n)
                for cc in range(NCH):
                    ins = t.transpose(P1a[:, cc * 128:(cc + 1) * 128],
                                      d_ktm[:, cc * 128:(cc + 1) * 128], idf[:])
                ins.then_inc(dTK, 1)
                t.wait_ge(dVC, B_LN + n)
                for cc in range(NCH):
                    ins = t.transpose(P1b[:, cc * 128:(cc + 1) * 128],
                                      d_vtm[:, cc * 128:(cc + 1) * 128], idf[:])
                ins.then_inc(dTV, 1)
                t.wait_ge(dSig, n)
                t.wait_ge(dEv, n)
                if i > 0:
                    t.wait_ge(dRE, i)
                for cc in range(NCH):
                    ins = t.transpose(P4[:, cc * 128:(cc + 1) * 128],
                                      d_rtm[:, cc * 128:(cc + 1) * 128], idb[:])
                ins.then_inc(dTR, 1)
            t.wait_ge(dScan, OWN)
            t.wait_ge(dSig, OWN)
            for q4 in range(4):
                ins = t.transpose(P2[:, q4 * 128:(q4 + 1) * 128],
                                  d_totc[:, q4 * 128:(q4 + 1) * 128], idf[:])
            ins.then_inc(dTB, 1)
            bar()
            t.wait_ge(exG, EXG_T["sig"])
            for kk in range(NCH):
                ins = t.transpose(P2[:, (kk % 4) * 256:(kk % 4) * 256 + 128] if False else
                                  P2[:, kk * 128:(kk + 1) * 128],
                                  d_sgr[:, kk * 128:(kk + 1) * 128], idf[:])
            ins.then_inc(eTz, 1)
            t.wait_ge(eW_, 16)
            for i in range(OWN):
                n = i + 1
                t.wait_ge(eRy, B_AC + n)
                if i > 0:
                    t.wait_ge(eOt, B_VO + i)
                for h in range(2):
                    for kc in range(NCH):
                        ins = t.matmul(P2[:, h * 512:(h + 1) * 512],
                                       e_ryb[:, kc * 128:(kc + 1) * 128],
                                       wslot(0)[:, kc * D + h * 512:kc * D + (h + 1) * 512],
                                       start=(kc == 0), stop=(kc == NCH - 1))
                ins.then_inc(eWo, 1)
            bar()
            ffn_tensor(t, base=1)

    _CACHE[key] = nc
    return nc


def _tiles_of(core):
    par = core % 2
    return [2 * j + 1 - par for j in range(OWN)]


def _flatmu(vec):
    col = np.ascontiguousarray(np.asarray(vec, np.float32).reshape(NCH, 128).T)
    return np.ascontiguousarray(
        np.broadcast_to(col[:, :, None], (128, NCH, 128)).reshape(128, D)
    ).astype(np.float32)


def _colsof(vec):
    return np.ascontiguousarray(np.asarray(vec, np.float32).reshape(NCH, 128).T)


def run_ffn_launch(stream_own, prow_own, Wr, Wk, Wv, mu_r, mu_k):
    nc = build_ffn()
    wrT = np.ascontiguousarray(np.asarray(Wr, np.float32).T)
    wkT = np.ascontiguousarray(np.asarray(Wk, np.float32).T)
    wvT = np.ascontiguousarray(np.asarray(Wv, np.float32).T)
    murf = _flatmu(mu_r)
    mukf = _flatmu(mu_k)
    identb = np.eye(128, dtype=np.float32).astype(BFnp)
    constv = np.ascontiguousarray(
        np.broadcast_to(np.array([EPS, 1.0], np.float32)[None, :], (128, 2)))
    in_maps = [dict(stream=stream_own[c], prow=prow_own[c], wrT=wrT, wkT=wkT,
                    wvT=wvT, murf=murf, mukf=mukf, identb=identb, constv=constv)
               for c in range(8)]
    res = run_bass_kernel_spmd(nc, in_maps, core_ids=list(range(8)))
    return [r["ostream"] for r in res.results]


def _consts():
    identb = np.eye(128, dtype=np.float32).astype(BFnp)
    constv = np.ascontiguousarray(
        np.broadcast_to(np.array([EPS, 1.0], np.float32)[None, :], (128, 2)))
    return identb, constv


def run_l1_launches(x_full, W):
    identb, constv = _consts()
    x_t = np.ascontiguousarray(x_full.reshape(B, NT, TILE, D)).astype(np.float32)
    x_own = [np.ascontiguousarray(x_t[c // 2, _tiles_of(c)]) for c in range(8)]
    wkvT = np.ascontiguousarray(np.asarray(W, np.float32).T)
    nc_a = build_l1a()
    res_a = run_bass_kernel_spmd(
        nc_a,
        [dict(x=x_own[c], wkvT=wkvT, identb=identb, constv=constv) for c in range(8)],
        core_ids=list(range(8)))
    ra = res_a.results
    kf = [np.zeros((NCH, 128, S), BFnp) for _ in range(B)]
    vf = [np.zeros((S, D), BFnp) for _ in range(B)]
    for c in range(8):
        b = c // 2
        kfm = ra[c]["o_kfm"].view(BFnp) if ra[c]["o_kfm"].dtype != BFnp else ra[c]["o_kfm"]
        vtm = ra[c]["o_vtm"].view(BFnp) if ra[c]["o_vtm"].dtype != BFnp else ra[c]["o_vtm"]
        for sl, g in enumerate(_tiles_of(c)):
            kf[b][:, :, g * TILE:(g + 1) * TILE] = kfm[:, :, sl, :]
            vf[b][g * TILE:(g + 1) * TILE, :] = vtm[sl]
    tri = np.where(np.arange(128)[None, :] <= np.arange(128)[:, None], 0.0, NEG).astype(np.float32)
    zer = np.zeros((128, 128), np.float32)
    alln = np.full((128, 128), NEG, np.float32)
    nc_b = build_l1b()
    in_maps = []
    for c in range(8):
        b = c // 2
        par = c % 2
        mp = np.stack([zer, tri]) if par == 0 else np.stack([tri, alln])
        in_maps.append(dict(
            q=np.ascontiguousarray(ra[c]["o_xfm"]),
            kf=np.ascontiguousarray(kf[b]),
            vf=np.ascontiguousarray(vf[b]),
            xtm=np.ascontiguousarray(ra[c]["o_xtm"]),
            x=x_own[c],
            maskp=np.ascontiguousarray(mp),
            identb=identb, constv=constv,
        ))
    res_b = run_bass_kernel_spmd(nc_b, in_maps, core_ids=list(range(8)))
    out = np.empty((B, NT, TILE, D), np.float32)
    for c in range(8):
        b = c // 2
        for sl, g in enumerate(_tiles_of(c)):
            out[b, g] = res_b.results[c]["ostream"][sl]
    return out


def _shard_stream(full_t):
    stream_own, prow_own = [], []
    for c in range(8):
        b = c // 2
        tiles = _tiles_of(c)
        stream_own.append(np.ascontiguousarray(full_t[b, tiles]))
        pr = np.zeros((OWN, D), np.float32)
        for j, g in enumerate(tiles):
            if g > 0:
                pr[j] = full_t[b, g - 1, TILE - 1]
        prow_own.append(pr)
    return stream_own, prow_own


def _assemble(per_core):
    out = np.empty((B, NT, TILE, D), np.float32)
    for c in range(8):
        b = c // 2
        for j, g in enumerate(_tiles_of(c)):
            out[b, g] = per_core[c][j]
    return out


def run_l3_launches(stream_t, Wk, Wv, Wr, Wo, td, tf, tmk, tmv, tmr):
    identb, constv = _consts()
    identf = np.eye(128, dtype=np.float32)
    stream_own, prow_own = _shard_stream(stream_t)
    wkT = np.ascontiguousarray(np.asarray(Wk, np.float32).T)
    wvT = np.ascontiguousarray(np.asarray(Wv, np.float32).T)
    wrT = np.ascontiguousarray(np.asarray(Wr, np.float32).T)
    woT = np.ascontiguousarray(np.asarray(Wo, np.float32).T)
    tmkf = _flatmu(np.asarray(tmk).reshape(-1))
    tmvf = _flatmu(np.asarray(tmv).reshape(-1))
    tmrf = _flatmu(np.asarray(tmr).reshape(-1))
    td_col = _colsof(td)
    tf_col = _colsof(tf)
    nc_a = build_l3a()
    res_a = run_bass_kernel_spmd(
        nc_a,
        [dict(stream=stream_own[c], prow=prow_own[c], wkT=wkT, wvT=wvT, wrT=wrT,
              tmkf=tmkf, tmvf=tmvf, tmrf=tmrf, td_col=td_col,
              identb=identb, identf=identf, constv=constv) for c in range(8)],
        core_ids=list(range(8)))
    ra = res_a.results
    tot_g = np.zeros((B, NT, 2, 128, 8), np.float32)
    for c in range(8):
        b = c // 2
        for j, g in enumerate(_tiles_of(c)):
            tot_g[b, g] = ra[c]["tot_o"][j]
    in_maps = []
    for c in range(8):
        b = c // 2
        par = c % 2
        sh = np.zeros((NT, 2, 128, 8), np.float32)
        if par == 0:
            sh[:] = tot_g[b]
        else:
            sh[1:] = tot_g[b][:-1]
        tot_sh = np.ascontiguousarray(sh.transpose(2, 1, 3, 0))
        in_maps.append(dict(
            stream=stream_own[c], z_sp=ra[c]["z_sp"], t1_sp=ra[c]["t1_sp"],
            r_sp=ra[c]["r_sp"], tot_sh=tot_sh, td_col=td_col, tf_col=tf_col,
            woT=woT, constv=constv))
    nc_c = build_l3c()
    res_c = run_bass_kernel_spmd(nc_c, in_maps, core_ids=list(range(8)))
    return _assemble([res_c.results[c]["ostream"] for c in range(8)])


def _fused_inmaps(inputs, debug=False):
    x = np.asarray(inputs["L0_attn_x"], np.float32).reshape(B, NT, TILE, D)
    identb, constv = _consts()
    identf = np.eye(128, dtype=np.float32)
    tri = np.where(np.arange(128)[None, :] <= np.arange(128)[:, None], 0.0, NEG).astype(np.float32)
    zer = np.zeros((128, 128), np.float32)
    alln = np.full((128, 128), NEG, np.float32)
    wkvT = np.ascontiguousarray(np.asarray(inputs["L0_attn_W"], np.float32).T)
    wT2 = np.ascontiguousarray(np.stack([
        np.asarray(inputs["L0_ffn_W_r"], np.float32).T,
        np.asarray(inputs["L0_ffn_W_k"], np.float32).T,
        np.asarray(inputs["L0_ffn_W_v"], np.float32).T]))
    wT3 = np.ascontiguousarray(np.stack([
        np.asarray(inputs["L1_attn_Wk"], np.float32).T,
        np.asarray(inputs["L1_attn_Wv"], np.float32).T,
        np.asarray(inputs["L1_attn_Wr"], np.float32).T]))
    woT3 = np.ascontiguousarray(np.asarray(inputs["L1_attn_Wo"], np.float32).T)
    wT4 = np.ascontiguousarray(np.stack([
        np.asarray(inputs["L1_ffn_W_r"], np.float32).T,
        np.asarray(inputs["L1_ffn_W_k"], np.float32).T,
        np.asarray(inputs["L1_ffn_W_v"], np.float32).T]))
    mu2f = np.ascontiguousarray(np.stack([
        _flatmu(inputs["L0_ffn_mu_r"]), _flatmu(inputs["L0_ffn_mu_k"])]))
    tm3f = np.ascontiguousarray(np.stack([
        _flatmu(np.asarray(inputs["L1_attn_time_mix_k"]).reshape(-1)),
        _flatmu(np.asarray(inputs["L1_attn_time_mix_v"]).reshape(-1)),
        _flatmu(np.asarray(inputs["L1_attn_time_mix_r"]).reshape(-1))]))
    mu4f = np.ascontiguousarray(np.stack([
        _flatmu(inputs["L1_ffn_mu_r"]), _flatmu(inputs["L1_ffn_mu_k"])]))
    td_col = _colsof(inputs["L1_attn_time_decay"])
    tf_col = _colsof(inputs["L1_attn_time_first"])
    in_maps = []
    for c in range(8):
        b = c // 2
        par = c % 2
        delta = 1 - par
        tiles = _tiles_of(c)
        mp = np.ascontiguousarray(np.stack([zer, tri]) if par == 0 else np.stack([tri, alln]))
        idxb = np.zeros((32, 1), np.uint32)
        zpm = np.ones((32, 1), np.float32)
        for j in range(OWN):
            t = 2 * j + delta - 1
            if t < 0:
                idxb[j, 0] = 0
                zpm[j, 0] = 0.0
            else:
                owner = 2 * b + 1 - (t % 2)
                idxb[j, 0] = owner * 32 + t // 2
        idxs = np.zeros((128, 8), np.uint32)
        for cc in range(8):
            for r in range(128):
                q = r // 64
                m = r % 64
                t = m + delta - 1
                if t < 0:
                    idxs[r, cc] = 0
                else:
                    owner = 2 * b + 1 - (t % 2)
                    idxs[r, cc] = owner * 512 + (t // 2) * 16 + q * 8 + cc
        tzm = np.full((128, 2), 1.0 if par == 0 else 0.0, np.float32)
        in_maps.append(dict(
            xfull=np.ascontiguousarray(x[b]),
            x_own=np.ascontiguousarray(x[b, tiles]),
            wkvT=wkvT, wT2=wT2, wT3=wT3, woT3=woT3, wT4=wT4,
            mu2f=mu2f, tm3f=tm3f, mu4f=mu4f,
            td_col=td_col, tf_col=tf_col, maskp=mp,
            identb=identb, identf=identf, constv=constv,
            idxb=idxb, idxs=idxs, zpm=zpm, tzm=tzm,
        ))
    return in_maps


def kernel_fused(debug=False, **inputs):
    nc = build_fused(debug=debug)
    in_maps = _fused_inmaps(inputs, debug=debug)
    res = run_bass_kernel_spmd(nc, in_maps, core_ids=list(range(8)))
    out = _assemble([res.results[c]["ostream"] for c in range(8)])
    if debug:
        return out, res
    return np.ascontiguousarray(out.reshape(B, S, D))


def kernel(**inputs):
    return kernel_fused(debug=False, **inputs)
